# version 02
# baseline (speedup 1.0000x reference)
import numpy as np
import ml_dtypes
import concourse.bass as bass
import concourse.mybir as mybir
from concourse.bass_utils import run_bass_kernel_spmd

F32 = mybir.dt.float32
BF16 = mybir.dt.bfloat16
I32 = mybir.dt.int32
AF = mybir.ActivationFunctionType
ALU = mybir.AluOpType
AX = mybir.AxisListType

N_CORES = 8
D = 1024
DC = 8
JH = 4096
JC = 32
H = 4
NS = 8
COLS = 352
TT = 128
BIG = 1.0e30
MAGIC = 12582912.0
EMASK = 0x7F800000
EXP3 = float(3 << 23)
INVC = float(254 << 23)
RSQC = float(0x5F3759DF)
LN_EPS = 1e-5

ENGINES = ("sync", "scalar", "vector", "gpsimd", "tensor")


class Ev:
    __slots__ = ("sem", "val")

    def __init__(self, sem):
        self.sem = sem
        self.val = None


class SemC:
    __slots__ = ("h", "total")

    def __init__(self, h):
        self.h = h
        self.total = 0


class Chain:

    def __init__(self, plan, sem):
        self.plan = plan
        self.sem = sem
        self.items = []
        self.last = None

    def emit(self, fn, waits=(), pub=None, inc=1, independent=False):
        pub = pub if pub is not None else self.sem
        ev = Ev(pub)
        prev = None if independent else self.last
        w = tuple(w for w in list(waits) + [prev] if w is not None)
        self.items.append((fn, w, pub, inc, ev))
        self.last = ev
        return ev


class Plan:
    def __init__(self, nc, es, spacers):
        self.nc = nc
        self.es = es
        self.streams = {e: [] for e in ENGINES}
        self._semidx = 0
        self.spacers = spacers

    def sem(self, name=None):
        h = self.es.enter_context(self.nc.semaphore(name or f"s{self._semidx}"))
        self._semidx += 1
        return SemC(h)

    def chain(self, sem):
        return Chain(self, sem)

    def _append(self, engine, fn, waits, pub, inc, ev):
        if pub is not None:
            pub.total += inc
            if ev is not None:
                ev.val = pub.total
        self.streams[engine].append((fn, waits, pub, inc, ev))

    def emit(self, engine, fn, waits=(), pub=None, inc=1):
        ev = Ev(pub) if pub is not None else None
        self._append(engine, fn, tuple(w for w in waits if w is not None), pub, inc, ev)
        return ev

    def dma(self, queue, out, in_, waits=(), pub=None):
        return self.emit(queue, lambda e, o=out, i=in_: e.dma_start(out=o, in_=i),
                         waits=waits, pub=pub, inc=16)

    def seal(self, engine, *chains):
        chains = [list(c.items) for c in chains if c is not None and c.items]
        if not chains:
            return
        idx = [0] * len(chains)
        while True:
            live = [k for k in range(len(chains)) if idx[k] < len(chains[k])]
            if not live:
                break
            for k in live:
                fn, waits, pub, inc, ev = chains[k][idx[k]]
                self._append(engine, fn, waits, pub, inc, ev)
                idx[k] += 1

    def finalize(self, block):
        def section(engname):
            def _(eng):
                waited = {}
                for fn, waits, pub, inc, ev in self.streams[engname]:
                    for w in waits:
                        assert w.val is not None, f"unsealed event waited on {engname}"
                        if waited.get(id(w.sem), -1) < w.val:
                            eng.wait_ge(w.sem.h, w.val)
                            waited[id(w.sem)] = w.val
                    inst = fn(eng)
                    if pub is not None:
                        assert inst is not None, f"pub with no instruction on {engname}"
                        inst.then_inc(pub.h, inc)
            return _
        block.sync(section("sync"))
        block.scalar(section("scalar"))
        block.vector(section("vector"))
        block.gpsimd(section("gpsimd"))
        block.tensor(section("tensor"))


def bcast(ap, shape):
    return ap.broadcast_to(list(shape))


def build(n_tiles=32, debug=()):
    from contextlib import ExitStack
    nc = bass.Bass("TRN2", target_bir_lowering=False)
    T = n_tiles * TT
    CP = n_tiles * H

    x_in = nc.dram_tensor("x_sh", [T, D], F32, kind="ExternalInput").ap()
    w_attn = nc.dram_tensor("w_attn", [D, COLS], F32, kind="ExternalInput").ap()
    b_attn = nc.dram_tensor("b_attn", [1, COLS], F32, kind="ExternalInput").ap()
    lnw_in = nc.dram_tensor("lnw_rs", [128, DC], F32, kind="ExternalInput").ap()
    lnb_in = nc.dram_tensor("lnb_rs", [128, DC], F32, kind="ExternalInput").ap()
    wfc_in = nc.dram_tensor("w_fc", [JH, D], F32, kind="ExternalInput").ap()
    bfc_in = nc.dram_tensor("b_fc", [1, JH], F32, kind="ExternalInput").ap()
    wpj_in = nc.dram_tensor("w_proj", [D, JH], F32, kind="ExternalInput").ap()
    bpj_in = nc.dram_tensor("b_proj", [1, D], F32, kind="ExternalInput").ap()
    outw_in = nc.dram_tensor("out_w", [NS * H, D], F32, kind="ExternalInput").ap()
    outb_in = nc.dram_tensor("out_b", [1, D], F32, kind="ExternalInput").ap()
    id_f = nc.dram_tensor("id_f", [128, 128], F32, kind="ExternalInput").ap()
    id_b = nc.dram_tensor("id_b", [128, 128], BF16, kind="ExternalInput").ap()
    ones_r = nc.dram_tensor("ones_r", [1, 128], F32, kind="ExternalInput").ap()
    g0m_in = nc.dram_tensor("g0mask", [1, 8], F32, kind="ExternalInput").ap()

    out_t = nc.dram_tensor("out_sh", [T, D], F32, kind="ExternalOutput")
    out_ap = out_t.ap()

    r_scr = nc.dram_tensor("r_scr", [T, D], F32)
    lg_scr = nc.dram_tensor("lg_scr", [n_tiles * H * TT * 88], F32)
    hq_scr = nc.dram_tensor("hq_scr", [n_tiles * JC * 128 * 128], BF16)
    wpqt_scr = nc.dram_tensor("wpqt_scr", [DC * 4 * 8 * 128 * 128], BF16)
    acc_scr = nc.dram_tensor("acc_scr", [T, D], F32)
    pre1_scr = nc.dram_tensor("pre1_scr", [CP * NS], F32)
    pre2_scr = nc.dram_tensor("pre2_scr", [CP * NS], F32)
    dc_scr = nc.dram_tensor("dc_scr", [2 * CP * NS * TT], F32)
    ue_scr = nc.dram_tensor("ue_scr", [H * NS], F32)
    hm_scr = nc.dram_tensor("hm_scr", [1, H * NS], F32)
    ag_scr = nc.dram_tensor("ag_scr", [N_CORES, H * NS], F32)
    g0_scr = nc.dram_tensor("g0_scr", [H * NS], F32)
    g_scr = nc.dram_tensor("g_scr", [CP * NS * TT], F32)
    ys_scr = nc.dram_tensor("ys_scr", [CP * NS * TT], BF16)

    dbg = {}

    def dbg_out(name, shape, dt=F32):
        if name in debug:
            dbg[name] = nc.dram_tensor("dbg_" + name, list(shape), dt, kind="ExternalOutput").ap()
            return dbg[name]
        return None

    es = ExitStack()
    sb = lambda name, shape, dt=F32: es.enter_context(nc.sbuf_tensor(name, list(shape), dt))
    psum = lambda name, shape, dt=F32: es.enter_context(nc.psum_tensor(name, list(shape), dt))
    _frees = []

    def sbf(name, shape, dt=F32):
        cm = nc.sbuf_tensor(name, list(shape), dt)
        h = cm.__enter__()
        _frees.append(cm)
        return h

    def psumf(name, shape, dt=F32):
        cm = nc.psum_tensor(name, list(shape), dt)
        h = cm.__enter__()
        _frees.append(cm)
        return h

    def free_phase1():
        for cm in reversed(_frees):
            cm.__exit__(None, None, None)
        _frees.clear()

    ones_sb = sb("ones1", [1, 128])
    bpj_sb = sb("bpj", [1, D])
    c_emask = sb("c_emask", [128, 1], I32)
    c_15 = sb("c_15", [128, 1])
    iota8 = sb("iota8", [128, 8])
    iorev8 = sb("iorev8", [128, 8])
    jnk = sb("jnk", [128, 4])
    XR = 2
    qs_amax = [sbf(f"q_amax{i}", [128, 64]) for i in range(2)]
    qs_sclb = [sbf(f"q_sclb{i}", [128, 64], I32) for i in range(2)]
    qs_invb = [sbf(f"q_invb{i}", [128, 64], I32) for i in range(2)]
    wattn_sb = sbf("wattn", [128, DC, COLS])
    battn_sb = sbf("battn", [1, COLS])
    lnw_sb = sbf("lnw", [128, DC])
    lnb_sb = sbf("lnb", [128, DC])
    idf_sb = sbf("idf", [128, 128])
    idb_sb = sbf("idb", [128, 128], BF16)
    bfc_sb = sbf("bfc", [1, JH])
    st_sb = sbf("st_sb", [128, 16])
    stv_sb = sbf("stv_sb", [128, 8])
    x_sb = sbf("x_sb", [128, XR, D])
    sq_sb = sbf("sq_sb", [128, D])
    xn_sb = sbf("xn_sb", [128, 2, D])
    xs_sb = sbf("xs_sb", [128, 2, D])
    r_sb = sbf("r_sb", [128, 2, D])
    xq_sb = sbf("xq_sb", [128, 2, D], BF16)
    xqf_sb = sbf("xqf_sb", [128, 2, DC, 128], BF16)
    xn2_sb = sbf("xn2_sb", [128, 2, DC, 128])
    lgs_sb = sbf("lgs_sb", [128, 2, COLS])
    h_sb = sbf("h_sb", [128, 2, 2048])
    hq_sb = sbf("hq_sb", [128, 2, 2048], BF16)
    hqf_sb = sbf("hqf_sb", [128, 2, 8, 128], BF16)
    wst_sb = sbf("wst_sb", [128, 2, 1024])
    wq_sb = sbf("wq_sb", [128, 2, 1024], BF16)
    wqt_sb = sbf("wqt_sb", [128, 2, 1024], BF16)
    wfcqT = sbf("wfcqT", [128, DC, JH], BF16)

    S_MU1, S_RSTD1, S_MU2, S_RSTD2, S_B1, S_B2, S_VE, S_R1P = 0, 1, 2, 3, 4, 5, 6, 7

    ps_attn = psumf("ps_attn", [128, COLS])
    ps_fc = [psumf(f"ps_fc{i}", [128, 512]) for i in range(4)]
    ps_xqt = psumf("ps_xqt", [128, 1024], BF16)
    ps_xst = psumf("ps_xst", [128, 512])
    ps_hqt = psumf("ps_hqt", [128, 1024], BF16)

    with nc.Block() as block:
        spacers = {
            "vector": lambda e: e.tensor_copy(jnk[:, 0:1], c_15[:, :]),
            "scalar": lambda e: e.activation(jnk[:, 1:2], c_15[:, :], AF.Identity),
            "gpsimd": lambda e: e.memset(jnk[:, 2:3], 0.0),
            "tensor": lambda e: e.nop(),
            "sync": lambda e: e.nop(),
        }
        p = Plan(nc, es, spacers)
        sWL = p.sem("sWL")
        sSC = p.sem("sSC")
        sDV = p.sem("sDV")
        sPE = p.sem("sPE")
        sX = [p.sem("sX0"), p.sem("sX1")]
        sWst = [p.sem("sWst0"), p.sem("sWst1")]
        sWqS = [p.sem("sWqS0"), p.sem("sWqS1")]
        sRs = [p.sem("sRs0"), p.sem("sRs1")]
        sLgs = [p.sem("sLgs0"), p.sem("sLgs1")]
        sHqs = [p.sem("sHqs0"), p.sem("sHqs1")]
        sCst = p.sem("sCst")
        sPh = p.sem("sPh")
        sDbg = p.sem("sDbg")
        dbg_prev = [None]

        def dbg_dma(out, in_, waits):
            dbg_prev[0] = p.dma("gpsimd", out, in_, waits=list(waits) + [dbg_prev[0]], pub=sDbg)
            return dbg_prev[0]

        bank = {"attn": None, "fc0": None, "fc1": None, "xqt": None, "xst": None, "hqt": None}

        p.emit("gpsimd", lambda e: e.memset(c_emask[:, :], EMASK))
        p.emit("gpsimd", lambda e: e.memset(c_15[:, :], 1.5))
        p.emit("gpsimd", lambda e: e.iota(iota8[:, :], pattern=[[1, 8]], base=0,
                                          channel_multiplier=0, allow_small_or_imprecise_dtypes=True))
        evConst = p.emit("gpsimd", lambda e: e.iota(iorev8[:, :], pattern=[[-1, 8]], base=8,
                                                    channel_multiplier=0,
                                                    allow_small_or_imprecise_dtypes=True), pub=sCst)

        p.dma("sync", wattn_sb[:, :, :], w_attn.rearrange("(c p) n -> p c n", p=128), pub=sWL)
        p.dma("sync", battn_sb[:, :], b_attn, pub=sWL)
        p.dma("sync", lnw_sb[:, :], lnw_in, pub=sWL)
        p.dma("sync", lnb_sb[:, :], lnb_in, pub=sWL)
        p.dma("sync", idf_sb[:, :], id_f, pub=sWL)
        p.dma("sync", idb_sb[:, :], id_b, pub=sWL)
        p.dma("sync", ones_sb[:, :], ones_r, pub=sWL)
        p.dma("sync", bfc_sb[:, :], bfc_in, pub=sWL)
        evSmall = p.dma("sync", bpj_sb[:, :], bpj_in, pub=sWL)

        def quantize_chain(ch, qi, src, dst, W, waits, pub):
            nb = W // 32
            amax, sclb, invb = qs_amax[qi], qs_sclb[qi], qs_invb[qi]
            ch.emit(lambda e: e.tensor_reduce(
                amax[:, 0:nb].rearrange("p (n o) -> p n o", o=1),
                src.rearrange("p (n k) -> p n k", k=32),
                axis=AX.X, op=ALU.max, apply_absolute_value=True),
                waits=list(waits) + [evConst])
            ch.emit(lambda e: e.tensor_scalar(amax[:, 0:nb], amax[:, 0:nb], 1e-30, None, ALU.max))
            ch.emit(lambda e: e.tensor_scalar(
                sclb[:, 0:nb], amax[:, 0:nb].bitcast(I32), c_emask[:, :], None, ALU.bitwise_and))
            ch.emit(lambda e: e.tensor_scalar(sclb[:, 0:nb], sclb[:, 0:nb], EXP3, None, ALU.subtract))
            ch.emit(lambda e: e.tensor_scalar(invb[:, 0:nb], sclb[:, 0:nb], -1.0, INVC, ALU.mult, ALU.add))
            ch.emit(lambda e: e.tensor_tensor(
                src.rearrange("p (n k) -> p n k", k=32),
                src.rearrange("p (n k) -> p n k", k=32),
                bcast(invb[:, 0:nb].bitcast(F32).rearrange("p (n o) -> p n o", o=1), (128, nb, 32)),
                ALU.mult))
            ch.emit(lambda e: e.tensor_scalar(src.bitcast(I32), src, MAGIC, None, ALU.add))
            return ch.emit(lambda e: e.scalar_tensor_tensor(
                dst.rearrange("p (n k) -> p n k", k=32),
                src.bitcast(I32).rearrange("p (n k) -> p n k", k=32),
                MAGIC,
                bcast(sclb[:, 0:nb].bitcast(F32).rearrange("p (n o) -> p n o", o=1), (128, nb, 32)),
                ALU.subtract, ALU.mult), pub=pub)

        evWtE = [None, None]
        evWqSp = [None, None]
        evWstFree = [None, None]
        evWfcDone = None
        evWpjDone = None

        def wprep_pe_and_out(sl, kind, jt, dcjs, evq):
            evT = None
            for k in range(8):
                ww = [evq, bank["xqt"], evSmall] if k == 0 else ()
                evT = p.emit("tensor", lambda e, k=k, sl=sl: e.transpose(
                    ps_xqt[:, k * 128:(k + 1) * 128], wq_sb[:, sl, k * 128:(k + 1) * 128],
                    idb_sb[:, :]), waits=ww, pub=(sPE if k == 7 else None))
            evE = p.emit("scalar", lambda e, sl=sl: e.activation(
                wqt_sb[:, sl, :], ps_xqt[:, :], AF.Copy), waits=[evT, evWqSp[sl]], pub=sSC)
            bank["xqt"] = evE
            evWtE[sl] = evE
            if kind == "fc":
                return p.emit("vector", lambda e, sl=sl, jt=jt: e.tensor_copy(
                    wfcqT[:, :, jt * 128:(jt + 1) * 128],
                    wqt_sb[:, sl, :].rearrange("p (c k) -> p c k", c=8)),
                    waits=[evE], pub=sDV)
            else:
                dcid, js = dcjs
                off = (dcid * 4 + js) * 8 * 128 * 128
                dst = bass.AP(wpqt_scr, off, [[128, 128], [128 * 128, 8], [1, 128]])
                ev = p.dma("gpsimd", dst,
                           wqt_sb[:, sl, :].rearrange("p (c k) -> p c k", c=8),
                           waits=[evE], pub=sWqS[sl])
                evWqSp[sl] = ev
                return ev

        units = [("fc", jt, None) for jt in range(JC)] + \
                [("pj", None, (dcid, js)) for dcid in range(DC) for js in range(4)]
        for u0 in range(0, len(units), 2):
            pair = units[u0:u0 + 2]
            evqs = []
            chains = []
            for i, (kind, jt, dcjs) in enumerate(pair):
                sl = i
                if kind == "fc":
                    src_ap = wfc_in[(jt) * 128:(jt + 1) * 128, :]
                else:
                    dcid, js = dcjs
                    src_ap = wpj_in[dcid * 128:(dcid + 1) * 128, js * 1024:(js + 1) * 1024]
                evWst = p.dma("sync", wst_sb[:, sl, :], src_ap,
                              waits=[evWstFree[sl]], pub=sWst[sl])
                ch = p.chain(sDV)
                evq = quantize_chain(ch, sl, wst_sb[:, sl, :], wq_sb[:, sl, :], 1024,
                                     [evWst, evWtE[sl]], sDV)
                evqs.append(evq)
                chains.append(ch)
                evWstFree[sl] = evq
            p.seal("vector", *chains)
            for i, (kind, jt, dcjs) in enumerate(pair):
                evDone = wprep_pe_and_out(i, kind, jt, dcjs, evqs[i])
                if kind == "fc":
                    evWfcDone = evDone
                else:
                    evWpjDone = evDone

        evXfree = [None] * XR
        ev_xs_t, ev_s3_t, evA_t, evB_t = {}, {}, {}, {}
        evXq_t, evXqE_t, evXn2E_t, evXn2h0 = {}, {}, {}, {}
        evG_t, evHq_t, evHqSp_t = {}, {}, {}
        evLgE_t, evLgSp_t, evRSp_t = {}, {}, {}
        peT_xq, peT_xs, peMa, peMf = {}, {}, {}, {}
        evDbgXn_t, evDbgH_t, evDbgHq_t = {}, {}, {}
        evXsFree = [None, None]
        evXnFree = [None, None]
        evRFree = [None, None]
        evXqFree = [None, None]

        dbg_xn = dbg_out("xn", [T, D])
        dbg_xs = dbg_out("xs", [T, D])
        dbg_lg = dbg_out("logits", [T, COLS])
        dbg_xq = dbg_out("xq", [T, D], BF16)
        dbg_h = dbg_out("h", [T, JH])
        dbg_hq = dbg_out("hq", [T, JH], BF16)

        def ln_small_chain(ch, rstd_col, extra=()):
            mu_col = rstd_col - 1
            mu = st_sb[:, mu_col:mu_col + 1]
            ch.emit(lambda e: e.tensor_scalar(mu, st_sb[:, S_B1:S_B1 + 1], 1.0 / D, None, ALU.mult),
                    waits=extra)
            ch.emit(lambda e: e.tensor_scalar(stv_sb[:, 4:5], st_sb[:, S_B2:S_B2 + 1], 1.0 / D, None, ALU.mult))
            ch.emit(lambda e: e.tensor_tensor(stv_sb[:, 3:4], mu, mu, ALU.mult))
            ch.emit(lambda e: e.tensor_tensor(stv_sb[:, 4:5], stv_sb[:, 4:5], stv_sb[:, 3:4], ALU.subtract))
            ch.emit(lambda e: e.tensor_scalar(st_sb[:, S_VE:S_VE + 1], stv_sb[:, 4:5], LN_EPS, None, ALU.add))
            v = st_sb[:, S_VE:S_VE + 1]
            r = st_sb[:, rstd_col:rstd_col + 1]
            t0 = stv_sb[:, 0:1]
            t1 = stv_sb[:, 1:2]
            nvh = stv_sb[:, 2:3]
            ch.emit(lambda e: e.tensor_scalar(t0.bitcast(I32), v.bitcast(I32), -0.5, RSQC, ALU.mult, ALU.add))
            ch.emit(lambda e: e.tensor_scalar(nvh, v, -0.5, None, ALU.mult))
            ch.emit(lambda e: e.tensor_copy(r, t0))
            for _ in range(4):
                ch.emit(lambda e: e.tensor_tensor(t1, r, r, ALU.mult))
                ch.emit(lambda e: e.scalar_tensor_tensor(t1, t1, nvh, c_15[:, :], ALU.mult, ALU.add))
                ch.emit(lambda e: e.tensor_tensor(r, r, t1, ALU.mult))
            ch.emit(lambda e: e.tensor_tensor(stv_sb[:, 5:6], mu, r, ALU.mult))
            ev_end = ch.emit(lambda e: e.tensor_scalar(stv_sb[:, 5:6], stv_sb[:, 5:6], -1.0, None, ALU.mult),
                             pub=sDV)
            if rstd_col == S_RSTD1:
                ev_end = ch.emit(lambda e: e.tensor_scalar(
                    st_sb[:, S_R1P:S_R1P + 1], r, 1.0, None, ALU.add), pub=sDV)
            return ev_end

        NT = n_tiles
        for blk in range(NT + 1):
            c = blk if blk < NT else None
            b = blk - 1 if blk >= 1 else None

            if c is not None:
                xsl = c % XR
                csl = c % 2
                evX = p.dma("sync", x_sb[:, xsl, :], x_in[c * TT:(c + 1) * TT, :],
                            waits=[evXfree[xsl]], pub=sX[c % 2])

            if c is not None:
                ev_s1 = p.emit("scalar", lambda e, xsl=xsl: e.activation(
                    sq_sb[:, :], x_sb[:, xsl, :], AF.Identity, accum_out=st_sb[:, S_B1:S_B1 + 1]),
                    waits=[evX, evConst, evB_t.get(c - 1), evSmall], pub=sSC)
                ev_s2 = p.emit("scalar", lambda e, xsl=xsl: e.activation(
                    sq_sb[:, :], x_sb[:, xsl, :], AF.Square, accum_out=st_sb[:, S_B2:S_B2 + 1]),
                    waits=[ev_s1], pub=sSC)

            chC = chA = None
            if b is not None:
                bsl = b % 2
                chC = p.chain(sDV)
                evXq_t[b] = quantize_chain(chC, 0, xn_sb[:, bsl, :], xq_sb[:, bsl, :], D,
                                           [ev_xs_t[b], evXqFree[bsl], evDbgXn_t.get(b)], sDV)
                evXnFree[bsl] = evXq_t[b]
            if c is not None:
                chA = p.chain(sDV)
                evA_t[c] = ln_small_chain(chA, S_RSTD1, extra=[ev_s2])
            p.seal("vector", chC, chA)

            if b is not None:
                evT = None
                for k in range(8):
                    ww = [evXq_t[b], bank["xqt"], evSmall] if k == 0 else ()
                    evT = p.emit("tensor", lambda e, k=k, bsl=bsl: e.transpose(
                        ps_xqt[:, k * 128:(k + 1) * 128], xq_sb[:, bsl, k * 128:(k + 1) * 128],
                        idb_sb[:, :]), waits=ww, pub=(sPE if k == 7 else None))
                peT_xq[b] = evT
                evXqE_t[b] = p.emit("scalar", lambda e, bsl=bsl: e.activation(
                    xqf_sb[:, bsl, :, :].rearrange("p c k -> p (c k)"), ps_xqt[:, :], AF.Copy),
                    waits=[peT_xq[b], peMf.get(b - 2, [None, None])[1]], pub=sSC)
                bank["xqt"] = evXqE_t[b]
                evXqFree[bsl] = evXqE_t[b]
                peT_xs[b] = [None, None]
                for half in range(2):
                    for k in range(4):
                        kk = half * 4 + k
                        ww = ()
                        if k == 0:
                            ww = [ev_xs_t[b], bank["xst"], evSmall] if half == 0 else [evXn2h0[b]]
                        evTs = p.emit("tensor", lambda e, kk=kk, k=k, bsl=bsl: e.transpose(
                            ps_xst[:, k * 128:(k + 1) * 128], xs_sb[:, bsl, kk * 128:(kk + 1) * 128],
                            idf_sb[:, :]), waits=ww, pub=(sPE if k == 3 else None))
                    peT_xs[b][half] = evTs
                    evXn2l = None
                    for k in range(4):
                        kk = half * 4 + k
                        ww = [peT_xs[b][half]] if k == 0 else []
                        if k == 0 and half == 0:
                            ww = [peT_xs[b][half], peMa.get(b - 2)]
                        evXn2l = p.emit("scalar", lambda e, kk=kk, k=k, bsl=bsl: e.activation(
                            xn2_sb[:, bsl, kk, :], ps_xst[:, k * 128:(k + 1) * 128], AF.Identity,
                            bias=lnb_sb[:, kk:kk + 1], scale=lnw_sb[:, kk:kk + 1]),
                            waits=ww, pub=(sSC if k == 3 else None))
                    if half == 0:
                        evXn2h0[b] = evXn2l
                bank["xst"] = evXn2l
                evXn2E_t[b] = evXn2l
                evXsFree[bsl] = evXn2l
                for dc in range(DC):
                    ww = [evXn2E_t[b], bank["attn"], evSmall] if dc == 0 else ()
                    p.emit("tensor", lambda e, dc=dc, bsl=bsl: e.matmul(
                        ps_attn[:, :], xn2_sb[:, bsl, dc, :], wattn_sb[:, dc, :],
                        start=(dc == 0), stop=False), waits=ww)
                peMa[b] = p.emit("tensor", lambda e: e.matmul(
                    ps_attn[:, :], ones_sb[:, :], battn_sb[:, :], start=False, stop=True), pub=sPE)
                evLgE_t[b] = p.emit("scalar", lambda e, bsl=bsl: e.activation(
                    lgs_sb[:, bsl, :], ps_attn[:, :], AF.Copy),
                    waits=[peMa[b], evLgSp_t.get(b - 2)], pub=sSC)
                bank["attn"] = evLgE_t[b]
                peMf[b] = [None, None]
                for half in range(2):
                    for dc in range(DC):
                        for q in range(4):
                            jt = half * 4 + q
                            ww = ()
                            if dc == 0 and q == 0:
                                if half == 0:
                                    ww = [evXqE_t[b], evWfcDone, evG_t.get((b - 1, 1)), evSmall]
                                else:
                                    ww = [evG_t[(b, 0)]]
                            p.emit("tensor", lambda e, dc=dc, q=q, jt=jt, bsl=bsl: e.matmul(
                                ps_fc[q][:, :], xqf_sb[:, bsl, dc, :],
                                wfcqT[:, dc, jt * 512:(jt + 1) * 512],
                                start=(dc == 0), stop=False), waits=ww)
                    evMfl = None
                    for q in range(4):
                        jt = half * 4 + q
                        evMfl = p.emit("tensor", lambda e, q=q, jt=jt: e.matmul(
                            ps_fc[q][:, :], ones_sb[:, :], bfc_sb[:, jt * 512:(jt + 1) * 512],
                            start=False, stop=True), pub=(sPE if q == 3 else None))
                    peMf[b][half] = evMfl
                    evG = None
                    for q in range(4):
                        w0 = []
                        if q == 0:
                            w0 = [peMf[b][half], evHq_t.get((b - 1, half)), evHqSp_t.get((b - 1, half)),
                                  evDbgH_t.get((b - 1, half))]
                        evG = p.emit("scalar", lambda e, q=q, half=half: e.activation(
                            h_sb[:, half, q * 512:(q + 1) * 512], ps_fc[q][:, :], AF.Gelu),
                            waits=w0, pub=(sSC if q == 3 else None))
                    evG_t[(b, half)] = evG
                    if dbg_h is not None:
                        evDbgH_t[(b, half)] = dbg_dma(
                            dbg_h[b * TT:(b + 1) * TT, half * 2048:(half + 1) * 2048],
                            h_sb[:, half, :], [evG])

            if c is not None:
                ev_xn = p.emit("scalar", lambda e, xsl=xsl, csl=csl: e.activation(
                    xn_sb[:, csl, :], x_sb[:, xsl, :], AF.Identity,
                    bias=stv_sb[:, 5:6], scale=st_sb[:, S_RSTD1:S_RSTD1 + 1],
                    accum_out=st_sb[:, S_B1:S_B1 + 1]),
                    waits=[evA_t[c], evXnFree[csl]], pub=sSC)
                ev_r = p.emit("scalar", lambda e, xsl=xsl, csl=csl: e.activation(
                    r_sb[:, csl, :], x_sb[:, xsl, :], AF.Identity,
                    bias=stv_sb[:, 5:6], scale=st_sb[:, S_R1P:S_R1P + 1]),
                    waits=[evRFree[csl]], pub=sSC)
                evXfree[xsl] = ev_r
                ev_s3_t[c] = p.emit("scalar", lambda e, csl=csl: e.activation(
                    sq_sb[:, :], xn_sb[:, csl, :], AF.Square, accum_out=st_sb[:, S_B2:S_B2 + 1]),
                    waits=[ev_xn, ev_s2], pub=sSC)
                ev_xs = None
            chD0 = chD1 = chB = None
            if b is not None:
                chD0 = p.chain(sDV)
                chD1 = p.chain(sDV)
                evHq_t[(b, 0)] = quantize_chain(chD0, 0, h_sb[:, 0, :], hq_sb[:, 0, :], 2048,
                                                [evG_t[(b, 0)], evHqSp_t.get((b - 1, 0)),
                                                 evDbgH_t.get((b, 0)), evDbgHq_t.get((b - 1, 0))], sDV)
                evHq_t[(b, 1)] = quantize_chain(chD1, 1, h_sb[:, 1, :], hq_sb[:, 1, :], 2048,
                                                [evG_t[(b, 1)], evHqSp_t.get((b - 1, 1)),
                                                 evDbgH_t.get((b, 1)), evDbgHq_t.get((b - 1, 1))], sDV)
            if c is not None:
                chB = p.chain(sDV)
                evB_t[c] = ln_small_chain(chB, S_RSTD2, extra=[ev_s3_t[c]])
            p.seal("vector", chD0, chD1, chB)

            if c is not None:
                ev_xs = p.emit("scalar", lambda e, csl=csl: e.activation(
                    xs_sb[:, csl, :], xn_sb[:, csl, :], AF.Identity,
                    bias=stv_sb[:, 5:6], scale=st_sb[:, S_RSTD2:S_RSTD2 + 1]),
                    waits=[evB_t[c], evXsFree[csl], ev_s3_t[c]], pub=sSC)
                ev_xs_t[c] = ev_xs
                evRFree[csl] = evRSp_t[c] = p.dma("gpsimd", r_scr.ap()[c * TT:(c + 1) * TT, :],
                                                  r_sb[:, csl, :], waits=[ev_r], pub=sRs[csl])
                if dbg_xn is not None:
                    evDbgXn_t[c] = dbg_dma(dbg_xn[c * TT:(c + 1) * TT, :], xn_sb[:, csl, :], [ev_xs])
                if dbg_xs is not None:
                    dbg_dma(dbg_xs[c * TT:(c + 1) * TT, :], xs_sb[:, csl, :], [ev_xs])

            if b is not None:
                for half in range(2):
                    evSpPrev = evHqSp_t.get((b - 1, half))
                    evHqEg = None
                    for g in range(2):
                        for k in range(8):
                            kk = g * 8 + k
                            ww = ()
                            if k == 0:
                                ww = [evHq_t[(b, half)], bank["hqt"], evSmall] if g == 0 else [evHqEg]
                            evTh = p.emit("tensor", lambda e, kk=kk, k=k, half=half: e.transpose(
                                ps_hqt[:, k * 128:(k + 1) * 128], hq_sb[:, half, kk * 128:(kk + 1) * 128],
                                idb_sb[:, :]), waits=ww, pub=(sPE if k == 7 else None))
                        evHqEg = p.emit("scalar", lambda e, half=half: e.activation(
                            hqf_sb[:, half, :, :].rearrange("p c k -> p (c k)"), ps_hqt[:, :], AF.Copy),
                            waits=[evTh, evSpPrev], pub=sSC)
                        bank["hqt"] = evHqEg
                        chunk0 = half * 16 + g * 8
                        off = (b * JC + chunk0) * 128 * 128
                        dsth = bass.AP(hq_scr, off, [[128, 128], [128 * 128, 8], [1, 128]])
                        evSpPrev = p.dma("gpsimd", dsth, hqf_sb[:, half, :, :],
                                         waits=[evHqEg], pub=sHqs[half])
                    evHqSp_t[(b, half)] = evSpPrev
                dstlg = bass.AP(lg_scr, b * H * TT * 88, [[88, 128], [TT * 88, 4], [1, 88]])
                evLgSp_t[b] = p.dma("gpsimd", dstlg,
                                    lgs_sb[:, bsl, :].rearrange("p (h w) -> p h w", h=4),
                                    waits=[evLgE_t[b]], pub=sLgs[b % 2])
                if dbg_lg is not None:
                    dbg_dma(dbg_lg[b * TT:(b + 1) * TT, :], lgs_sb[:, bsl, :], [evLgE_t[b]])
                if dbg_xq is not None:
                    dbg_dma(dbg_xq[b * TT:(b + 1) * TT, :], xq_sb[:, bsl, :], [evXq_t[b]])
                if dbg_hq is not None:
                    for half in range(2):
                        evDbgHq_t[(b, half)] = dbg_dma(
                            dbg_hq[b * TT:(b + 1) * TT, half * 2048:(half + 1) * 2048],
                            hq_sb[:, half, :], [evHq_t[(b, half)]])



        TPC = TT
        TL = n_tiles
        evP1 = p.emit("gpsimd", lambda e: e.memset(jnk[:, 3:4], 0.0),
                      waits=[x for x in [peMf.get(NT - 1, [None, None])[1],
                                         evHqSp_t.get((NT - 1, 0)), evHqSp_t.get((NT - 1, 1)),
                                         evRSp_t.get(NT - 1), evRSp_t.get(NT - 2),
                                         evLgSp_t.get(NT - 1), evLgSp_t.get(NT - 2),
                                         evXq_t.get(NT - 1), evXqE_t.get(NT - 1),
                                         evXn2E_t.get(NT - 1), peMa.get(NT - 1),
                                         ev_s3_t.get(NT - 1), evWpjDone, dbg_prev[0]] if x],
                      pub=sCst)
        free_phase1()

        dbg_E = dbg_out("E", [CP, TPC * 64])
        dbg_pidx = dbg_out("pidx", [CP, TPC * 8])
        dbg_psi = dbg_out("psi", [CP, TPC * 8])
        dbg_phi = dbg_out("phi", [CP, TPC * 8])
        dbg_dtil = dbg_out("dtil", [CP, NS * TPC])
        dbg_ctil = dbg_out("ctil", [CP, NS * TPC])
        dbg_ysT = dbg_out("ysT", [NS * H, T], BF16)

        wpqT_sb = sb("wpqT", [128, JC, 1024], BF16)
        idp_sb = sb("idp", [128, 128])
        psg2 = sb("psg2", [CP, TPC, 8])
        smallE = sb("smallE", [4, TL, 8])
        smallP = sb("smallP", [4, TL + 1, 8])
        smallU = sb("smallU", [4, 8])
        smallH = sb("smallH", [4, 8])
        smallT = sb("smallT", [4, 8, 8])
        agf = sb("agf", [1, 8, 32])
        magb = sb("magb", [1, 8, 32])
        g0r = sb("g0r", [1, 32])
        g0col = sb("g0col", [32, 1])
        g0m_sb = sb("g0m_sb", [1, 8])
        trib = sb("trib", [128, 8, 8])
        hqs_sb = sb("hqs", [128, 2, JC, 128], BF16)
        rld_sb = sb("rld", [128, 2, D])
        ffe_sb = sb("ffe", [128, 2, D])
        _t2 = []
        def sbt2(name, shape, dt=F32):
            cm = nc.sbuf_tensor(name, list(shape), dt)
            h2 = cm.__enter__()
            _t2.append(cm)
            return h2
        dtil = sbt2("dtil", [CP, NS, TPC])
        ctil = sbt2("ctil", [CP, NS, TPC])
        _t1 = []
        def sbt1(name, shape, dt=F32):
            cm = nc.sbuf_tensor(name, list(shape), dt)
            h1 = cm.__enter__()
            _t1.append(cm)
            return h1
        THL = TPC // 2
        TQ = 16
        lgA = sbt1("lgA", [CP, THL, 88])
        sgS = sbt1("sgS", [CP, THL, 8])
        sgG = sbt1("sgG", [CP, THL, 8])
        bvec = sbt1("bvec", [CP, THL, 8])
        dvec = sbt1("dvec", [CP, TPC, 8])
        cvec = sbt1("cvec", [CP, TPC, 8])
        m64b = sbt1("m64b", [CP, THL])
        rsb = sbt1("rsb", [CP, THL, 8])
        rrb = sbt1("rrb", [CP, THL, 8])
        srt = sbt1("srt", [CP, TQ, 8, 8])
        gtmp = sbt1("gtmp", [CP, TQ, 8, 8])
        rmaxb = sbt1("rmaxb", [CP, TPC, 8])
        rankb = sbt1("rankb", [CP, TPC, 8])
        avn = sbt1("avn", [CP, TQ, 8])
        mk8 = sbt1("mk8", [CP, TQ, 8])
        mkb = sbt1("mkb", [CP, TQ])
        t2b = sbt1("t2b", [CP, TQ, 8])
        m2b = sbt1("m2b", [CP, TQ])
        cselb = sbt1("cselb", [CP, TQ, 8])
        pidxs = sbt1("pidxs", [CP, TPC, 8])
        pidx = sbt1("pidx", [CP, TPC, 8])
        PDf = sbt1("PDf", [CP, TPC // 2, 8, 8], BF16)
        psig = sbt1("psig", [CP, TPC, 8])
        phib = sbt1("phib", [CP, TPC, 8])
        pst = sbt1("pst", [CP, 8, 8])

        ps_pj = [psum(f"ps_pj{i}", [128, 1024]) for i in range(2)]
        ps_mx = [psum(f"ps_mx{i}", [128, 1024]) for i in range(2)]

        sB2 = p.sem("sB2")
        sCC = p.sem("sCC")
        sHqL = [p.sem("sHqL0"), p.sem("sHqL1")]
        sRL = [p.sem("sRL0"), p.sem("sRL1")]
        sAcS = [p.sem("sAcS0"), p.sem("sAcS1")]
        sAcL = [p.sem("sAcL0"), p.sem("sAcL1")]
        sOu = [p.sem("sOu0"), p.sem("sOu1")]
        sWpL = p.sem("sWpL")
        b2_prev = [None]

        def b2dma(out, in_, waits):
            b2_prev[0] = p.dma("scalar", out, in_, waits=list(waits) + [b2_prev[0], evP1], pub=sB2)
            return b2_prev[0]

        evWpL = None
        for dcid in range(DC):
            srcw = bass.AP(wpqt_scr, dcid * 4 * 8 * 128 * 128,
                           [[128, 128], [8 * 128 * 128, 4], [128 * 128, 8], [1, 128]])
            ww0 = [evP1, evWqSp[0], evWqSp[1]] if dcid == 0 else [evP1, evWpL]
            evWpL = p.dma("sync", wpqT_sb[:, :, dcid * 128:(dcid + 1) * 128]
                          .rearrange("p (a b) k -> p a b k", a=4),
                          srcw, waits=ww0, pub=sWpL)
        evIdL = p.dma("sync", idp_sb[:, :], id_f, waits=[evP1, evWpL], pub=sWpL)
        evG0L = p.dma("sync", g0m_sb[:, :], g0m_in, waits=[evP1, evIdL], pub=sWpL)
        evHqL = [None, None]
        evRL = [None, None]
        evPjE = {}
        evAccSp = {}
        for tl in range(NT):
            sl2 = tl % 2
            evHqL[sl2] = p.dma("sync", hqs_sb[:, sl2, :, :],
                               bass.AP(hq_scr, tl * JC * 128 * 128,
                                       [[128, 128], [128 * 128, JC], [1, 128]]),
                               waits=[evPjE.get(tl - 2), evP1], pub=sHqL[sl2])
            evRL[sl2] = p.dma("sync", rld_sb[:, sl2, :], r_scr.ap()[tl * TT:(tl + 1) * TT, :],
                              waits=[evPjE.get(tl - 2), evP1], pub=sRL[sl2])
            for jc in range(JC):
                for hf in range(2):
                    ww = ()
                    if jc == 0 and hf == 0:
                        ww = [evHqL[sl2], evWpL, evPjE.get(tl - 2), evG0L]
                    p.emit("tensor", lambda e, jc=jc, hf=hf, sl2=sl2, tl=tl: e.matmul(
                        ps_pj[tl % 2][:, hf * 512:(hf + 1) * 512], hqs_sb[:, sl2, jc, :],
                        wpqT_sb[:, jc, hf * 512:(hf + 1) * 512],
                        start=(jc == 0), stop=False), waits=ww)
            for hf in range(2):
                p.emit("tensor", lambda e, hf=hf, sl2=sl2, tl=tl: e.matmul(
                    ps_pj[tl % 2][:, hf * 512:(hf + 1) * 512], idp_sb[:, :],
                    rld_sb[:, sl2, hf * 512:(hf + 1) * 512],
                    start=False, stop=False),
                    waits=[evRL[sl2], evIdL] if hf == 0 else ())
            evMj = None
            for hf in range(2):
                evMj = p.emit("tensor", lambda e, hf=hf, tl=tl: e.matmul(
                    ps_pj[tl % 2][:, hf * 512:(hf + 1) * 512], ones_sb[:, :],
                    bpj_sb[:, hf * 512:(hf + 1) * 512],
                    start=False, stop=True), pub=(sPE if hf == 1 else None))
            evPjE[tl] = p.emit("scalar", lambda e, sl2=sl2, tl=tl: e.activation(
                ffe_sb[:, sl2, :], ps_pj[tl % 2][:, :], AF.Copy),
                waits=[evMj, evAccSp.get(tl - 2)], pub=sSC)
            evAccSp[tl] = p.dma("sync", acc_scr.ap()[tl * TT:(tl + 1) * TT, :],
                                ffe_sb[:, sl2, :], waits=[evPjE[tl]], pub=sAcS[sl2])

        def drain_acc(k):
            pass

        ch = p.chain(sDV)
        ch.emit(lambda e: e.tensor_tensor(
            trib[:, :, :], bcast(iota8[:, :].unsqueeze(2), (128, 8, 8)),
            bcast(iota8[:, :].unsqueeze(1), (128, 8, 8)), ALU.is_lt),
            waits=[evConst, evP1])
        p.seal("vector", ch)
        evTri = ch.last

        evPsiL = None
        evDbgE = None
        evLgL = None
        for th in range(2):
            tof = th * THL
            evLgL = p.dma("scalar", lgA[:, :, :],
                          bass.AP(lg_scr, tof * 88, [[TPC * 88, CP], [88, THL], [1, 88]]),
                          waits=[evP1, evPsiL], pub=sB2)
            evSgS = p.emit("scalar", lambda e: e.activation(sgS[:, :, :], lgA[:, :, 64:72], AF.Sigmoid),
                           waits=[evLgL], pub=sSC)
            evSgG = p.emit("scalar", lambda e: e.activation(sgG[:, :, :], lgA[:, :, 72:80], AF.Sigmoid),
                           waits=[evSgS], pub=sSC)
            evBv = p.emit("scalar", lambda e: e.activation(bvec[:, :, :], lgA[:, :, 80:88], AF.Copy),
                          waits=[evSgG], pub=sSC)
            ch = p.chain(sDV)
            ch.emit(lambda e: e.tensor_scalar(sgS[:, :, :], sgS[:, :, :], 2.0, -1.0, ALU.mult, ALU.add),
                    waits=[evSgS])
            ch.emit(lambda e, tof=tof: e.tensor_tensor(dvec[:, tof:tof + THL, :], sgG[:, :, :],
                                                       sgS[:, :, :], ALU.mult), waits=[evSgG])
            ch.emit(lambda e: e.tensor_scalar(sgG[:, :, :], sgG[:, :, :], -1.0, 1.0, ALU.mult, ALU.add))
            ch.emit(lambda e, tof=tof: e.tensor_tensor(cvec[:, tof:tof + THL, :], sgG[:, :, :],
                                                       bvec[:, :, :], ALU.mult), waits=[evBv])
            ch.emit(lambda e: e.tensor_reduce(
                m64b[:, :].rearrange("p (t o) -> p t o", o=1),
                lgA[:, :, 0:64], axis=AX.X, op=ALU.max))
            evEsub = ch.emit(lambda e: e.tensor_tensor(
                lgA[:, :, 0:64], lgA[:, :, 0:64],
                bcast(m64b[:, :].rearrange("p (t o) -> p t o", o=1), (CP, THL, 64)), ALU.subtract))
            p.seal("vector", ch)
            drain_acc(2)
            evExp = p.emit("scalar", lambda e: e.activation(
                lgA[:, :, 0:64], lgA[:, :, 0:64], AF.Exp), waits=[evEsub], pub=sSC)
            E4 = lgA[:, :, 0:64].rearrange("p t (i j) -> p t i j", j=8)
            ch = p.chain(sDV)
            prevw = evExp
            for it in range(5):
                ch.emit(lambda e, E4=E4: e.tensor_reduce(
                    rsb[:, :, :].unsqueeze(3), E4,
                    axis=AX.X, op=ALU.add), waits=[prevw])
                prevw = None
                ch.emit(lambda e: e.reciprocal(rrb[:, :, :], rsb[:, :, :]))
                ch.emit(lambda e, E4=E4: e.tensor_tensor(
                    E4, E4, bcast(rrb[:, :, :].unsqueeze(3), (CP, THL, 8, 8)),
                    ALU.mult))
                ch.emit(lambda e, E4=E4: e.tensor_reduce(
                    rsb[:, :, :].unsqueeze(3),
                    E4.rearrange("p t i j -> p t j i"), axis=AX.X, op=ALU.add))
                ch.emit(lambda e: e.reciprocal(rrb[:, :, :], rsb[:, :, :]))
                ch.emit(lambda e, E4=E4: e.tensor_tensor(
                    E4, E4, bcast(rrb[:, :, :].unsqueeze(2), (CP, THL, 8, 8)),
                    ALU.mult))
                if it == 2:
                    p.seal("vector", ch)
                    drain_acc(2)
                    nch = p.chain(sDV)
                    nch.last = ch.last
                    ch = nch
            evSk = ch.last
            p.seal("vector", ch)
            drain_acc(2)
            if dbg_E is not None:
                evDbgE = b2dma(dbg_E[:, tof * 64:(tof + THL) * 64], lgA[:, :, 0:64], [evSk])
            evGr = evSk
            for tq in range(THL // TQ):
                t0 = tof + tq * TQ
                tl0 = tq * TQ
                Eh = E4[:, tl0:tl0 + TQ, :, :]
                ch = p.chain(sDV)
                ch.emit(lambda e, Eh=Eh, t0=t0: e.tensor_reduce(
                    rmaxb[:, t0:t0 + TQ, :].unsqueeze(3), Eh,
                    axis=AX.X, op=ALU.max), waits=[evGr])
                ch.emit(lambda e, t0=t0: e.tensor_tensor(
                    gtmp[:, :, :, :],
                    bcast(rmaxb[:, t0:t0 + TQ, :].unsqueeze(3), (CP, TQ, 8, 8)),
                    bcast(rmaxb[:, t0:t0 + TQ, :].unsqueeze(2), (CP, TQ, 8, 8)),
                    ALU.is_gt))
                ch.emit(lambda e, t0=t0: e.tensor_tensor(
                    srt[:, :, :, :],
                    bcast(rmaxb[:, t0:t0 + TQ, :].unsqueeze(3), (CP, TQ, 8, 8)),
                    bcast(rmaxb[:, t0:t0 + TQ, :].unsqueeze(2), (CP, TQ, 8, 8)),
                    ALU.is_equal))
                ch.emit(lambda e: e.tensor_tensor(
                    srt[:, :, :, :], srt[:, :, :, :],
                    bcast(trib[0:CP, :, :].unsqueeze(1), (CP, TQ, 8, 8)), ALU.mult),
                    waits=[evTri])
                ch.emit(lambda e: e.tensor_tensor(gtmp[:, :, :, :], gtmp[:, :, :, :],
                                                  srt[:, :, :, :], ALU.add))
                ch.emit(lambda e, t0=t0: e.tensor_reduce(
                    rankb[:, t0:t0 + TQ, :].unsqueeze(3),
                    gtmp[:, :, :, :].rearrange("p t a b -> p t b a"), axis=AX.X, op=ALU.add))
                for i in range(8):
                    ch.emit(lambda e, i=i, t0=t0: e.tensor_tensor(
                        mk8[:, :, :],
                        bcast(rankb[:, t0:t0 + TQ, i].rearrange("p (t o) -> p t o", o=1),
                              (CP, TQ, 8)),
                        bcast(iota8[0:CP, :].unsqueeze(1), (CP, TQ, 8)), ALU.is_equal))
                    if i == 0:
                        ch.emit(lambda e, i=i, Eh=Eh: e.tensor_tensor(
                            srt[:, :, :, :],
                            bcast(mk8[:, :, :].unsqueeze(3), (CP, TQ, 8, 8)),
                            bcast(Eh[:, :, i, :].unsqueeze(2), (CP, TQ, 8, 8)),
                            ALU.mult))
                    else:
                        ch.emit(lambda e, i=i, Eh=Eh: e.tensor_tensor(
                            gtmp[:, :, :, :],
                            bcast(mk8[:, :, :].unsqueeze(3), (CP, TQ, 8, 8)),
                            bcast(Eh[:, :, i, :].unsqueeze(2), (CP, TQ, 8, 8)),
                            ALU.mult))
                        ch.emit(lambda e: e.tensor_tensor(srt[:, :, :, :], srt[:, :, :, :],
                                                          gtmp[:, :, :, :], ALU.add))
                ch.emit(lambda e: e.memset(avn[:, :, :], 0.0))
                for s in range(8):
                    ch.emit(lambda e, s=s: e.tensor_tensor(mk8[:, :, :], srt[:, :, s, :],
                                                           avn[:, :, :], ALU.add))
                    ch.emit(lambda e: e.tensor_reduce(
                        mkb[:, :].rearrange("p (t o) -> p t o", o=1), mk8[:, :, :],
                        axis=AX.X, op=ALU.max))
                    ch.emit(lambda e: e.tensor_tensor(
                        t2b[:, :, :], mk8[:, :, :],
                        bcast(mkb[:, :].rearrange("p (t o) -> p t o", o=1), (CP, TQ, 8)),
                        ALU.is_equal))
                    ch.emit(lambda e: e.tensor_tensor(
                        t2b[:, :, :], t2b[:, :, :],
                        bcast(iorev8[0:CP, :].unsqueeze(1), (CP, TQ, 8)), ALU.mult))
                    ch.emit(lambda e: e.tensor_reduce(
                        m2b[:, :].rearrange("p (t o) -> p t o", o=1), t2b[:, :, :],
                        axis=AX.X, op=ALU.max))
                    ch.emit(lambda e: e.tensor_tensor(
                        cselb[:, :, :], t2b[:, :, :],
                        bcast(m2b[:, :].rearrange("p (t o) -> p t o", o=1), (CP, TQ, 8)),
                        ALU.is_equal))
                    ch.emit(lambda e: e.scalar_tensor_tensor(
                        avn[:, :, :], cselb[:, :, :], -BIG, avn[:, :, :], ALU.mult, ALU.add))
                    ch.emit(lambda e: e.tensor_tensor(
                        t2b[:, :, :], cselb[:, :, :],
                        bcast(iota8[0:CP, :].unsqueeze(1), (CP, TQ, 8)), ALU.mult))
                    ch.emit(lambda e, s=s, t0=t0: e.tensor_reduce(
                        pidxs[:, t0:t0 + TQ, s].rearrange("p (t o) -> p t o", o=1), t2b[:, :, :],
                        axis=AX.X, op=ALU.add))
                for rk in range(8):
                    ch.emit(lambda e, rk=rk, t0=t0: e.tensor_scalar(
                        mk8[:, :, :], rankb[:, t0:t0 + TQ, :], float(rk), None, ALU.is_equal))
                    if rk == 0:
                        ch.emit(lambda e, rk=rk, t0=t0: e.tensor_tensor(
                            pidx[:, t0:t0 + TQ, :], mk8[:, :, :],
                            bcast(pidxs[:, t0:t0 + TQ, rk].rearrange("p (t o) -> p t o", o=1),
                                  (CP, TQ, 8)), ALU.mult))
                    else:
                        ch.emit(lambda e, rk=rk, t0=t0: e.tensor_tensor(
                            t2b[:, :, :], mk8[:, :, :],
                            bcast(pidxs[:, t0:t0 + TQ, rk].rearrange("p (t o) -> p t o", o=1),
                                  (CP, TQ, 8)), ALU.mult))
                        ch.emit(lambda e, t0=t0: e.tensor_tensor(
                            pidx[:, t0:t0 + TQ, :], pidx[:, t0:t0 + TQ, :], t2b[:, :, :],
                            ALU.add))
                p.seal("vector", ch)
                evGr = ch.last
                drain_acc(2)
            ch = p.chain(sDV)
            ch.emit(lambda e, tof=tof: e.tensor_tensor(
                PDf[:, :, :, :],
                bcast(pidx[:, tof:tof + THL, :].unsqueeze(3), (CP, THL, 8, 8)),
                bcast(iota8[0:CP, :].unsqueeze(1).unsqueeze(1), (CP, THL, 8, 8)), ALU.is_equal),
                waits=[evGr, evDbgE])
            for t in range(tof, tof + THL):
                if t == 0:
                    ch.emit(lambda e: e.tensor_tensor(
                        pst[:, :, :], PDf[:, 0, :, :],
                        bcast(iota8[0:CP, :].unsqueeze(1), (CP, 8, 8)), ALU.mult))
                else:
                    ch.emit(lambda e, t=t, tof=tof: e.tensor_tensor(
                        pst[:, :, :], PDf[:, t - tof, :, :],
                        bcast(psig[:, t - 1, :].unsqueeze(1), (CP, 8, 8)), ALU.mult))
                ch.emit(lambda e, t=t: e.tensor_reduce(
                    psig[:, t, :].rearrange("p (i o) -> p i o", o=1), pst[:, :, :],
                    axis=AX.X, op=ALU.add))
            p.seal("vector", ch)
            evPsiL = ch.last
            drain_acc(2)
            if dbg_pidx is not None:
                b2dma(dbg_pidx[:, tof * 8:(tof + THL) * 8],
                      pidx[:, tof:tof + THL, :].rearrange("p t i -> p (t i)"), [evPsiL])

        evB1 = b2dma(bass.AP(pre1_scr, 0, [[8, CP], [1, 8]]), psig[:, TPC - 1, :], [evPsiL])
        evB2 = b2dma(smallE[:, :, :], bass.AP(pre1_scr, 0, [[8, 4], [32, TL], [1, 8]]), [evB1])
        ch = p.chain(sDV)
        ch.emit(lambda e: e.tensor_copy(smallP[:, 0, :], iota8[0:4, :]), waits=[evB2])
        for cc in range(1, TL + 1):
            for j in range(8):
                ch.emit(lambda e, cc=cc, j=j: e.tensor_scalar(
                    smallT[:, j, :], smallE[:, cc - 1, :], float(j),
                    smallP[:, cc - 1, j:j + 1], ALU.is_equal, ALU.mult),
                    independent=(j != 0))
            ch.emit(lambda e, cc=cc: e.tensor_reduce(
                smallP[:, cc, :].rearrange("p (i o) -> p i o", o=1),
                smallT[:, :, :].rearrange("p j i -> p i j"), axis=AX.X, op=ALU.add))
        p.seal("vector", ch)
        evPre = ch.last
        evB3 = b2dma(bass.AP(pre2_scr, 0, [[8, 4], [32, TL], [1, 8]]), smallP[:, 0:TL, :], [evPre])
        evB4 = b2dma(rrb[:, 0, :], bass.AP(pre2_scr, 0, [[8, CP], [1, 8]]), [evB3])
        ch = p.chain(sDV)
        for j in range(8):
            if j == 0:
                ch.emit(lambda e, j=j: e.tensor_scalar(
                    psg2[:, :, :], psig[:, :, :], float(j),
                    rrb[:, 0, j:j + 1], ALU.is_equal, ALU.mult), waits=[evB4, evPsiL])
            else:
                ch.emit(lambda e, j=j: e.tensor_scalar(
                    rmaxb[:, :, :], psig[:, :, :], float(j),
                    rrb[:, 0, j:j + 1], ALU.is_equal, ALU.mult))
                ch.emit(lambda e: e.tensor_tensor(psg2[:, :, :], psg2[:, :, :],
                                                  rmaxb[:, :, :], ALU.add))
        ch.emit(lambda e: e.memset(phib[:, :, :], 0.0))
        for i in range(1, 8):
            ch.emit(lambda e, i=i: e.tensor_tensor(
                rmaxb[:, :, :],
                bcast(psg2[:, :, i].rearrange("p (t o) -> p t o", o=1), (CP, TPC, 8)),
                bcast(iota8[0:CP, :].unsqueeze(1), (CP, TPC, 8)), ALU.is_equal))
            ch.emit(lambda e, i=i: e.scalar_tensor_tensor(
                phib[:, :, :], rmaxb[:, :, :], float(i), phib[:, :, :], ALU.mult, ALU.add))
        for (dst, vsrc) in ((dtil, dvec), (ctil, cvec)):
            dview = dst[:, :, :].rearrange("p i t -> p t i")
            for j in range(8):
                if j == 0:
                    ch.emit(lambda e, j=j: e.tensor_scalar(
                        rmaxb[:, :, :], phib[:, :, :], float(j), None, ALU.is_equal))
                    ch.emit(lambda e, j=j, dview=dview, vsrc=vsrc: e.tensor_tensor(
                        dview, rmaxb[:, :, :],
                        bcast(vsrc[:, :, j].rearrange("p (t o) -> p t o", o=1), (CP, TPC, 8)),
                        ALU.mult))
                else:
                    ch.emit(lambda e, j=j: e.tensor_scalar(
                        rmaxb[:, :, :], phib[:, :, :], float(j), None, ALU.is_equal))
                    ch.emit(lambda e, j=j, vsrc=vsrc: e.tensor_tensor(
                        rmaxb[:, :, :], rmaxb[:, :, :],
                        bcast(vsrc[:, :, j].rearrange("p (t o) -> p t o", o=1), (CP, TPC, 8)),
                        ALU.mult))
                    ch.emit(lambda e, dview=dview: e.tensor_tensor(
                        dview, dview, rmaxb[:, :, :], ALU.add))
        p.seal("vector", ch)
        evGath = ch.last
        drain_acc(4)
        if dbg_phi is not None:
            b2dma(dbg_phi[:, :], phib[:, :, :].rearrange("p t i -> p (t i)"), [evGath])
        if dbg_psi is not None:
            b2dma(dbg_psi[:, :], psg2[:, :, :].rearrange("p t i -> p (t i)"), [evGath])
        if dbg_dtil is not None:
            b2dma(dbg_dtil[:, :], dtil[:, :, :].rearrange("p i t -> p (i t)"), [evGath])
        if dbg_ctil is not None:
            b2dma(dbg_ctil[:, :], ctil[:, :, :].rearrange("p i t -> p (i t)"), [evGath])

        evB5 = b2dma(bass.AP(dc_scr, 0, [[NS * TPC, CP], [TPC, NS], [1, TPC]]),
                     dtil[:, :, :], [evGath])
        evB6 = b2dma(bass.AP(dc_scr, CP * NS * TPC, [[NS * TPC, CP], [TPC, NS], [1, TPC]]),
                     ctil[:, :, :], [evB5])
        for cm2 in reversed(_t1):
            cm2.__exit__(None, None, None)
        for cm2 in reversed(_t2):
            cm2.__exit__(None, None, None)
        packD = sb("packD", [32, T])
        packC = sb("packC", [32, T])
        packU = sb("packU", [32, T])
        packZ = sb("packZ", [32, T])
        ysT_bf = sb("ysTbf", [NS * H, T], BF16)
        ysb = sb("ysb", [CP, NS, TPC])
        ysb_bf = sb("ysbbf", [CP, NS, TPC], BF16)
        gch = sb("gch", [CP, NS, TPC])
        scrT = sb("scrT", [CP, TPC, 8])
        outwL = sb("outwL", [NS * H, D], BF16)
        outwF = sb("outwF", [NS * H, D])
        outbL = sb("outbL", [1, D])
        evB7 = b2dma(packD[:, :].rearrange("p (c t) -> p c t", t=TPC),
                     bass.AP(dc_scr, 0, [[TPC, 32], [NS * TPC * 4, TL], [1, TPC]]), [evB6])
        evB8 = b2dma(packC[:, :].rearrange("p (c t) -> p c t", t=TPC),
                     bass.AP(dc_scr, CP * NS * TPC, [[TPC, 32], [NS * TPC * 4, TL], [1, TPC]]),
                     [evB7])
        evZer = p.emit("gpsimd", lambda e: e.memset(packZ[:, :], 0.0),
                       waits=[evB6], pub=sCst)
        ch = p.chain(sDV)
        ch.emit(lambda e: e.tensor_tensor_scan(
            packU[:, :], packD[:, :], packC[:, :], 0.0, ALU.mult, ALU.add),
            waits=[evB8, evZer])
        ch.emit(lambda e: e.tensor_tensor_scan(
            packD[:, :], packD[:, :], packZ[:, :], 1.0, ALU.mult, ALU.add))
        p.seal("vector", ch)
        evScan = ch.last
        evB9 = b2dma(bass.AP(ue_scr, 0, [[1, 32], [1, 1]]), packU[:, T - 1:T], [evScan])
        evB10 = b2dma(smallU[:, :], bass.AP(ue_scr, 0, [[8, 4], [1, 8]]), [evB9])
        ch = p.chain(sDV)
        for j in range(8):
            ch.emit(lambda e, j=j: e.tensor_scalar(
                smallT[:, j, :], smallP[:, TL, :], float(j),
                smallU[:, j:j + 1], ALU.is_equal, ALU.mult),
                waits=[evB10] if j == 0 else (), independent=(j != 0))
        ch.emit(lambda e: e.tensor_reduce(
            smallH[:, :].rearrange("p (i o) -> p i o", o=1),
            smallT[:, :, :].rearrange("p j i -> p i j"), axis=AX.X, op=ALU.add))
        p.seal("vector", ch)
        evHmd = ch.last
        evB11 = b2dma(bass.AP(hm_scr, 0, [[8, 4], [1, 8]]), smallH[:, :], [evHmd])
        evCC = p.emit("gpsimd", lambda e: e.collective_compute(
            "AllGather", ALU.bypass, replica_groups=[list(range(N_CORES))],
            ins=[hm_scr.ap()], outs=[ag_scr.ap()]), waits=[evB11], pub=sCC)
        evB12 = b2dma(agf[:, :, :], bass.AP(ag_scr, 0, [[256, 1], [32, 8], [1, 32]]), [evCC])
        ch = p.chain(sDV)
        ch.emit(lambda e: e.tensor_tensor(
            magb[:, :, :], agf[:, :, :],
            bcast(g0m_sb[:, :].unsqueeze(2), (1, 8, 32)), ALU.mult),
            waits=[evB12, evG0L])
        ch.emit(lambda e: e.tensor_reduce(
            g0r[:, :].rearrange("p (k o) -> p k o", o=1),
            magb[:, :, :].rearrange("p r k -> p k r"), axis=AX.X, op=ALU.add))
        p.seal("vector", ch)
        evG0 = ch.last
        evB13 = b2dma(bass.AP(g0_scr, 0, [[32, 1], [1, 32]]), g0r[:, :], [evG0])
        evB14 = b2dma(g0col[:, :], bass.AP(g0_scr, 0, [[1, 32], [1, 1]]), [evB13])
        ch = p.chain(sDV)
        ch.emit(lambda e: e.tensor_scalar(
            packC[:, :], packD[:, :], g0col[:, :], None, ALU.mult),
            waits=[evB14, evScan])
        ch.emit(lambda e: e.tensor_tensor(
            packC[:, :], packC[:, :], packU[:, :], ALU.add))
        p.seal("vector", ch)
        evY = ch.last
        evB15 = b2dma(bass.AP(g_scr, 0, [[TPC, 32], [NS * TPC * 4, TL], [1, TPC]]),
                      packC[:, :].rearrange("p (c t) -> p c t", t=TPC), [evY])
        evB16 = b2dma(gch[:, :, :], bass.AP(g_scr, 0, [[NS * TPC, CP], [TPC, NS], [1, TPC]]),
                      [evB15])
        ch = p.chain(sDV)
        yview = ysb[:, :, :].rearrange("p i t -> p t i")
        for j in range(8):
            ch.emit(lambda e, j=j: e.tensor_scalar(
                scrT[:, :, :], psg2[:, :, :], float(j), None, ALU.is_equal),
                waits=[evB16] if j == 0 else ())
            if j == 0:
                ch.emit(lambda e, j=j: e.tensor_tensor(
                    yview, scrT[:, :, :],
                    bcast(gch[:, j, :].rearrange("p (t o) -> p t o", o=1), (CP, TPC, 8)),
                    ALU.mult))
            else:
                ch.emit(lambda e, j=j: e.tensor_tensor(
                    scrT[:, :, :], scrT[:, :, :],
                    bcast(gch[:, j, :].rearrange("p (t o) -> p t o", o=1), (CP, TPC, 8)),
                    ALU.mult))
                ch.emit(lambda e: e.tensor_tensor(yview, yview, scrT[:, :, :], ALU.add))
        ch.emit(lambda e: e.tensor_copy(ysb_bf[:, :, :], ysb[:, :, :]))
        p.seal("vector", ch)
        evYs = ch.last
        drain_acc(32)
        evB17 = b2dma(bass.AP(ys_scr, 0, [[NS * TPC, CP], [TPC, NS], [1, TPC]]),
                      ysb_bf[:, :, :], [evYs])
        evYsT = b2dma(ysT_bf[:, :].rearrange("p (c t) -> p c t", t=TPC),
                      bass.AP(ys_scr, 0, [[TPC, 32], [NS * TPC * 4, TL], [1, TPC]]), [evB17])
        if dbg_ysT is not None:
            b2dma(dbg_ysT[:, :], ysT_bf[:, :], [evYsT])

        evOwF = p.dma("sync", outwF[:, :], outw_in,
                      waits=[evP1, evG0L, evGath, evB6], pub=sWpL)
        evObL = p.dma("sync", outbL[:, :], outb_in, waits=[evP1, evOwF], pub=sWpL)
        ch = p.chain(sDV)
        ch.emit(lambda e: e.tensor_copy(outwL[:, :], outwF[:, :]), waits=[evOwF])
        p.seal("vector", ch)
        evOwc2 = ch.last

        evMxE = {}
        evOut = {}
        evAcL = [None, None]
        for tl in range(NT):
            sl3 = tl % 2
            rfree = evPjE[NT - 2 + sl3] if tl < 2 else evMxE[tl - 2]
            evAcL[sl3] = p.dma("sync", rld_sb[:, sl3, :], acc_scr.ap()[tl * TT:(tl + 1) * TT, :],
                               waits=[evAccSp[tl], rfree], pub=sAcL[sl3])
            for hf in range(2):
                ww = [evYsT, evMxE.get(tl - 2), evOwc2, evObL] if hf == 0 else ()
                p.emit("tensor", lambda e, hf=hf, tl=tl: e.matmul(
                    ps_mx[tl % 2][:, hf * 512:(hf + 1) * 512],
                    ysT_bf[:, tl * TT:(tl + 1) * TT], outwL[:, hf * 512:(hf + 1) * 512],
                    start=True, stop=False), waits=ww)
            for hf in range(2):
                p.emit("tensor", lambda e, hf=hf, sl3=sl3, tl=tl: e.matmul(
                    ps_mx[tl % 2][:, hf * 512:(hf + 1) * 512], idp_sb[:, :],
                    rld_sb[:, sl3, hf * 512:(hf + 1) * 512],
                    start=False, stop=False),
                    waits=[evAcL[sl3]] if hf == 0 else ())
            evMm = None
            for hf in range(2):
                evMm = p.emit("tensor", lambda e, hf=hf, tl=tl: e.matmul(
                    ps_mx[tl % 2][:, hf * 512:(hf + 1) * 512], ones_sb[:, :],
                    outbL[:, hf * 512:(hf + 1) * 512],
                    start=False, stop=True), pub=(sPE if hf == 1 else None))
            ffree = evAccSp[NT - 2 + sl3] if tl < 2 else evOut[tl - 2]
            evMxE[tl] = p.emit("scalar", lambda e, sl3=sl3, tl=tl: e.activation(
                ffe_sb[:, sl3, :], ps_mx[tl % 2][:, :], AF.Copy),
                waits=[evMm, ffree], pub=sSC)
            evOut[tl] = p.dma("sync", out_ap[tl * TT:(tl + 1) * TT, :], ffe_sb[:, sl3, :],
                              waits=[evMxE[tl]], pub=sOu[sl3])

        fin = []
        for s in (sWL, sSC, sDV, sPE, sCst, sDbg, sB2, sCC, sWpL,
                  sX[0], sX[1], sWst[0], sWst[1], sWqS[0], sWqS[1],
                  sRs[0], sRs[1], sLgs[0], sLgs[1], sHqs[0], sHqs[1],
                  sHqL[0], sHqL[1], sRL[0], sRL[1], sAcS[0], sAcS[1],
                  sAcL[0], sAcL[1], sOu[0], sOu[1]):
            e2 = Ev(s)
            e2.val = s.total
            fin.append(e2)
        p.emit("sync", lambda e: e.nop(), waits=fin)
        p.finalize(block)
    es.close()
    return nc, dbg


def shard_inputs(inputs, core_id, n_tiles=32):
    T = n_tiles * TT
    b, half = core_id // 2, core_id % 2
    x = np.ascontiguousarray(inputs["L0_attn_x"][b, half * 4096: half * 4096 + T]).astype(np.float32, copy=False)
    pw, sw = inputs["L0_attn_perm_W"], inputs["L0_attn_sign_W"]
    gw, iw = inputs["L0_attn_gate_W"], inputs["L0_attn_in_W"]
    pb, sbv = inputs["L0_attn_perm_b"], inputs["L0_attn_sign_b"]
    gb, ib = inputs["L0_attn_gate_b"], inputs["L0_attn_in_b"]
    w_attn = np.concatenate(
        [np.concatenate([pw[:, h * 64:(h + 1) * 64], sw[:, h * 8:(h + 1) * 8],
                         gw[:, h * 8:(h + 1) * 8], iw[:, h * 8:(h + 1) * 8]], axis=1)
         for h in range(H)], axis=1)
    b_attn = np.concatenate(
        [np.concatenate([pb[h * 64:(h + 1) * 64], sbv[h * 8:(h + 1) * 8],
                         gb[h * 8:(h + 1) * 8], ib[h * 8:(h + 1) * 8]])
         for h in range(H)])[None, :]
    g0mask = np.zeros((1, 8), np.float32)
    if core_id % 2 == 1:
        g0mask[0, core_id - 1] = 1.0
    im = {
        "x_sh": x,
        "w_attn": np.ascontiguousarray(w_attn, dtype=np.float32),
        "b_attn": np.ascontiguousarray(b_attn, dtype=np.float32),
        "lnw_rs": np.ascontiguousarray(np.asarray(inputs["L0_attn_ln_w"]).reshape(DC, 128).T, dtype=np.float32),
        "lnb_rs": np.ascontiguousarray(np.asarray(inputs["L0_attn_ln_b"]).reshape(DC, 128).T, dtype=np.float32),
        "w_fc": np.ascontiguousarray(inputs["L0_ffn_w_fc"], dtype=np.float32),
        "b_fc": np.ascontiguousarray(np.asarray(inputs["L0_ffn_b_fc"])[None, :], dtype=np.float32),
        "w_proj": np.ascontiguousarray(inputs["L0_ffn_w_proj"], dtype=np.float32),
        "b_proj": np.ascontiguousarray(np.asarray(inputs["L0_ffn_b_proj"])[None, :], dtype=np.float32),
        "out_w": np.ascontiguousarray(inputs["L0_attn_out_W"], dtype=np.float32),
        "out_b": np.ascontiguousarray(np.asarray(inputs["L0_attn_out_b"])[None, :], dtype=np.float32),
        "id_f": np.eye(128, dtype=np.float32),
        "id_b": np.eye(128).astype(ml_dtypes.bfloat16),
        "ones_r": np.ones((1, 128), np.float32),
        "g0mask": g0mask,
    }
    return im


def kernel(**inputs):
    n_tiles = 32
    nc, _ = build(n_tiles=n_tiles)
    in_maps = [shard_inputs(inputs, c, n_tiles) for c in range(N_CORES)]
    res = run_bass_kernel_spmd(nc, in_maps, core_ids=list(range(N_CORES)))
    out = np.zeros((4, 8192, 1024), dtype=np.float32)
    for c in range(N_CORES):
        b, half = c // 2, c % 2
        out[b, half * 4096:(half + 1) * 4096] = res.results[c]["out_sh"]
    return out


# version 03
# speedup vs baseline: 1.0770x; 1.0770x over previous
import numpy as np
import ml_dtypes
import concourse.bass as bass
import concourse.mybir as mybir
from concourse.bass_utils import run_bass_kernel_spmd

F32 = mybir.dt.float32
BF16 = mybir.dt.bfloat16
I32 = mybir.dt.int32
AF = mybir.ActivationFunctionType
ALU = mybir.AluOpType
AX = mybir.AxisListType

N_CORES = 8
D = 1024
DC = 8
JH = 4096
JC = 32
H = 4
NS = 8
COLS = 352
TT = 128
BIG = 1.0e30
MAGIC = 12582912.0
EMASK = 0x7F800000
EXP3 = float(3 << 23)
INVC = float(254 << 23)
RSQC = float(0x5F3759DF)
LN_EPS = 1e-5

ENGINES = ("sync", "scalar", "vector", "gpsimd", "tensor")


class Ev:
    __slots__ = ("sem", "val")

    def __init__(self, sem):
        self.sem = sem
        self.val = None


class SemC:
    __slots__ = ("h", "total")

    def __init__(self, h):
        self.h = h
        self.total = 0


class Chain:

    def __init__(self, plan, sem):
        self.plan = plan
        self.sem = sem
        self.items = []
        self.last = None

    def emit(self, fn, waits=(), pub=None, inc=1, independent=False):
        pub = pub if pub is not None else self.sem
        ev = Ev(pub)
        prev = None if independent else self.last
        w = tuple(w for w in list(waits) + [prev] if w is not None)
        self.items.append((fn, w, pub, inc, ev))
        self.last = ev
        return ev


class Plan:
    def __init__(self, nc, es, spacers):
        self.nc = nc
        self.es = es
        self.streams = {e: [] for e in ENGINES}
        self._semidx = 0
        self.spacers = spacers

    def sem(self, name=None):
        h = self.es.enter_context(self.nc.semaphore(name or f"s{self._semidx}"))
        self._semidx += 1
        return SemC(h)

    def chain(self, sem):
        return Chain(self, sem)

    def _append(self, engine, fn, waits, pub, inc, ev):
        if pub is not None:
            pub.total += inc
            if ev is not None:
                ev.val = pub.total
        self.streams[engine].append((fn, waits, pub, inc, ev))

    def emit(self, engine, fn, waits=(), pub=None, inc=1):
        ev = Ev(pub) if pub is not None else None
        self._append(engine, fn, tuple(w for w in waits if w is not None), pub, inc, ev)
        return ev

    def dma(self, queue, out, in_, waits=(), pub=None):
        return self.emit(queue, lambda e, o=out, i=in_: e.dma_start(out=o, in_=i),
                         waits=waits, pub=pub, inc=16)

    def seal(self, engine, *chains):
        chains = [list(c.items) for c in chains if c is not None and c.items]
        if not chains:
            return
        idx = [0] * len(chains)
        while True:
            live = [k for k in range(len(chains)) if idx[k] < len(chains[k])]
            if not live:
                break
            for k in live:
                fn, waits, pub, inc, ev = chains[k][idx[k]]
                self._append(engine, fn, waits, pub, inc, ev)
                idx[k] += 1

    def finalize(self, block, hw_fast=True):
        pub_engines = {}
        dma_sems = set()
        for engx in ENGINES:
            for fn, waits, pub, inc, ev in self.streams[engx]:
                if pub is not None:
                    pub_engines.setdefault(id(pub), set()).add(engx)
                    if inc != 1:
                        dma_sems.add(id(pub))

        def section(engname):
            def _(eng):
                waited = {}
                published = {}
                prev_pub = None
                for fn, waits, pub, inc, ev in self.streams[engname]:
                    spaced = False
                    for w in waits:
                        assert w.val is not None, f"unsealed event waited on {engname}"
                        sid = id(w.sem)
                        if (hw_fast and inc == 1 and sid not in dma_sems
                                and pub_engines.get(sid) == {engname}
                                and published.get(sid, 0) >= w.val):
                            if (not spaced and prev_pub is not None
                                    and prev_pub[0] == sid and prev_pub[1] == w.val):
                                self.spacers[engname](eng)
                                spaced = True
                                prev_pub = None
                            continue
                        if waited.get(sid, -1) < w.val:
                            eng.wait_ge(w.sem.h, w.val)
                            waited[sid] = w.val
                    inst = fn(eng)
                    if pub is not None:
                        assert inst is not None, f"pub with no instruction on {engname}"
                        inst.then_inc(pub.h, inc)
                        sid2 = id(pub)
                        published[sid2] = published.get(sid2, 0) + inc
                        prev_pub = (sid2, published[sid2])
                    else:
                        prev_pub = None
            return _
        block.sync(section("sync"))
        block.scalar(section("scalar"))
        block.vector(section("vector"))
        block.gpsimd(section("gpsimd"))
        block.tensor(section("tensor"))


def bcast(ap, shape):
    return ap.broadcast_to(list(shape))


def build(n_tiles=32, debug=()):
    from contextlib import ExitStack
    nc = bass.Bass("TRN2", target_bir_lowering=False)
    T = n_tiles * TT
    CP = n_tiles * H

    x_in = nc.dram_tensor("x_sh", [T, D], F32, kind="ExternalInput").ap()
    w_attn = nc.dram_tensor("w_attn", [D, COLS], F32, kind="ExternalInput").ap()
    b_attn = nc.dram_tensor("b_attn", [1, COLS], F32, kind="ExternalInput").ap()
    lnw_in = nc.dram_tensor("lnw_rs", [128, DC], F32, kind="ExternalInput").ap()
    lnb_in = nc.dram_tensor("lnb_rs", [128, DC], F32, kind="ExternalInput").ap()
    wfc_in = nc.dram_tensor("w_fc", [JH, D], F32, kind="ExternalInput").ap()
    bfc_in = nc.dram_tensor("b_fc", [1, JH], F32, kind="ExternalInput").ap()
    wpj_in = nc.dram_tensor("w_proj", [D, JH], F32, kind="ExternalInput").ap()
    bpj_in = nc.dram_tensor("b_proj", [1, D], F32, kind="ExternalInput").ap()
    outw_in = nc.dram_tensor("out_w", [NS * H, D], F32, kind="ExternalInput").ap()
    outb_in = nc.dram_tensor("out_b", [1, D], F32, kind="ExternalInput").ap()
    id_f = nc.dram_tensor("id_f", [128, 128], F32, kind="ExternalInput").ap()
    id_b = nc.dram_tensor("id_b", [128, 128], BF16, kind="ExternalInput").ap()
    ones_r = nc.dram_tensor("ones_r", [1, 128], F32, kind="ExternalInput").ap()
    g0m_in = nc.dram_tensor("g0mask", [1, 8], F32, kind="ExternalInput").ap()

    out_t = nc.dram_tensor("out_sh", [T, D], F32, kind="ExternalOutput")
    out_ap = out_t.ap()

    r_scr = nc.dram_tensor("r_scr", [T, D], F32)
    lg_scr = nc.dram_tensor("lg_scr", [n_tiles * H * TT * 88], F32)
    hq_scr = nc.dram_tensor("hq_scr", [n_tiles * JC * 128 * 128], BF16)
    wpqt_scr = nc.dram_tensor("wpqt_scr", [DC * 4 * 8 * 128 * 128], BF16)
    acc_scr = nc.dram_tensor("acc_scr", [T, D], F32)
    pre1_scr = nc.dram_tensor("pre1_scr", [CP * NS], F32)
    pre2_scr = nc.dram_tensor("pre2_scr", [CP * NS], F32)
    dc_scr = nc.dram_tensor("dc_scr", [2 * CP * NS * TT], F32)
    ue_scr = nc.dram_tensor("ue_scr", [H * NS], F32)
    hm_scr = nc.dram_tensor("hm_scr", [1, H * NS], F32)
    ag_scr = nc.dram_tensor("ag_scr", [N_CORES, H * NS], F32)
    g0_scr = nc.dram_tensor("g0_scr", [H * NS], F32)
    g_scr = nc.dram_tensor("g_scr", [CP * NS * TT], F32)
    ys_scr = nc.dram_tensor("ys_scr", [CP * NS * TT], BF16)

    dbg = {}

    def dbg_out(name, shape, dt=F32):
        if name in debug:
            dbg[name] = nc.dram_tensor("dbg_" + name, list(shape), dt, kind="ExternalOutput").ap()
            return dbg[name]
        return None

    es = ExitStack()
    sb = lambda name, shape, dt=F32: es.enter_context(nc.sbuf_tensor(name, list(shape), dt))
    psum = lambda name, shape, dt=F32: es.enter_context(nc.psum_tensor(name, list(shape), dt))
    _frees = []

    def sbf(name, shape, dt=F32):
        cm = nc.sbuf_tensor(name, list(shape), dt)
        h = cm.__enter__()
        _frees.append(cm)
        return h

    def psumf(name, shape, dt=F32):
        cm = nc.psum_tensor(name, list(shape), dt)
        h = cm.__enter__()
        _frees.append(cm)
        return h

    def free_phase1():
        for cm in reversed(_frees):
            cm.__exit__(None, None, None)
        _frees.clear()

    ones_sb = sb("ones1", [1, 128])
    bpj_sb = sb("bpj", [1, D])
    c_emask = sb("c_emask", [128, 1], I32)
    c_15 = sb("c_15", [128, 1])
    iota8 = sb("iota8", [128, 8])
    iorev8 = sb("iorev8", [128, 8])
    jnk = sb("jnk", [128, 4])
    XR = 2
    qs_amax = [sbf(f"q_amax{i}", [128, 64]) for i in range(2)]
    qs_sclb = [sbf(f"q_sclb{i}", [128, 64], I32) for i in range(2)]
    qs_invb = [sbf(f"q_invb{i}", [128, 64], I32) for i in range(2)]
    wattn_sb = sbf("wattn", [128, DC, COLS])
    battn_sb = sbf("battn", [1, COLS])
    lnw_sb = sbf("lnw", [128, DC])
    lnb_sb = sbf("lnb", [128, DC])
    idf_sb = sbf("idf", [128, 128])
    idb_sb = sbf("idb", [128, 128], BF16)
    bfc_sb = sbf("bfc", [1, JH])
    st_sb = sbf("st_sb", [128, 16])
    stv_sb = sbf("stv_sb", [128, 8])
    x_sb = sbf("x_sb", [128, XR, D])
    sq_sb = sbf("sq_sb", [128, D])
    xn_sb = sbf("xn_sb", [128, 2, D])
    xs_sb = sbf("xs_sb", [128, 2, D])
    r_sb = sbf("r_sb", [128, 2, D])
    xq_sb = sbf("xq_sb", [128, 2, D], BF16)
    xqf_sb = sbf("xqf_sb", [128, 2, DC, 128], BF16)
    xn2_sb = sbf("xn2_sb", [128, 2, DC, 128])
    lgs_sb = sbf("lgs_sb", [128, 2, COLS])
    h_sb = sbf("h_sb", [128, 2, 2048])
    hq_sb = sbf("hq_sb", [128, 2, 2048], BF16)
    hqf_sb = sbf("hqf_sb", [128, 2, 8, 128], BF16)
    wst_sb = sbf("wst_sb", [128, 2, 1024])
    wq_sb = sbf("wq_sb", [128, 2, 1024], BF16)
    wqt_sb = sbf("wqt_sb", [128, 2, 1024], BF16)
    wfcqT = sbf("wfcqT", [128, DC, JH], BF16)

    S_MU1, S_RSTD1, S_MU2, S_RSTD2, S_B1, S_B2, S_VE, S_R1P = 0, 1, 2, 3, 4, 5, 6, 7

    ps_attn = psumf("ps_attn", [128, COLS])
    ps_fc = [psumf(f"ps_fc{i}", [128, 512]) for i in range(4)]
    ps_xqt = psumf("ps_xqt", [128, 1024], BF16)
    ps_xst = psumf("ps_xst", [128, 512])
    ps_hqt = psumf("ps_hqt", [128, 1024], BF16)

    with nc.Block() as block:
        spacers = {
            "vector": lambda e: e.tensor_copy(jnk[:, 0:1], c_15[:, :]),
            "scalar": lambda e: e.activation(jnk[:, 1:2], c_15[:, :], AF.Identity),
            "gpsimd": lambda e: e.memset(jnk[:, 2:3], 0.0),
            "tensor": lambda e: e.nop(),
            "sync": lambda e: e.nop(),
        }
        p = Plan(nc, es, spacers)
        sWL = p.sem("sWL")
        sSC = p.sem("sSC")
        sDV = p.sem("sDV")
        sPE = p.sem("sPE")
        sX = [p.sem("sX0"), p.sem("sX1")]
        sWst = [p.sem("sWst0"), p.sem("sWst1")]
        sWqS = [p.sem("sWqS0"), p.sem("sWqS1")]
        sRs = [p.sem("sRs0"), p.sem("sRs1")]
        sLgs = [p.sem("sLgs0"), p.sem("sLgs1")]
        sHqs = [p.sem("sHqs0"), p.sem("sHqs1")]
        sCst = p.sem("sCst")
        sPh = p.sem("sPh")
        sDbg = p.sem("sDbg")
        dbg_prev = [None]

        def dbg_dma(out, in_, waits):
            dbg_prev[0] = p.dma("gpsimd", out, in_, waits=list(waits) + [dbg_prev[0]], pub=sDbg)
            return dbg_prev[0]

        bank = {"attn": None, "fc0": None, "fc1": None, "xqt": None, "xst": None, "hqt": None}

        p.emit("gpsimd", lambda e: e.memset(c_emask[:, :], EMASK))
        p.emit("gpsimd", lambda e: e.memset(c_15[:, :], 1.5))
        p.emit("gpsimd", lambda e: e.iota(iota8[:, :], pattern=[[1, 8]], base=0,
                                          channel_multiplier=0, allow_small_or_imprecise_dtypes=True))
        evConst = p.emit("gpsimd", lambda e: e.iota(iorev8[:, :], pattern=[[-1, 8]], base=8,
                                                    channel_multiplier=0,
                                                    allow_small_or_imprecise_dtypes=True), pub=sCst)

        p.dma("sync", wattn_sb[:, :, :], w_attn.rearrange("(c p) n -> p c n", p=128), pub=sWL)
        p.dma("sync", battn_sb[:, :], b_attn, pub=sWL)
        p.dma("sync", lnw_sb[:, :], lnw_in, pub=sWL)
        p.dma("sync", lnb_sb[:, :], lnb_in, pub=sWL)
        p.dma("sync", idf_sb[:, :], id_f, pub=sWL)
        p.dma("sync", idb_sb[:, :], id_b, pub=sWL)
        p.dma("sync", ones_sb[:, :], ones_r, pub=sWL)
        p.dma("sync", bfc_sb[:, :], bfc_in, pub=sWL)
        evSmall = p.dma("sync", bpj_sb[:, :], bpj_in, pub=sWL)

        def quantize_chain(ch, qi, src, dst, W, waits, pub):
            nb = W // 32
            amax, sclb, invb = qs_amax[qi], qs_sclb[qi], qs_invb[qi]
            ch.emit(lambda e: e.tensor_reduce(
                amax[:, 0:nb].rearrange("p (n o) -> p n o", o=1),
                src.rearrange("p (n k) -> p n k", k=32),
                axis=AX.X, op=ALU.max, apply_absolute_value=True),
                waits=list(waits) + [evConst])
            ch.emit(lambda e: e.tensor_scalar(amax[:, 0:nb], amax[:, 0:nb], 1e-30, None, ALU.max))
            ch.emit(lambda e: e.tensor_scalar(
                sclb[:, 0:nb], amax[:, 0:nb].bitcast(I32), c_emask[:, :], None, ALU.bitwise_and))
            ch.emit(lambda e: e.tensor_scalar(sclb[:, 0:nb], sclb[:, 0:nb], EXP3, None, ALU.subtract))
            ch.emit(lambda e: e.tensor_scalar(invb[:, 0:nb], sclb[:, 0:nb], -1.0, INVC, ALU.mult, ALU.add))
            ch.emit(lambda e: e.tensor_tensor(
                src.rearrange("p (n k) -> p n k", k=32),
                src.rearrange("p (n k) -> p n k", k=32),
                bcast(invb[:, 0:nb].bitcast(F32).rearrange("p (n o) -> p n o", o=1), (128, nb, 32)),
                ALU.mult))
            ch.emit(lambda e: e.tensor_scalar(src.bitcast(I32), src, MAGIC, None, ALU.add))
            return ch.emit(lambda e: e.scalar_tensor_tensor(
                dst.rearrange("p (n k) -> p n k", k=32),
                src.bitcast(I32).rearrange("p (n k) -> p n k", k=32),
                MAGIC,
                bcast(sclb[:, 0:nb].bitcast(F32).rearrange("p (n o) -> p n o", o=1), (128, nb, 32)),
                ALU.subtract, ALU.mult), pub=pub)

        evWtE = [None, None]
        evWqSp = [None, None]
        evWstFree = [None, None]
        evWfcDone = None
        evWpjDone = None

        def wprep_pe_and_out(sl, kind, jt, dcjs, evq):
            evT = None
            for k in range(8):
                ww = [evq, bank["xqt"], evSmall] if k == 0 else ()
                evT = p.emit("tensor", lambda e, k=k, sl=sl: e.transpose(
                    ps_xqt[:, k * 128:(k + 1) * 128], wq_sb[:, sl, k * 128:(k + 1) * 128],
                    idb_sb[:, :]), waits=ww, pub=(sPE if k == 7 else None))
            evE = p.emit("scalar", lambda e, sl=sl: e.activation(
                wqt_sb[:, sl, :], ps_xqt[:, :], AF.Copy), waits=[evT, evWqSp[sl]], pub=sSC)
            bank["xqt"] = evE
            evWtE[sl] = evE
            if kind == "fc":
                return p.emit("vector", lambda e, sl=sl, jt=jt: e.tensor_copy(
                    wfcqT[:, :, jt * 128:(jt + 1) * 128],
                    wqt_sb[:, sl, :].rearrange("p (c k) -> p c k", c=8)),
                    waits=[evE], pub=sDV)
            else:
                dcid, js = dcjs
                off = (dcid * 4 + js) * 8 * 128 * 128
                dst = bass.AP(wpqt_scr, off, [[128, 128], [128 * 128, 8], [1, 128]])
                ev = p.dma("gpsimd", dst,
                           wqt_sb[:, sl, :].rearrange("p (c k) -> p c k", c=8),
                           waits=[evE], pub=sWqS[sl])
                evWqSp[sl] = ev
                return ev

        units = [("fc", jt, None) for jt in range(JC)] + \
                [("pj", None, (dcid, js)) for dcid in range(DC) for js in range(4)]
        for u0 in range(0, len(units), 2):
            pair = units[u0:u0 + 2]
            evqs = []
            chains = []
            for i, (kind, jt, dcjs) in enumerate(pair):
                sl = i
                if kind == "fc":
                    src_ap = wfc_in[(jt) * 128:(jt + 1) * 128, :]
                else:
                    dcid, js = dcjs
                    src_ap = wpj_in[dcid * 128:(dcid + 1) * 128, js * 1024:(js + 1) * 1024]
                evWst = p.dma("sync", wst_sb[:, sl, :], src_ap,
                              waits=[evWstFree[sl]], pub=sWst[sl])
                ch = p.chain(sDV)
                evq = quantize_chain(ch, sl, wst_sb[:, sl, :], wq_sb[:, sl, :], 1024,
                                     [evWst, evWtE[sl]], sDV)
                evqs.append(evq)
                chains.append(ch)
                evWstFree[sl] = evq
            p.seal("vector", *chains)
            for i, (kind, jt, dcjs) in enumerate(pair):
                evDone = wprep_pe_and_out(i, kind, jt, dcjs, evqs[i])
                if kind == "fc":
                    evWfcDone = evDone
                else:
                    evWpjDone = evDone

        evXfree = [None] * XR
        ev_xs_t, ev_s3_t, evA_t, evB_t = {}, {}, {}, {}
        evXq_t, evXqE_t, evXn2E_t, evXn2h0 = {}, {}, {}, {}
        evG_t, evHq_t, evHqSp_t = {}, {}, {}
        evLgE_t, evLgSp_t, evRSp_t = {}, {}, {}
        peT_xq, peT_xs, peMa, peMf = {}, {}, {}, {}
        evDbgXn_t, evDbgH_t, evDbgHq_t = {}, {}, {}
        evXsFree = [None, None]
        evXnFree = [None, None]
        evRFree = [None, None]
        evXqFree = [None, None]

        dbg_xn = dbg_out("xn", [T, D])
        dbg_xs = dbg_out("xs", [T, D])
        dbg_lg = dbg_out("logits", [T, COLS])
        dbg_xq = dbg_out("xq", [T, D], BF16)
        dbg_h = dbg_out("h", [T, JH])
        dbg_hq = dbg_out("hq", [T, JH], BF16)

        def ln_small_chain(ch, rstd_col, extra=()):
            mu_col = rstd_col - 1
            mu = st_sb[:, mu_col:mu_col + 1]
            ch.emit(lambda e: e.tensor_scalar(mu, st_sb[:, S_B1:S_B1 + 1], 1.0 / D, None, ALU.mult),
                    waits=extra)
            ch.emit(lambda e: e.tensor_scalar(stv_sb[:, 4:5], st_sb[:, S_B2:S_B2 + 1], 1.0 / D, None, ALU.mult))
            ch.emit(lambda e: e.tensor_tensor(stv_sb[:, 3:4], mu, mu, ALU.mult))
            ch.emit(lambda e: e.tensor_tensor(stv_sb[:, 4:5], stv_sb[:, 4:5], stv_sb[:, 3:4], ALU.subtract))
            ch.emit(lambda e: e.tensor_scalar(st_sb[:, S_VE:S_VE + 1], stv_sb[:, 4:5], LN_EPS, None, ALU.add))
            v = st_sb[:, S_VE:S_VE + 1]
            r = st_sb[:, rstd_col:rstd_col + 1]
            t0 = stv_sb[:, 0:1]
            t1 = stv_sb[:, 1:2]
            nvh = stv_sb[:, 2:3]
            ch.emit(lambda e: e.tensor_scalar(t0.bitcast(I32), v.bitcast(I32), -0.5, RSQC, ALU.mult, ALU.add))
            ch.emit(lambda e: e.tensor_scalar(nvh, v, -0.5, None, ALU.mult))
            ch.emit(lambda e: e.tensor_copy(r, t0))
            for _ in range(4):
                ch.emit(lambda e: e.tensor_tensor(t1, r, r, ALU.mult))
                ch.emit(lambda e: e.scalar_tensor_tensor(t1, t1, nvh, c_15[:, :], ALU.mult, ALU.add))
                ch.emit(lambda e: e.tensor_tensor(r, r, t1, ALU.mult))
            ch.emit(lambda e: e.tensor_tensor(stv_sb[:, 5:6], mu, r, ALU.mult))
            ev_end = ch.emit(lambda e: e.tensor_scalar(stv_sb[:, 5:6], stv_sb[:, 5:6], -1.0, None, ALU.mult),
                             pub=sDV)
            if rstd_col == S_RSTD1:
                ev_end = ch.emit(lambda e: e.tensor_scalar(
                    st_sb[:, S_R1P:S_R1P + 1], r, 1.0, None, ALU.add), pub=sDV)
            return ev_end

        NT = n_tiles
        for blk in range(NT + 1):
            c = blk if blk < NT else None
            b = blk - 1 if blk >= 1 else None

            if c is not None:
                xsl = c % XR
                csl = c % 2
                evX = p.dma("sync", x_sb[:, xsl, :], x_in[c * TT:(c + 1) * TT, :],
                            waits=[evXfree[xsl]], pub=sX[c % 2])

            if c is not None:
                ev_s1 = p.emit("scalar", lambda e, xsl=xsl: e.activation(
                    sq_sb[:, :], x_sb[:, xsl, :], AF.Identity, accum_out=st_sb[:, S_B1:S_B1 + 1]),
                    waits=[evX, evConst, evB_t.get(c - 1), evSmall], pub=sSC)
                ev_s2 = p.emit("scalar", lambda e, xsl=xsl: e.activation(
                    sq_sb[:, :], x_sb[:, xsl, :], AF.Square, accum_out=st_sb[:, S_B2:S_B2 + 1]),
                    waits=[ev_s1], pub=sSC)

            chC = chA = None
            if b is not None:
                bsl = b % 2
                chC = p.chain(sDV)
                evXq_t[b] = quantize_chain(chC, 0, xn_sb[:, bsl, :], xq_sb[:, bsl, :], D,
                                           [ev_xs_t[b], evXqFree[bsl], evDbgXn_t.get(b)], sDV)
                evXnFree[bsl] = evXq_t[b]
            if c is not None:
                chA = p.chain(sDV)
                evA_t[c] = ln_small_chain(chA, S_RSTD1, extra=[ev_s2])
            p.seal("vector", chC, chA)

            if b is not None:
                evT = None
                for k in range(8):
                    ww = [evXq_t[b], bank["xqt"], evSmall] if k == 0 else ()
                    evT = p.emit("tensor", lambda e, k=k, bsl=bsl: e.transpose(
                        ps_xqt[:, k * 128:(k + 1) * 128], xq_sb[:, bsl, k * 128:(k + 1) * 128],
                        idb_sb[:, :]), waits=ww, pub=(sPE if k == 7 else None))
                peT_xq[b] = evT
                evXqE_t[b] = p.emit("scalar", lambda e, bsl=bsl: e.activation(
                    xqf_sb[:, bsl, :, :].rearrange("p c k -> p (c k)"), ps_xqt[:, :], AF.Copy),
                    waits=[peT_xq[b], peMf.get(b - 2, [None, None])[1]], pub=sSC)
                bank["xqt"] = evXqE_t[b]
                evXqFree[bsl] = evXqE_t[b]
                peT_xs[b] = [None, None]
                for half in range(2):
                    for k in range(4):
                        kk = half * 4 + k
                        ww = ()
                        if k == 0:
                            ww = [ev_xs_t[b], bank["xst"], evSmall] if half == 0 else [evXn2h0[b]]
                        evTs = p.emit("tensor", lambda e, kk=kk, k=k, bsl=bsl: e.transpose(
                            ps_xst[:, k * 128:(k + 1) * 128], xs_sb[:, bsl, kk * 128:(kk + 1) * 128],
                            idf_sb[:, :]), waits=ww, pub=(sPE if k == 3 else None))
                    peT_xs[b][half] = evTs
                    evXn2l = None
                    for k in range(4):
                        kk = half * 4 + k
                        ww = [peT_xs[b][half]] if k == 0 else []
                        if k == 0 and half == 0:
                            ww = [peT_xs[b][half], peMa.get(b - 2)]
                        evXn2l = p.emit("scalar", lambda e, kk=kk, k=k, bsl=bsl: e.activation(
                            xn2_sb[:, bsl, kk, :], ps_xst[:, k * 128:(k + 1) * 128], AF.Identity,
                            bias=lnb_sb[:, kk:kk + 1], scale=lnw_sb[:, kk:kk + 1]),
                            waits=ww, pub=(sSC if k == 3 else None))
                    if half == 0:
                        evXn2h0[b] = evXn2l
                bank["xst"] = evXn2l
                evXn2E_t[b] = evXn2l
                evXsFree[bsl] = evXn2l
                for dc in range(DC):
                    ww = [evXn2E_t[b], bank["attn"], evSmall] if dc == 0 else ()
                    p.emit("tensor", lambda e, dc=dc, bsl=bsl: e.matmul(
                        ps_attn[:, :], xn2_sb[:, bsl, dc, :], wattn_sb[:, dc, :],
                        start=(dc == 0), stop=False), waits=ww)
                peMa[b] = p.emit("tensor", lambda e: e.matmul(
                    ps_attn[:, :], ones_sb[:, :], battn_sb[:, :], start=False, stop=True), pub=sPE)
                evLgE_t[b] = p.emit("scalar", lambda e, bsl=bsl: e.activation(
                    lgs_sb[:, bsl, :], ps_attn[:, :], AF.Copy),
                    waits=[peMa[b], evLgSp_t.get(b - 2)], pub=sSC)
                bank["attn"] = evLgE_t[b]
                peMf[b] = [None, None]
                for half in range(2):
                    for dc in range(DC):
                        for q in range(4):
                            jt = half * 4 + q
                            ww = ()
                            if dc == 0 and q == 0:
                                if half == 0:
                                    ww = [evXqE_t[b], evWfcDone, evG_t.get((b - 1, 1)), evSmall]
                                else:
                                    ww = [evG_t[(b, 0)]]
                            p.emit("tensor", lambda e, dc=dc, q=q, jt=jt, bsl=bsl: e.matmul(
                                ps_fc[q][:, :], xqf_sb[:, bsl, dc, :],
                                wfcqT[:, dc, jt * 512:(jt + 1) * 512],
                                start=(dc == 0), stop=False), waits=ww)
                    evMfl = None
                    for q in range(4):
                        jt = half * 4 + q
                        evMfl = p.emit("tensor", lambda e, q=q, jt=jt: e.matmul(
                            ps_fc[q][:, :], ones_sb[:, :], bfc_sb[:, jt * 512:(jt + 1) * 512],
                            start=False, stop=True), pub=(sPE if q == 3 else None))
                    peMf[b][half] = evMfl
                    evG = None
                    for q in range(4):
                        w0 = []
                        if q == 0:
                            w0 = [peMf[b][half], evHq_t.get((b - 1, half)), evHqSp_t.get((b - 1, half)),
                                  evDbgH_t.get((b - 1, half))]
                        evG = p.emit("scalar", lambda e, q=q, half=half: e.activation(
                            h_sb[:, half, q * 512:(q + 1) * 512], ps_fc[q][:, :], AF.Gelu),
                            waits=w0, pub=(sSC if q == 3 else None))
                    evG_t[(b, half)] = evG
                    if dbg_h is not None:
                        evDbgH_t[(b, half)] = dbg_dma(
                            dbg_h[b * TT:(b + 1) * TT, half * 2048:(half + 1) * 2048],
                            h_sb[:, half, :], [evG])

            if c is not None:
                ev_xn = p.emit("scalar", lambda e, xsl=xsl, csl=csl: e.activation(
                    xn_sb[:, csl, :], x_sb[:, xsl, :], AF.Identity,
                    bias=stv_sb[:, 5:6], scale=st_sb[:, S_RSTD1:S_RSTD1 + 1],
                    accum_out=st_sb[:, S_B1:S_B1 + 1]),
                    waits=[evA_t[c], evXnFree[csl]], pub=sSC)
                ev_r = p.emit("scalar", lambda e, xsl=xsl, csl=csl: e.activation(
                    r_sb[:, csl, :], x_sb[:, xsl, :], AF.Identity,
                    bias=stv_sb[:, 5:6], scale=st_sb[:, S_R1P:S_R1P + 1]),
                    waits=[evRFree[csl]], pub=sSC)
                evXfree[xsl] = ev_r
                ev_s3_t[c] = p.emit("scalar", lambda e, csl=csl: e.activation(
                    sq_sb[:, :], xn_sb[:, csl, :], AF.Square, accum_out=st_sb[:, S_B2:S_B2 + 1]),
                    waits=[ev_xn, ev_s2], pub=sSC)
                ev_xs = None
            chD0 = chD1 = chB = None
            if b is not None:
                chD0 = p.chain(sDV)
                chD1 = p.chain(sDV)
                evHq_t[(b, 0)] = quantize_chain(chD0, 0, h_sb[:, 0, :], hq_sb[:, 0, :], 2048,
                                                [evG_t[(b, 0)], evHqSp_t.get((b - 1, 0)),
                                                 evDbgH_t.get((b, 0)), evDbgHq_t.get((b - 1, 0))], sDV)
                evHq_t[(b, 1)] = quantize_chain(chD1, 1, h_sb[:, 1, :], hq_sb[:, 1, :], 2048,
                                                [evG_t[(b, 1)], evHqSp_t.get((b - 1, 1)),
                                                 evDbgH_t.get((b, 1)), evDbgHq_t.get((b - 1, 1))], sDV)
            if c is not None:
                chB = p.chain(sDV)
                evB_t[c] = ln_small_chain(chB, S_RSTD2, extra=[ev_s3_t[c]])
            p.seal("vector", chD0, chD1, chB)

            if c is not None:
                ev_xs = p.emit("scalar", lambda e, csl=csl: e.activation(
                    xs_sb[:, csl, :], xn_sb[:, csl, :], AF.Identity,
                    bias=stv_sb[:, 5:6], scale=st_sb[:, S_RSTD2:S_RSTD2 + 1]),
                    waits=[evB_t[c], evXsFree[csl], ev_s3_t[c]], pub=sSC)
                ev_xs_t[c] = ev_xs
                evRFree[csl] = evRSp_t[c] = p.dma("gpsimd", r_scr.ap()[c * TT:(c + 1) * TT, :],
                                                  r_sb[:, csl, :], waits=[ev_r], pub=sRs[csl])
                if dbg_xn is not None:
                    evDbgXn_t[c] = dbg_dma(dbg_xn[c * TT:(c + 1) * TT, :], xn_sb[:, csl, :], [ev_xs])
                if dbg_xs is not None:
                    dbg_dma(dbg_xs[c * TT:(c + 1) * TT, :], xs_sb[:, csl, :], [ev_xs])

            if b is not None:
                for half in range(2):
                    evSpPrev = evHqSp_t.get((b - 1, half))
                    evHqEg = None
                    for g in range(2):
                        for k in range(8):
                            kk = g * 8 + k
                            ww = ()
                            if k == 0:
                                ww = [evHq_t[(b, half)], bank["hqt"], evSmall] if g == 0 else [evHqEg]
                            evTh = p.emit("tensor", lambda e, kk=kk, k=k, half=half: e.transpose(
                                ps_hqt[:, k * 128:(k + 1) * 128], hq_sb[:, half, kk * 128:(kk + 1) * 128],
                                idb_sb[:, :]), waits=ww, pub=(sPE if k == 7 else None))
                        evHqEg = p.emit("scalar", lambda e, half=half: e.activation(
                            hqf_sb[:, half, :, :].rearrange("p c k -> p (c k)"), ps_hqt[:, :], AF.Copy),
                            waits=[evTh, evSpPrev], pub=sSC)
                        bank["hqt"] = evHqEg
                        chunk0 = half * 16 + g * 8
                        off = (b * JC + chunk0) * 128 * 128
                        dsth = bass.AP(hq_scr, off, [[128, 128], [128 * 128, 8], [1, 128]])
                        evSpPrev = p.dma("gpsimd", dsth, hqf_sb[:, half, :, :],
                                         waits=[evHqEg], pub=sHqs[half])
                    evHqSp_t[(b, half)] = evSpPrev
                dstlg = bass.AP(lg_scr, b * H * TT * 88, [[88, 128], [TT * 88, 4], [1, 88]])
                evLgSp_t[b] = p.dma("gpsimd", dstlg,
                                    lgs_sb[:, bsl, :].rearrange("p (h w) -> p h w", h=4),
                                    waits=[evLgE_t[b]], pub=sLgs[b % 2])
                if dbg_lg is not None:
                    dbg_dma(dbg_lg[b * TT:(b + 1) * TT, :], lgs_sb[:, bsl, :], [evLgE_t[b]])
                if dbg_xq is not None:
                    dbg_dma(dbg_xq[b * TT:(b + 1) * TT, :], xq_sb[:, bsl, :], [evXq_t[b]])
                if dbg_hq is not None:
                    for half in range(2):
                        evDbgHq_t[(b, half)] = dbg_dma(
                            dbg_hq[b * TT:(b + 1) * TT, half * 2048:(half + 1) * 2048],
                            hq_sb[:, half, :], [evHq_t[(b, half)]])



        TPC = TT
        TL = n_tiles
        evP1 = p.emit("gpsimd", lambda e: e.memset(jnk[:, 3:4], 0.0),
                      waits=[x for x in [peMf.get(NT - 1, [None, None])[1],
                                         evHqSp_t.get((NT - 1, 0)), evHqSp_t.get((NT - 1, 1)),
                                         evRSp_t.get(NT - 1), evRSp_t.get(NT - 2),
                                         evLgSp_t.get(NT - 1), evLgSp_t.get(NT - 2),
                                         evXq_t.get(NT - 1), evXqE_t.get(NT - 1),
                                         evXn2E_t.get(NT - 1), peMa.get(NT - 1),
                                         ev_s3_t.get(NT - 1), evWpjDone, dbg_prev[0]] if x],
                      pub=sCst)
        free_phase1()

        dbg_E = dbg_out("E", [CP, TPC * 64])
        dbg_pidx = dbg_out("pidx", [CP, TPC * 8])
        dbg_psi = dbg_out("psi", [CP, TPC * 8])
        dbg_phi = dbg_out("phi", [CP, TPC * 8])
        dbg_dtil = dbg_out("dtil", [CP, NS * TPC])
        dbg_ctil = dbg_out("ctil", [CP, NS * TPC])
        dbg_ysT = dbg_out("ysT", [NS * H, T], BF16)

        wpqT_sb = sb("wpqT", [128, JC, 1024], BF16)
        idp_sb = sb("idp", [128, 128])
        psg2 = sb("psg2", [CP, TPC, 8])
        smallE = sb("smallE", [4, TL, 8])
        smallP = sb("smallP", [4, TL + 1, 8])
        smallU = sb("smallU", [4, 8])
        smallH = sb("smallH", [4, 8])
        smallT = sb("smallT", [4, 8, 8])
        agf = sb("agf", [1, 8, 32])
        magb = sb("magb", [1, 8, 32])
        g0r = sb("g0r", [1, 32])
        g0col = sb("g0col", [32, 1])
        g0m_sb = sb("g0m_sb", [1, 8])
        trib = sb("trib", [128, 8, 8])
        hqs_sb = sb("hqs", [128, 2, JC, 128], BF16)
        rld_sb = sb("rld", [128, 2, D])
        ffe_sb = sb("ffe", [128, 2, D])
        _t2 = []
        def sbt2(name, shape, dt=F32):
            cm = nc.sbuf_tensor(name, list(shape), dt)
            h2 = cm.__enter__()
            _t2.append(cm)
            return h2
        dtil = sbt2("dtil", [CP, NS, TPC])
        ctil = sbt2("ctil", [CP, NS, TPC])
        _t1 = []
        def sbt1(name, shape, dt=F32):
            cm = nc.sbuf_tensor(name, list(shape), dt)
            h1 = cm.__enter__()
            _t1.append(cm)
            return h1
        THL = TPC // 2
        TQ = 16
        lgA = sbt1("lgA", [CP, THL, 88])
        sgS = sbt1("sgS", [CP, THL, 8])
        sgG = sbt1("sgG", [CP, THL, 8])
        bvec = sbt1("bvec", [CP, THL, 8])
        dvec = sbt1("dvec", [CP, TPC, 8])
        cvec = sbt1("cvec", [CP, TPC, 8])
        m64b = sbt1("m64b", [CP, THL])
        rsb = sbt1("rsb", [CP, THL, 8])
        rrb = sbt1("rrb", [CP, THL, 8])
        srt = sbt1("srt", [CP, TQ, 8, 8])
        gtmp = sbt1("gtmp", [CP, TQ, 8, 8])
        rmaxb = sbt1("rmaxb", [CP, TPC, 8])
        rankb = sbt1("rankb", [CP, TPC, 8])
        avn = sbt1("avn", [CP, TQ, 8])
        mk8 = sbt1("mk8", [CP, TQ, 8])
        mkb = sbt1("mkb", [CP, TQ])
        t2b = sbt1("t2b", [CP, TQ, 8])
        m2b = sbt1("m2b", [CP, TQ])
        cselb = sbt1("cselb", [CP, TQ, 8])
        pidxs = sbt1("pidxs", [CP, TPC, 8])
        pidx = sbt1("pidx", [CP, TPC, 8])
        PDf = sbt1("PDf", [CP, TPC // 2, 8, 8], BF16)
        psig = sbt1("psig", [CP, TPC, 8])
        phib = sbt1("phib", [CP, TPC, 8])
        pst = sbt1("pst", [CP, 8, 8])

        ps_pj = [psum(f"ps_pj{i}", [128, 1024]) for i in range(2)]
        ps_mx = [psum(f"ps_mx{i}", [128, 1024]) for i in range(2)]

        sB2 = p.sem("sB2")
        sCC = p.sem("sCC")
        sHqL = [p.sem("sHqL0"), p.sem("sHqL1")]
        sRL = [p.sem("sRL0"), p.sem("sRL1")]
        sAcS = [p.sem("sAcS0"), p.sem("sAcS1")]
        sAcL = [p.sem("sAcL0"), p.sem("sAcL1")]
        sOu = [p.sem("sOu0"), p.sem("sOu1")]
        sWpL = p.sem("sWpL")
        b2_prev = [None]

        def b2dma(out, in_, waits):
            b2_prev[0] = p.dma("scalar", out, in_, waits=list(waits) + [b2_prev[0], evP1], pub=sB2)
            return b2_prev[0]

        evWpL = None
        for dcid in range(DC):
            srcw = bass.AP(wpqt_scr, dcid * 4 * 8 * 128 * 128,
                           [[128, 128], [8 * 128 * 128, 4], [128 * 128, 8], [1, 128]])
            ww0 = [evP1, evWqSp[0], evWqSp[1]] if dcid == 0 else [evP1, evWpL]
            evWpL = p.dma("sync", wpqT_sb[:, :, dcid * 128:(dcid + 1) * 128]
                          .rearrange("p (a b) k -> p a b k", a=4),
                          srcw, waits=ww0, pub=sWpL)
        evIdL = p.dma("sync", idp_sb[:, :], id_f, waits=[evP1, evWpL], pub=sWpL)
        evG0L = p.dma("sync", g0m_sb[:, :], g0m_in, waits=[evP1, evIdL], pub=sWpL)
        evHqL = [None, None]
        evRL = [None, None]
        evPjE = {}
        evAccSp = {}
        for tl in range(NT):
            sl2 = tl % 2
            evHqL[sl2] = p.dma("sync", hqs_sb[:, sl2, :, :],
                               bass.AP(hq_scr, tl * JC * 128 * 128,
                                       [[128, 128], [128 * 128, JC], [1, 128]]),
                               waits=[evPjE.get(tl - 2), evP1], pub=sHqL[sl2])
            evRL[sl2] = p.dma("sync", rld_sb[:, sl2, :], r_scr.ap()[tl * TT:(tl + 1) * TT, :],
                              waits=[evPjE.get(tl - 2), evP1], pub=sRL[sl2])
            for jc in range(JC):
                for hf in range(2):
                    ww = ()
                    if jc == 0 and hf == 0:
                        ww = [evHqL[sl2], evWpL, evPjE.get(tl - 2), evG0L]
                    p.emit("tensor", lambda e, jc=jc, hf=hf, sl2=sl2, tl=tl: e.matmul(
                        ps_pj[tl % 2][:, hf * 512:(hf + 1) * 512], hqs_sb[:, sl2, jc, :],
                        wpqT_sb[:, jc, hf * 512:(hf + 1) * 512],
                        start=(jc == 0), stop=False), waits=ww)
            for hf in range(2):
                p.emit("tensor", lambda e, hf=hf, sl2=sl2, tl=tl: e.matmul(
                    ps_pj[tl % 2][:, hf * 512:(hf + 1) * 512], idp_sb[:, :],
                    rld_sb[:, sl2, hf * 512:(hf + 1) * 512],
                    start=False, stop=False),
                    waits=[evRL[sl2], evIdL] if hf == 0 else ())
            evMj = None
            for hf in range(2):
                evMj = p.emit("tensor", lambda e, hf=hf, tl=tl: e.matmul(
                    ps_pj[tl % 2][:, hf * 512:(hf + 1) * 512], ones_sb[:, :],
                    bpj_sb[:, hf * 512:(hf + 1) * 512],
                    start=False, stop=True), pub=(sPE if hf == 1 else None))
            evPjE[tl] = p.emit("scalar", lambda e, sl2=sl2, tl=tl: e.activation(
                ffe_sb[:, sl2, :], ps_pj[tl % 2][:, :], AF.Copy),
                waits=[evMj, evAccSp.get(tl - 2)], pub=sSC)
            evAccSp[tl] = p.dma("sync", acc_scr.ap()[tl * TT:(tl + 1) * TT, :],
                                ffe_sb[:, sl2, :], waits=[evPjE[tl]], pub=sAcS[sl2])

        def drain_acc(k):
            pass

        ch = p.chain(sDV)
        ch.emit(lambda e: e.tensor_tensor(
            trib[:, :, :], bcast(iota8[:, :].unsqueeze(2), (128, 8, 8)),
            bcast(iota8[:, :].unsqueeze(1), (128, 8, 8)), ALU.is_lt),
            waits=[evConst, evP1])
        p.seal("vector", ch)
        evTri = ch.last

        evPsiL = None
        evDbgE = None
        evLgL = None
        for th in range(2):
            tof = th * THL
            evLgL = p.dma("scalar", lgA[:, :, :],
                          bass.AP(lg_scr, tof * 88, [[TPC * 88, CP], [88, THL], [1, 88]]),
                          waits=[evP1, evPsiL], pub=sB2)
            evSgS = p.emit("scalar", lambda e: e.activation(sgS[:, :, :], lgA[:, :, 64:72], AF.Sigmoid),
                           waits=[evLgL], pub=sSC)
            evSgG = p.emit("scalar", lambda e: e.activation(sgG[:, :, :], lgA[:, :, 72:80], AF.Sigmoid),
                           waits=[evSgS], pub=sSC)
            evBv = p.emit("scalar", lambda e: e.activation(bvec[:, :, :], lgA[:, :, 80:88], AF.Copy),
                          waits=[evSgG], pub=sSC)
            ch = p.chain(sDV)
            ch.emit(lambda e: e.tensor_scalar(sgS[:, :, :], sgS[:, :, :], 2.0, -1.0, ALU.mult, ALU.add),
                    waits=[evSgS])
            ch.emit(lambda e, tof=tof: e.tensor_tensor(dvec[:, tof:tof + THL, :], sgG[:, :, :],
                                                       sgS[:, :, :], ALU.mult), waits=[evSgG])
            ch.emit(lambda e: e.tensor_scalar(sgG[:, :, :], sgG[:, :, :], -1.0, 1.0, ALU.mult, ALU.add))
            ch.emit(lambda e, tof=tof: e.tensor_tensor(cvec[:, tof:tof + THL, :], sgG[:, :, :],
                                                       bvec[:, :, :], ALU.mult), waits=[evBv])
            ch.emit(lambda e: e.tensor_reduce(
                m64b[:, :].rearrange("p (t o) -> p t o", o=1),
                lgA[:, :, 0:64], axis=AX.X, op=ALU.max))
            evEsub = ch.emit(lambda e: e.tensor_tensor(
                lgA[:, :, 0:64], lgA[:, :, 0:64],
                bcast(m64b[:, :].rearrange("p (t o) -> p t o", o=1), (CP, THL, 64)), ALU.subtract))
            p.seal("vector", ch)
            drain_acc(2)
            evExp = p.emit("scalar", lambda e: e.activation(
                lgA[:, :, 0:64], lgA[:, :, 0:64], AF.Exp), waits=[evEsub], pub=sSC)
            E4 = lgA[:, :, 0:64].rearrange("p t (i j) -> p t i j", j=8)
            ch = p.chain(sDV)
            prevw = evExp
            for it in range(5):
                ch.emit(lambda e, E4=E4: e.tensor_reduce(
                    rsb[:, :, :].unsqueeze(3), E4,
                    axis=AX.X, op=ALU.add), waits=[prevw])
                prevw = None
                ch.emit(lambda e: e.reciprocal(rrb[:, :, :], rsb[:, :, :]))
                ch.emit(lambda e, E4=E4: e.tensor_tensor(
                    E4, E4, bcast(rrb[:, :, :].unsqueeze(3), (CP, THL, 8, 8)),
                    ALU.mult))
                ch.emit(lambda e, E4=E4: e.tensor_reduce(
                    rsb[:, :, :].unsqueeze(3),
                    E4.rearrange("p t i j -> p t j i"), axis=AX.X, op=ALU.add))
                ch.emit(lambda e: e.reciprocal(rrb[:, :, :], rsb[:, :, :]))
                ch.emit(lambda e, E4=E4: e.tensor_tensor(
                    E4, E4, bcast(rrb[:, :, :].unsqueeze(2), (CP, THL, 8, 8)),
                    ALU.mult))
                if it == 2:
                    p.seal("vector", ch)
                    drain_acc(2)
                    nch = p.chain(sDV)
                    nch.last = ch.last
                    ch = nch
            evSk = ch.last
            p.seal("vector", ch)
            drain_acc(2)
            if dbg_E is not None:
                evDbgE = b2dma(dbg_E[:, tof * 64:(tof + THL) * 64], lgA[:, :, 0:64], [evSk])
            evGr = evSk
            for tq in range(THL // TQ):
                t0 = tof + tq * TQ
                tl0 = tq * TQ
                Eh = E4[:, tl0:tl0 + TQ, :, :]
                ch = p.chain(sDV)
                ch.emit(lambda e, Eh=Eh, t0=t0: e.tensor_reduce(
                    rmaxb[:, t0:t0 + TQ, :].unsqueeze(3), Eh,
                    axis=AX.X, op=ALU.max), waits=[evGr])
                ch.emit(lambda e, t0=t0: e.tensor_tensor(
                    gtmp[:, :, :, :],
                    bcast(rmaxb[:, t0:t0 + TQ, :].unsqueeze(3), (CP, TQ, 8, 8)),
                    bcast(rmaxb[:, t0:t0 + TQ, :].unsqueeze(2), (CP, TQ, 8, 8)),
                    ALU.is_gt))
                ch.emit(lambda e, t0=t0: e.tensor_tensor(
                    srt[:, :, :, :],
                    bcast(rmaxb[:, t0:t0 + TQ, :].unsqueeze(3), (CP, TQ, 8, 8)),
                    bcast(rmaxb[:, t0:t0 + TQ, :].unsqueeze(2), (CP, TQ, 8, 8)),
                    ALU.is_equal))
                ch.emit(lambda e: e.tensor_tensor(
                    srt[:, :, :, :], srt[:, :, :, :],
                    bcast(trib[0:CP, :, :].unsqueeze(1), (CP, TQ, 8, 8)), ALU.mult),
                    waits=[evTri])
                ch.emit(lambda e: e.tensor_tensor(gtmp[:, :, :, :], gtmp[:, :, :, :],
                                                  srt[:, :, :, :], ALU.add))
                ch.emit(lambda e, t0=t0: e.tensor_reduce(
                    rankb[:, t0:t0 + TQ, :].unsqueeze(3),
                    gtmp[:, :, :, :].rearrange("p t a b -> p t b a"), axis=AX.X, op=ALU.add))
                for i in range(8):
                    ch.emit(lambda e, i=i, t0=t0: e.tensor_tensor(
                        mk8[:, :, :],
                        bcast(rankb[:, t0:t0 + TQ, i].rearrange("p (t o) -> p t o", o=1),
                              (CP, TQ, 8)),
                        bcast(iota8[0:CP, :].unsqueeze(1), (CP, TQ, 8)), ALU.is_equal))
                    if i == 0:
                        ch.emit(lambda e, i=i, Eh=Eh: e.tensor_tensor(
                            srt[:, :, :, :],
                            bcast(mk8[:, :, :].unsqueeze(3), (CP, TQ, 8, 8)),
                            bcast(Eh[:, :, i, :].unsqueeze(2), (CP, TQ, 8, 8)),
                            ALU.mult))
                    else:
                        ch.emit(lambda e, i=i, Eh=Eh: e.tensor_tensor(
                            gtmp[:, :, :, :],
                            bcast(mk8[:, :, :].unsqueeze(3), (CP, TQ, 8, 8)),
                            bcast(Eh[:, :, i, :].unsqueeze(2), (CP, TQ, 8, 8)),
                            ALU.mult))
                        ch.emit(lambda e: e.tensor_tensor(srt[:, :, :, :], srt[:, :, :, :],
                                                          gtmp[:, :, :, :], ALU.add))
                ch.emit(lambda e: e.memset(avn[:, :, :], 0.0))
                for s in range(8):
                    ch.emit(lambda e, s=s: e.tensor_tensor(mk8[:, :, :], srt[:, :, s, :],
                                                           avn[:, :, :], ALU.add))
                    ch.emit(lambda e: e.tensor_reduce(
                        mkb[:, :].rearrange("p (t o) -> p t o", o=1), mk8[:, :, :],
                        axis=AX.X, op=ALU.max))
                    ch.emit(lambda e: e.tensor_tensor(
                        t2b[:, :, :], mk8[:, :, :],
                        bcast(mkb[:, :].rearrange("p (t o) -> p t o", o=1), (CP, TQ, 8)),
                        ALU.is_equal))
                    ch.emit(lambda e: e.tensor_tensor(
                        t2b[:, :, :], t2b[:, :, :],
                        bcast(iorev8[0:CP, :].unsqueeze(1), (CP, TQ, 8)), ALU.mult))
                    ch.emit(lambda e: e.tensor_reduce(
                        m2b[:, :].rearrange("p (t o) -> p t o", o=1), t2b[:, :, :],
                        axis=AX.X, op=ALU.max))
                    ch.emit(lambda e: e.tensor_tensor(
                        cselb[:, :, :], t2b[:, :, :],
                        bcast(m2b[:, :].rearrange("p (t o) -> p t o", o=1), (CP, TQ, 8)),
                        ALU.is_equal))
                    ch.emit(lambda e: e.scalar_tensor_tensor(
                        avn[:, :, :], cselb[:, :, :], -BIG, avn[:, :, :], ALU.mult, ALU.add))
                    ch.emit(lambda e: e.tensor_tensor(
                        t2b[:, :, :], cselb[:, :, :],
                        bcast(iota8[0:CP, :].unsqueeze(1), (CP, TQ, 8)), ALU.mult))
                    ch.emit(lambda e, s=s, t0=t0: e.tensor_reduce(
                        pidxs[:, t0:t0 + TQ, s].rearrange("p (t o) -> p t o", o=1), t2b[:, :, :],
                        axis=AX.X, op=ALU.add))
                for rk in range(8):
                    ch.emit(lambda e, rk=rk, t0=t0: e.tensor_scalar(
                        mk8[:, :, :], rankb[:, t0:t0 + TQ, :], float(rk), None, ALU.is_equal))
                    if rk == 0:
                        ch.emit(lambda e, rk=rk, t0=t0: e.tensor_tensor(
                            pidx[:, t0:t0 + TQ, :], mk8[:, :, :],
                            bcast(pidxs[:, t0:t0 + TQ, rk].rearrange("p (t o) -> p t o", o=1),
                                  (CP, TQ, 8)), ALU.mult))
                    else:
                        ch.emit(lambda e, rk=rk, t0=t0: e.tensor_tensor(
                            t2b[:, :, :], mk8[:, :, :],
                            bcast(pidxs[:, t0:t0 + TQ, rk].rearrange("p (t o) -> p t o", o=1),
                                  (CP, TQ, 8)), ALU.mult))
                        ch.emit(lambda e, t0=t0: e.tensor_tensor(
                            pidx[:, t0:t0 + TQ, :], pidx[:, t0:t0 + TQ, :], t2b[:, :, :],
                            ALU.add))
                p.seal("vector", ch)
                evGr = ch.last
                drain_acc(2)
            ch = p.chain(sDV)
            ch.emit(lambda e, tof=tof: e.tensor_tensor(
                PDf[:, :, :, :],
                bcast(pidx[:, tof:tof + THL, :].unsqueeze(3), (CP, THL, 8, 8)),
                bcast(iota8[0:CP, :].unsqueeze(1).unsqueeze(1), (CP, THL, 8, 8)), ALU.is_equal),
                waits=[evGr, evDbgE])
            for t in range(tof, tof + THL):
                if t == 0:
                    ch.emit(lambda e: e.tensor_tensor(
                        pst[:, :, :], PDf[:, 0, :, :],
                        bcast(iota8[0:CP, :].unsqueeze(1), (CP, 8, 8)), ALU.mult))
                else:
                    ch.emit(lambda e, t=t, tof=tof: e.tensor_tensor(
                        pst[:, :, :], PDf[:, t - tof, :, :],
                        bcast(psig[:, t - 1, :].unsqueeze(1), (CP, 8, 8)), ALU.mult))
                ch.emit(lambda e, t=t: e.tensor_reduce(
                    psig[:, t, :].rearrange("p (i o) -> p i o", o=1), pst[:, :, :],
                    axis=AX.X, op=ALU.add))
            p.seal("vector", ch)
            evPsiL = ch.last
            drain_acc(2)
            if dbg_pidx is not None:
                b2dma(dbg_pidx[:, tof * 8:(tof + THL) * 8],
                      pidx[:, tof:tof + THL, :].rearrange("p t i -> p (t i)"), [evPsiL])

        evB1 = b2dma(bass.AP(pre1_scr, 0, [[8, CP], [1, 8]]), psig[:, TPC - 1, :], [evPsiL])
        evB2 = b2dma(smallE[:, :, :], bass.AP(pre1_scr, 0, [[8, 4], [32, TL], [1, 8]]), [evB1])
        ch = p.chain(sDV)
        ch.emit(lambda e: e.tensor_copy(smallP[:, 0, :], iota8[0:4, :]), waits=[evB2])
        for cc in range(1, TL + 1):
            for j in range(8):
                ch.emit(lambda e, cc=cc, j=j: e.tensor_scalar(
                    smallT[:, j, :], smallE[:, cc - 1, :], float(j),
                    smallP[:, cc - 1, j:j + 1], ALU.is_equal, ALU.mult),
                    independent=(j != 0))
            ch.emit(lambda e, cc=cc: e.tensor_reduce(
                smallP[:, cc, :].rearrange("p (i o) -> p i o", o=1),
                smallT[:, :, :].rearrange("p j i -> p i j"), axis=AX.X, op=ALU.add))
        p.seal("vector", ch)
        evPre = ch.last
        evB3 = b2dma(bass.AP(pre2_scr, 0, [[8, 4], [32, TL], [1, 8]]), smallP[:, 0:TL, :], [evPre])
        evB4 = b2dma(rrb[:, 0, :], bass.AP(pre2_scr, 0, [[8, CP], [1, 8]]), [evB3])
        ch = p.chain(sDV)
        for j in range(8):
            if j == 0:
                ch.emit(lambda e, j=j: e.tensor_scalar(
                    psg2[:, :, :], psig[:, :, :], float(j),
                    rrb[:, 0, j:j + 1], ALU.is_equal, ALU.mult), waits=[evB4, evPsiL])
            else:
                ch.emit(lambda e, j=j: e.tensor_scalar(
                    rmaxb[:, :, :], psig[:, :, :], float(j),
                    rrb[:, 0, j:j + 1], ALU.is_equal, ALU.mult))
                ch.emit(lambda e: e.tensor_tensor(psg2[:, :, :], psg2[:, :, :],
                                                  rmaxb[:, :, :], ALU.add))
        ch.emit(lambda e: e.memset(phib[:, :, :], 0.0))
        for i in range(1, 8):
            ch.emit(lambda e, i=i: e.tensor_tensor(
                rmaxb[:, :, :],
                bcast(psg2[:, :, i].rearrange("p (t o) -> p t o", o=1), (CP, TPC, 8)),
                bcast(iota8[0:CP, :].unsqueeze(1), (CP, TPC, 8)), ALU.is_equal))
            ch.emit(lambda e, i=i: e.scalar_tensor_tensor(
                phib[:, :, :], rmaxb[:, :, :], float(i), phib[:, :, :], ALU.mult, ALU.add))
        for (dst, vsrc) in ((dtil, dvec), (ctil, cvec)):
            dview = dst[:, :, :].rearrange("p i t -> p t i")
            for j in range(8):
                if j == 0:
                    ch.emit(lambda e, j=j: e.tensor_scalar(
                        rmaxb[:, :, :], phib[:, :, :], float(j), None, ALU.is_equal))
                    ch.emit(lambda e, j=j, dview=dview, vsrc=vsrc: e.tensor_tensor(
                        dview, rmaxb[:, :, :],
                        bcast(vsrc[:, :, j].rearrange("p (t o) -> p t o", o=1), (CP, TPC, 8)),
                        ALU.mult))
                else:
                    ch.emit(lambda e, j=j: e.tensor_scalar(
                        rmaxb[:, :, :], phib[:, :, :], float(j), None, ALU.is_equal))
                    ch.emit(lambda e, j=j, vsrc=vsrc: e.tensor_tensor(
                        rmaxb[:, :, :], rmaxb[:, :, :],
                        bcast(vsrc[:, :, j].rearrange("p (t o) -> p t o", o=1), (CP, TPC, 8)),
                        ALU.mult))
                    ch.emit(lambda e, dview=dview: e.tensor_tensor(
                        dview, dview, rmaxb[:, :, :], ALU.add))
        p.seal("vector", ch)
        evGath = ch.last
        drain_acc(4)
        if dbg_phi is not None:
            b2dma(dbg_phi[:, :], phib[:, :, :].rearrange("p t i -> p (t i)"), [evGath])
        if dbg_psi is not None:
            b2dma(dbg_psi[:, :], psg2[:, :, :].rearrange("p t i -> p (t i)"), [evGath])
        if dbg_dtil is not None:
            b2dma(dbg_dtil[:, :], dtil[:, :, :].rearrange("p i t -> p (i t)"), [evGath])
        if dbg_ctil is not None:
            b2dma(dbg_ctil[:, :], ctil[:, :, :].rearrange("p i t -> p (i t)"), [evGath])

        evB5 = b2dma(bass.AP(dc_scr, 0, [[NS * TPC, CP], [TPC, NS], [1, TPC]]),
                     dtil[:, :, :], [evGath])
        evB6 = b2dma(bass.AP(dc_scr, CP * NS * TPC, [[NS * TPC, CP], [TPC, NS], [1, TPC]]),
                     ctil[:, :, :], [evB5])
        for cm2 in reversed(_t1):
            cm2.__exit__(None, None, None)
        for cm2 in reversed(_t2):
            cm2.__exit__(None, None, None)
        packD = sb("packD", [32, T])
        packC = sb("packC", [32, T])
        packU = sb("packU", [32, T])
        packZ = sb("packZ", [32, T])
        ysT_bf = sb("ysTbf", [NS * H, T], BF16)
        ysb = sb("ysb", [CP, NS, TPC])
        ysb_bf = sb("ysbbf", [CP, NS, TPC], BF16)
        gch = sb("gch", [CP, NS, TPC])
        scrT = sb("scrT", [CP, TPC, 8])
        outwL = sb("outwL", [NS * H, D], BF16)
        outwF = sb("outwF", [NS * H, D])
        outbL = sb("outbL", [1, D])
        evB7 = b2dma(packD[:, :].rearrange("p (c t) -> p c t", t=TPC),
                     bass.AP(dc_scr, 0, [[TPC, 32], [NS * TPC * 4, TL], [1, TPC]]), [evB6])
        evB8 = b2dma(packC[:, :].rearrange("p (c t) -> p c t", t=TPC),
                     bass.AP(dc_scr, CP * NS * TPC, [[TPC, 32], [NS * TPC * 4, TL], [1, TPC]]),
                     [evB7])
        evZer = p.emit("gpsimd", lambda e: e.memset(packZ[:, :], 0.0),
                       waits=[evB6], pub=sCst)
        ch = p.chain(sDV)
        ch.emit(lambda e: e.tensor_tensor_scan(
            packU[:, :], packD[:, :], packC[:, :], 0.0, ALU.mult, ALU.add),
            waits=[evB8, evZer])
        ch.emit(lambda e: e.tensor_tensor_scan(
            packD[:, :], packD[:, :], packZ[:, :], 1.0, ALU.mult, ALU.add))
        p.seal("vector", ch)
        evScan = ch.last
        evB9 = b2dma(bass.AP(ue_scr, 0, [[1, 32], [1, 1]]), packU[:, T - 1:T], [evScan])
        evB10 = b2dma(smallU[:, :], bass.AP(ue_scr, 0, [[8, 4], [1, 8]]), [evB9])
        ch = p.chain(sDV)
        for j in range(8):
            ch.emit(lambda e, j=j: e.tensor_scalar(
                smallT[:, j, :], smallP[:, TL, :], float(j),
                smallU[:, j:j + 1], ALU.is_equal, ALU.mult),
                waits=[evB10] if j == 0 else (), independent=(j != 0))
        ch.emit(lambda e: e.tensor_reduce(
            smallH[:, :].rearrange("p (i o) -> p i o", o=1),
            smallT[:, :, :].rearrange("p j i -> p i j"), axis=AX.X, op=ALU.add))
        p.seal("vector", ch)
        evHmd = ch.last
        evB11 = b2dma(bass.AP(hm_scr, 0, [[8, 4], [1, 8]]), smallH[:, :], [evHmd])
        evCC = p.emit("gpsimd", lambda e: e.collective_compute(
            "AllGather", ALU.bypass, replica_groups=[list(range(N_CORES))],
            ins=[hm_scr.ap()], outs=[ag_scr.ap()]), waits=[evB11], pub=sCC)
        evB12 = b2dma(agf[:, :, :], bass.AP(ag_scr, 0, [[256, 1], [32, 8], [1, 32]]), [evCC])
        ch = p.chain(sDV)
        ch.emit(lambda e: e.tensor_tensor(
            magb[:, :, :], agf[:, :, :],
            bcast(g0m_sb[:, :].unsqueeze(2), (1, 8, 32)), ALU.mult),
            waits=[evB12, evG0L])
        ch.emit(lambda e: e.tensor_reduce(
            g0r[:, :].rearrange("p (k o) -> p k o", o=1),
            magb[:, :, :].rearrange("p r k -> p k r"), axis=AX.X, op=ALU.add))
        p.seal("vector", ch)
        evG0 = ch.last
        evB13 = b2dma(bass.AP(g0_scr, 0, [[32, 1], [1, 32]]), g0r[:, :], [evG0])
        evB14 = b2dma(g0col[:, :], bass.AP(g0_scr, 0, [[1, 32], [1, 1]]), [evB13])
        ch = p.chain(sDV)
        ch.emit(lambda e: e.tensor_scalar(
            packC[:, :], packD[:, :], g0col[:, :], None, ALU.mult),
            waits=[evB14, evScan])
        ch.emit(lambda e: e.tensor_tensor(
            packC[:, :], packC[:, :], packU[:, :], ALU.add))
        p.seal("vector", ch)
        evY = ch.last
        evB15 = b2dma(bass.AP(g_scr, 0, [[TPC, 32], [NS * TPC * 4, TL], [1, TPC]]),
                      packC[:, :].rearrange("p (c t) -> p c t", t=TPC), [evY])
        evB16 = b2dma(gch[:, :, :], bass.AP(g_scr, 0, [[NS * TPC, CP], [TPC, NS], [1, TPC]]),
                      [evB15])
        ch = p.chain(sDV)
        yview = ysb[:, :, :].rearrange("p i t -> p t i")
        for j in range(8):
            ch.emit(lambda e, j=j: e.tensor_scalar(
                scrT[:, :, :], psg2[:, :, :], float(j), None, ALU.is_equal),
                waits=[evB16] if j == 0 else ())
            if j == 0:
                ch.emit(lambda e, j=j: e.tensor_tensor(
                    yview, scrT[:, :, :],
                    bcast(gch[:, j, :].rearrange("p (t o) -> p t o", o=1), (CP, TPC, 8)),
                    ALU.mult))
            else:
                ch.emit(lambda e, j=j: e.tensor_tensor(
                    scrT[:, :, :], scrT[:, :, :],
                    bcast(gch[:, j, :].rearrange("p (t o) -> p t o", o=1), (CP, TPC, 8)),
                    ALU.mult))
                ch.emit(lambda e: e.tensor_tensor(yview, yview, scrT[:, :, :], ALU.add))
        ch.emit(lambda e: e.tensor_copy(ysb_bf[:, :, :], ysb[:, :, :]))
        p.seal("vector", ch)
        evYs = ch.last
        drain_acc(32)
        evB17 = b2dma(bass.AP(ys_scr, 0, [[NS * TPC, CP], [TPC, NS], [1, TPC]]),
                      ysb_bf[:, :, :], [evYs])
        evYsT = b2dma(ysT_bf[:, :].rearrange("p (c t) -> p c t", t=TPC),
                      bass.AP(ys_scr, 0, [[TPC, 32], [NS * TPC * 4, TL], [1, TPC]]), [evB17])
        if dbg_ysT is not None:
            b2dma(dbg_ysT[:, :], ysT_bf[:, :], [evYsT])

        evOwF = p.dma("sync", outwF[:, :], outw_in,
                      waits=[evP1, evG0L, evGath, evB6], pub=sWpL)
        evObL = p.dma("sync", outbL[:, :], outb_in, waits=[evP1, evOwF], pub=sWpL)
        ch = p.chain(sDV)
        ch.emit(lambda e: e.tensor_copy(outwL[:, :], outwF[:, :]), waits=[evOwF])
        p.seal("vector", ch)
        evOwc2 = ch.last

        evMxE = {}
        evOut = {}
        evAcL = [None, None]
        for tl in range(NT):
            sl3 = tl % 2
            rfree = evPjE[NT - 2 + sl3] if tl < 2 else evMxE[tl - 2]
            evAcL[sl3] = p.dma("sync", rld_sb[:, sl3, :], acc_scr.ap()[tl * TT:(tl + 1) * TT, :],
                               waits=[evAccSp[tl], rfree], pub=sAcL[sl3])
            for hf in range(2):
                ww = [evYsT, evMxE.get(tl - 2), evOwc2, evObL] if hf == 0 else ()
                p.emit("tensor", lambda e, hf=hf, tl=tl: e.matmul(
                    ps_mx[tl % 2][:, hf * 512:(hf + 1) * 512],
                    ysT_bf[:, tl * TT:(tl + 1) * TT], outwL[:, hf * 512:(hf + 1) * 512],
                    start=True, stop=False), waits=ww)
            for hf in range(2):
                p.emit("tensor", lambda e, hf=hf, sl3=sl3, tl=tl: e.matmul(
                    ps_mx[tl % 2][:, hf * 512:(hf + 1) * 512], idp_sb[:, :],
                    rld_sb[:, sl3, hf * 512:(hf + 1) * 512],
                    start=False, stop=False),
                    waits=[evAcL[sl3]] if hf == 0 else ())
            evMm = None
            for hf in range(2):
                evMm = p.emit("tensor", lambda e, hf=hf, tl=tl: e.matmul(
                    ps_mx[tl % 2][:, hf * 512:(hf + 1) * 512], ones_sb[:, :],
                    outbL[:, hf * 512:(hf + 1) * 512],
                    start=False, stop=True), pub=(sPE if hf == 1 else None))
            ffree = evAccSp[NT - 2 + sl3] if tl < 2 else evOut[tl - 2]
            evMxE[tl] = p.emit("scalar", lambda e, sl3=sl3, tl=tl: e.activation(
                ffe_sb[:, sl3, :], ps_mx[tl % 2][:, :], AF.Copy),
                waits=[evMm, ffree], pub=sSC)
            evOut[tl] = p.dma("sync", out_ap[tl * TT:(tl + 1) * TT, :], ffe_sb[:, sl3, :],
                              waits=[evMxE[tl]], pub=sOu[sl3])

        fin = []
        for s in (sWL, sSC, sDV, sPE, sCst, sDbg, sB2, sCC, sWpL,
                  sX[0], sX[1], sWst[0], sWst[1], sWqS[0], sWqS[1],
                  sRs[0], sRs[1], sLgs[0], sLgs[1], sHqs[0], sHqs[1],
                  sHqL[0], sHqL[1], sRL[0], sRL[1], sAcS[0], sAcS[1],
                  sAcL[0], sAcL[1], sOu[0], sOu[1]):
            e2 = Ev(s)
            e2.val = s.total
            fin.append(e2)
        p.emit("sync", lambda e: e.nop(), waits=fin)
        p.finalize(block)
    es.close()
    return nc, dbg


def shard_inputs(inputs, core_id, n_tiles=32):
    T = n_tiles * TT
    b, half = core_id // 2, core_id % 2
    x = np.ascontiguousarray(inputs["L0_attn_x"][b, half * 4096: half * 4096 + T]).astype(np.float32, copy=False)
    pw, sw = inputs["L0_attn_perm_W"], inputs["L0_attn_sign_W"]
    gw, iw = inputs["L0_attn_gate_W"], inputs["L0_attn_in_W"]
    pb, sbv = inputs["L0_attn_perm_b"], inputs["L0_attn_sign_b"]
    gb, ib = inputs["L0_attn_gate_b"], inputs["L0_attn_in_b"]
    w_attn = np.concatenate(
        [np.concatenate([pw[:, h * 64:(h + 1) * 64], sw[:, h * 8:(h + 1) * 8],
                         gw[:, h * 8:(h + 1) * 8], iw[:, h * 8:(h + 1) * 8]], axis=1)
         for h in range(H)], axis=1)
    b_attn = np.concatenate(
        [np.concatenate([pb[h * 64:(h + 1) * 64], sbv[h * 8:(h + 1) * 8],
                         gb[h * 8:(h + 1) * 8], ib[h * 8:(h + 1) * 8]])
         for h in range(H)])[None, :]
    g0mask = np.zeros((1, 8), np.float32)
    if core_id % 2 == 1:
        g0mask[0, core_id - 1] = 1.0
    im = {
        "x_sh": x,
        "w_attn": np.ascontiguousarray(w_attn, dtype=np.float32),
        "b_attn": np.ascontiguousarray(b_attn, dtype=np.float32),
        "lnw_rs": np.ascontiguousarray(np.asarray(inputs["L0_attn_ln_w"]).reshape(DC, 128).T, dtype=np.float32),
        "lnb_rs": np.ascontiguousarray(np.asarray(inputs["L0_attn_ln_b"]).reshape(DC, 128).T, dtype=np.float32),
        "w_fc": np.ascontiguousarray(inputs["L0_ffn_w_fc"], dtype=np.float32),
        "b_fc": np.ascontiguousarray(np.asarray(inputs["L0_ffn_b_fc"])[None, :], dtype=np.float32),
        "w_proj": np.ascontiguousarray(inputs["L0_ffn_w_proj"], dtype=np.float32),
        "b_proj": np.ascontiguousarray(np.asarray(inputs["L0_ffn_b_proj"])[None, :], dtype=np.float32),
        "out_w": np.ascontiguousarray(inputs["L0_attn_out_W"], dtype=np.float32),
        "out_b": np.ascontiguousarray(np.asarray(inputs["L0_attn_out_b"])[None, :], dtype=np.float32),
        "id_f": np.eye(128, dtype=np.float32),
        "id_b": np.eye(128).astype(ml_dtypes.bfloat16),
        "ones_r": np.ones((1, 128), np.float32),
        "g0mask": g0mask,
    }
    return im


def kernel(**inputs):
    n_tiles = 32
    nc, _ = build(n_tiles=n_tiles)
    in_maps = [shard_inputs(inputs, c, n_tiles) for c in range(N_CORES)]
    res = run_bass_kernel_spmd(nc, in_maps, core_ids=list(range(N_CORES)))
    out = np.zeros((4, 8192, 1024), dtype=np.float32)
    for c in range(N_CORES):
        b, half = c // 2, c % 2
        out[b, half * 4096:(half + 1) * 4096] = res.results[c]["out_sh"]
    return out


# version 04
# speedup vs baseline: 1.2664x; 1.1758x over previous
import numpy as np
import ml_dtypes
import concourse.bass as bass
import concourse.mybir as mybir
from concourse.bass_utils import run_bass_kernel_spmd

F32 = mybir.dt.float32
BF16 = mybir.dt.bfloat16
I32 = mybir.dt.int32
AF = mybir.ActivationFunctionType
ALU = mybir.AluOpType
AX = mybir.AxisListType

N_CORES = 8
D = 1024
DC = 8
JH = 4096
JC = 32
H = 4
NS = 8
COLS = 352
TT = 128
BIG = 1.0e30
MAGIC = 12582912.0
EMASK = 0x7F800000
EXP3 = float(3 << 23)
INVC = float(254 << 23)
RSQC = float(0x5F3759DF)
LN_EPS = 1e-5

ENGINES = ("sync", "scalar", "vector", "gpsimd", "tensor")


class Ev:
    __slots__ = ("sem", "val")

    def __init__(self, sem):
        self.sem = sem
        self.val = None


class SemC:
    __slots__ = ("h", "total")

    def __init__(self, h):
        self.h = h
        self.total = 0


class Chain:

    def __init__(self, plan, sem):
        self.plan = plan
        self.sem = sem
        self.items = []
        self.last = None

    def emit(self, fn, waits=(), pub=None, inc=1, independent=False):
        pub = pub if pub is not None else self.sem
        ev = Ev(pub)
        prev = None if independent else self.last
        w = tuple(w for w in list(waits) + [prev] if w is not None)
        self.items.append((fn, w, pub, inc, ev))
        self.last = ev
        return ev


class Plan:
    def __init__(self, nc, es, spacers):
        self.nc = nc
        self.es = es
        self.streams = {e: [] for e in ENGINES}
        self._semidx = 0
        self.spacers = spacers

    def sem(self, name=None):
        h = self.es.enter_context(self.nc.semaphore(name or f"s{self._semidx}"))
        self._semidx += 1
        return SemC(h)

    def chain(self, sem):
        return Chain(self, sem)

    def _append(self, engine, fn, waits, pub, inc, ev):
        if pub is not None:
            pub.total += inc
            if ev is not None:
                ev.val = pub.total
        self.streams[engine].append((fn, waits, pub, inc, ev))

    def emit(self, engine, fn, waits=(), pub=None, inc=1):
        ev = Ev(pub) if pub is not None else None
        self._append(engine, fn, tuple(w for w in waits if w is not None), pub, inc, ev)
        return ev

    def dma(self, queue, out, in_, waits=(), pub=None):
        return self.emit(queue, lambda e, o=out, i=in_: e.dma_start(out=o, in_=i),
                         waits=waits, pub=pub, inc=16)

    def seal(self, engine, *chains):
        chains = [list(c.items) for c in chains if c is not None and c.items]
        if not chains:
            return
        idx = [0] * len(chains)
        while True:
            live = [k for k in range(len(chains)) if idx[k] < len(chains[k])]
            if not live:
                break
            for k in live:
                fn, waits, pub, inc, ev = chains[k][idx[k]]
                self._append(engine, fn, waits, pub, inc, ev)
                idx[k] += 1

    def finalize(self, block, hw_fast=True):
        pub_engines = {}
        dma_sems = set()
        for engx in ENGINES:
            for fn, waits, pub, inc, ev in self.streams[engx]:
                if pub is not None:
                    pub_engines.setdefault(id(pub), set()).add(engx)
                    if inc != 1:
                        dma_sems.add(id(pub))

        def section(engname):
            def _(eng):
                waited = {}
                published = {}
                prev_pub = None
                for fn, waits, pub, inc, ev in self.streams[engname]:
                    spaced = False
                    for w in waits:
                        assert w.val is not None, f"unsealed event waited on {engname}"
                        sid = id(w.sem)
                        if (hw_fast and inc == 1 and sid not in dma_sems
                                and pub_engines.get(sid) == {engname}
                                and published.get(sid, 0) >= w.val):
                            if (not spaced and prev_pub is not None
                                    and prev_pub[0] == sid and prev_pub[1] == w.val):
                                self.spacers[engname](eng)
                                spaced = True
                                prev_pub = None
                            continue
                        if waited.get(sid, -1) < w.val:
                            eng.wait_ge(w.sem.h, w.val)
                            waited[sid] = w.val
                    inst = fn(eng)
                    if pub is not None:
                        assert inst is not None, f"pub with no instruction on {engname}"
                        inst.then_inc(pub.h, inc)
                        sid2 = id(pub)
                        published[sid2] = published.get(sid2, 0) + inc
                        prev_pub = (sid2, published[sid2])
                    else:
                        prev_pub = None
            return _
        block.sync(section("sync"))
        block.scalar(section("scalar"))
        block.vector(section("vector"))
        block.gpsimd(section("gpsimd"))
        block.tensor(section("tensor"))


def bcast(ap, shape):
    return ap.broadcast_to(list(shape))


def build(n_tiles=32, debug=()):
    from contextlib import ExitStack
    nc = bass.Bass("TRN2", target_bir_lowering=False)
    T = n_tiles * TT
    CP = n_tiles * H

    x_in = nc.dram_tensor("x_sh", [T, D], F32, kind="ExternalInput").ap()
    w_attn = nc.dram_tensor("w_attn", [D, COLS], F32, kind="ExternalInput").ap()
    b_attn = nc.dram_tensor("b_attn", [1, COLS], F32, kind="ExternalInput").ap()
    lnw_in = nc.dram_tensor("lnw_rs", [128, DC], F32, kind="ExternalInput").ap()
    lnb_in = nc.dram_tensor("lnb_rs", [128, DC], F32, kind="ExternalInput").ap()
    wfc_in = nc.dram_tensor("w_fc", [JH, D], F32, kind="ExternalInput").ap()
    bfc_in = nc.dram_tensor("b_fc", [1, JH], F32, kind="ExternalInput").ap()
    wpj_in = nc.dram_tensor("w_proj", [D, JH], F32, kind="ExternalInput").ap()
    bpj_in = nc.dram_tensor("b_proj", [1, D], F32, kind="ExternalInput").ap()
    outw_in = nc.dram_tensor("out_w", [NS * H, D], F32, kind="ExternalInput").ap()
    outb_in = nc.dram_tensor("out_b", [1, D], F32, kind="ExternalInput").ap()
    id_f = nc.dram_tensor("id_f", [128, 128], F32, kind="ExternalInput").ap()
    id_b = nc.dram_tensor("id_b", [128, 128], BF16, kind="ExternalInput").ap()
    ones_r = nc.dram_tensor("ones_r", [1, 128], F32, kind="ExternalInput").ap()
    g0m_in = nc.dram_tensor("g0mask", [1, 8], F32, kind="ExternalInput").ap()

    out_t = nc.dram_tensor("out_sh", [T, D], F32, kind="ExternalOutput")
    out_ap = out_t.ap()

    r_scr = nc.dram_tensor("r_scr", [T, D], F32)
    lg_scr = nc.dram_tensor("lg_scr", [n_tiles * H * TT * 88], F32)
    hq_scr = nc.dram_tensor("hq_scr", [n_tiles * JC * 128 * 128], BF16)
    wpqt_scr = nc.dram_tensor("wpqt_scr", [DC * 4 * 8 * 128 * 128], BF16)
    acc_scr = nc.dram_tensor("acc_scr", [T, D], F32)
    pre1_scr = nc.dram_tensor("pre1_scr", [CP * NS], F32)
    pre2_scr = nc.dram_tensor("pre2_scr", [CP * NS], F32)
    dc_scr = nc.dram_tensor("dc_scr", [2 * CP * NS * TT], F32)
    ue_scr = nc.dram_tensor("ue_scr", [H * NS], F32)
    hm_scr = nc.dram_tensor("hm_scr", [1, H * NS], F32)
    ag_scr = nc.dram_tensor("ag_scr", [N_CORES, H * NS], F32)
    g0_scr = nc.dram_tensor("g0_scr", [H * NS], F32)
    g_scr = nc.dram_tensor("g_scr", [CP * NS * TT], F32)
    ys_scr = nc.dram_tensor("ys_scr", [CP * NS * TT], BF16)

    dbg = {}

    def dbg_out(name, shape, dt=F32):
        if name in debug:
            dbg[name] = nc.dram_tensor("dbg_" + name, list(shape), dt, kind="ExternalOutput").ap()
            return dbg[name]
        return None

    es = ExitStack()
    sb = lambda name, shape, dt=F32: es.enter_context(nc.sbuf_tensor(name, list(shape), dt))
    psum = lambda name, shape, dt=F32: es.enter_context(nc.psum_tensor(name, list(shape), dt))
    _frees = []

    def sbf(name, shape, dt=F32):
        cm = nc.sbuf_tensor(name, list(shape), dt)
        h = cm.__enter__()
        _frees.append(cm)
        return h

    def psumf(name, shape, dt=F32):
        cm = nc.psum_tensor(name, list(shape), dt)
        h = cm.__enter__()
        _frees.append(cm)
        return h

    def free_phase1():
        for cm in reversed(_frees):
            cm.__exit__(None, None, None)
        _frees.clear()

    ones_sb = sb("ones1", [1, 128])
    bpj_sb = sb("bpj", [1, D])
    c_emask = sb("c_emask", [128, 1], I32)
    c_15 = sb("c_15", [128, 1])
    iota8 = sb("iota8", [128, 8])
    iorev8 = sb("iorev8", [128, 8])
    jnk = sb("jnk", [128, 4])
    XR = 2
    qs_amax = [sbf(f"q_amax{i}", [128, 64]) for i in range(2)]
    qs_sclb = [sbf(f"q_sclb{i}", [128, 64], I32) for i in range(2)]
    qs_invb = [sbf(f"q_invb{i}", [128, 64], I32) for i in range(2)]
    wattn_sb = sbf("wattn", [128, DC, COLS])
    battn_sb = sbf("battn", [1, COLS])
    lnw_sb = sbf("lnw", [128, DC])
    lnb_sb = sbf("lnb", [128, DC])
    idf_sb = sbf("idf", [128, 128])
    idb_sb = sbf("idb", [128, 128], BF16)
    bfc_sb = sbf("bfc", [1, JH])
    st_sb = sbf("st_sb", [128, 16])
    stv_sb = sbf("stv_sb", [128, 8])
    x_sb = sbf("x_sb", [128, XR, D])
    sq_sb = sbf("sq_sb", [128, D])
    xn_sb = sbf("xn_sb", [128, 2, D])
    xs_sb = sbf("xs_sb", [128, 2, D])
    r_sb = sbf("r_sb", [128, 2, D])
    xq_sb = sbf("xq_sb", [128, 2, D], BF16)
    xqf_sb = sbf("xqf_sb", [128, 2, DC, 128], BF16)
    xn2_sb = sbf("xn2_sb", [128, 2, DC, 128])
    lgs_sb = sbf("lgs_sb", [128, 2, COLS])
    h_sb = sbf("h_sb", [128, 2, 2048])
    hq_sb = sbf("hq_sb", [128, 2, 2048], BF16)
    hqf_sb = sbf("hqf_sb", [128, 2, 8, 128], BF16)
    wst_sb = sbf("wst_sb", [128, 2, 1024])
    wq_sb = sbf("wq_sb", [128, 2, 1024], BF16)
    wqt_sb = sbf("wqt_sb", [128, 2, 1024], BF16)
    wfcqT = sbf("wfcqT", [128, DC, JH], BF16)

    S_MU1, S_RSTD1, S_MU2, S_RSTD2, S_B1, S_B2, S_VE, S_R1P = 0, 1, 2, 3, 4, 5, 6, 7

    ps_attn = psumf("ps_attn", [128, COLS])
    ps_fc = [psumf(f"ps_fc{i}", [128, 512]) for i in range(4)]
    ps_xqt = psumf("ps_xqt", [128, 1024], BF16)
    ps_xst = psumf("ps_xst", [128, 512])
    ps_hqt = psumf("ps_hqt", [128, 1024], BF16)

    with nc.Block() as block:
        spacers = {
            "vector": lambda e: e.tensor_copy(jnk[:, 0:1], c_15[:, :]),
            "scalar": lambda e: e.activation(jnk[:, 1:2], c_15[:, :], AF.Identity),
            "gpsimd": lambda e: e.memset(jnk[:, 2:3], 0.0),
            "tensor": lambda e: e.nop(),
            "sync": lambda e: e.nop(),
        }
        p = Plan(nc, es, spacers)
        sWL = p.sem("sWL")
        sSC = p.sem("sSC")
        sDV = p.sem("sDV")
        sPE = p.sem("sPE")
        sX = [p.sem("sX0"), p.sem("sX1")]
        sWst = [p.sem("sWst0"), p.sem("sWst1")]
        sWqS = [p.sem("sWqS0"), p.sem("sWqS1")]
        sRs = [p.sem("sRs0"), p.sem("sRs1")]
        sLgs = [p.sem("sLgs0"), p.sem("sLgs1")]
        sHqs = [p.sem("sHqs0"), p.sem("sHqs1")]
        sCst = p.sem("sCst")
        sPh = p.sem("sPh")
        sDbg = p.sem("sDbg")
        dbg_prev = [None]

        def dbg_dma(out, in_, waits):
            dbg_prev[0] = p.dma("gpsimd", out, in_, waits=list(waits) + [dbg_prev[0]], pub=sDbg)
            return dbg_prev[0]

        bank = {"attn": None, "fc0": None, "fc1": None, "xqt": None, "xst": None, "hqt": None}

        p.emit("gpsimd", lambda e: e.memset(c_emask[:, :], EMASK))
        p.emit("gpsimd", lambda e: e.memset(c_15[:, :], 1.5))
        p.emit("gpsimd", lambda e: e.iota(iota8[:, :], pattern=[[1, 8]], base=0,
                                          channel_multiplier=0, allow_small_or_imprecise_dtypes=True))
        evConst = p.emit("gpsimd", lambda e: e.iota(iorev8[:, :], pattern=[[-1, 8]], base=8,
                                                    channel_multiplier=0,
                                                    allow_small_or_imprecise_dtypes=True), pub=sCst)

        p.dma("sync", wattn_sb[:, :, :], w_attn.rearrange("(c p) n -> p c n", p=128), pub=sWL)
        p.dma("sync", battn_sb[:, :], b_attn, pub=sWL)
        p.dma("sync", lnw_sb[:, :], lnw_in, pub=sWL)
        p.dma("sync", lnb_sb[:, :], lnb_in, pub=sWL)
        p.dma("sync", idf_sb[:, :], id_f, pub=sWL)
        p.dma("sync", idb_sb[:, :], id_b, pub=sWL)
        p.dma("sync", ones_sb[:, :], ones_r, pub=sWL)
        p.dma("sync", bfc_sb[:, :], bfc_in, pub=sWL)
        evSmall = p.dma("sync", bpj_sb[:, :], bpj_in, pub=sWL)

        def quantize_chain(ch, qi, src, dst, W, waits, pub):
            nb = W // 32
            amax, sclb, invb = qs_amax[qi], qs_sclb[qi], qs_invb[qi]
            ch.emit(lambda e: e.tensor_reduce(
                amax[:, 0:nb].rearrange("p (n o) -> p n o", o=1),
                src.rearrange("p (n k) -> p n k", k=32),
                axis=AX.X, op=ALU.max, apply_absolute_value=True),
                waits=list(waits) + [evConst])
            ch.emit(lambda e: e.tensor_scalar(amax[:, 0:nb], amax[:, 0:nb], 1e-30, None, ALU.max))
            ch.emit(lambda e: e.tensor_scalar(
                sclb[:, 0:nb], amax[:, 0:nb].bitcast(I32), c_emask[:, :], None, ALU.bitwise_and))
            ch.emit(lambda e: e.tensor_scalar(sclb[:, 0:nb], sclb[:, 0:nb], EXP3, None, ALU.subtract))
            ch.emit(lambda e: e.tensor_scalar(invb[:, 0:nb], sclb[:, 0:nb], -1.0, INVC, ALU.mult, ALU.add))
            ch.emit(lambda e: e.tensor_tensor(
                src.rearrange("p (n k) -> p n k", k=32),
                src.rearrange("p (n k) -> p n k", k=32),
                bcast(invb[:, 0:nb].bitcast(F32).rearrange("p (n o) -> p n o", o=1), (128, nb, 32)),
                ALU.mult))
            ch.emit(lambda e: e.tensor_scalar(src.bitcast(I32), src, MAGIC, None, ALU.add))
            return ch.emit(lambda e: e.scalar_tensor_tensor(
                dst.rearrange("p (n k) -> p n k", k=32),
                src.bitcast(I32).rearrange("p (n k) -> p n k", k=32),
                MAGIC,
                bcast(sclb[:, 0:nb].bitcast(F32).rearrange("p (n o) -> p n o", o=1), (128, nb, 32)),
                ALU.subtract, ALU.mult), pub=pub)

        evWtE = [None, None]
        evWqSp = [None, None]
        evWstFree = [None, None]
        evWfcDone = None
        evWpjDone = None

        def wprep_pe_and_out(sl, kind, jt, dcjs, evq):
            evT = None
            for k in range(8):
                ww = [evq, bank["xqt"], evSmall] if k == 0 else ()
                evT = p.emit("tensor", lambda e, k=k, sl=sl: e.transpose(
                    ps_xqt[:, k * 128:(k + 1) * 128], wq_sb[:, sl, k * 128:(k + 1) * 128],
                    idb_sb[:, :]), waits=ww, pub=(sPE if k == 7 else None))
            evE = p.emit("scalar", lambda e, sl=sl: e.activation(
                wqt_sb[:, sl, :], ps_xqt[:, :], AF.Copy), waits=[evT, evWqSp[sl]], pub=sSC)
            bank["xqt"] = evE
            evWtE[sl] = evE
            if kind == "fc":
                return p.emit("vector", lambda e, sl=sl, jt=jt: e.tensor_copy(
                    wfcqT[:, :, jt * 128:(jt + 1) * 128],
                    wqt_sb[:, sl, :].rearrange("p (c k) -> p c k", c=8)),
                    waits=[evE], pub=sDV)
            else:
                dcid, js = dcjs
                off = (dcid * 4 + js) * 8 * 128 * 128
                dst = bass.AP(wpqt_scr, off, [[128, 128], [128 * 128, 8], [1, 128]])
                ev = p.dma("gpsimd", dst,
                           wqt_sb[:, sl, :].rearrange("p (c k) -> p c k", c=8),
                           waits=[evE], pub=sWqS[sl])
                evWqSp[sl] = ev
                return ev

        units = [("fc", jt, None) for jt in range(JC)] + \
                [("pj", None, (dcid, js)) for dcid in range(DC) for js in range(4)]
        for u0 in range(0, len(units), 2):
            pair = units[u0:u0 + 2]
            evqs = []
            chains = []
            for i, (kind, jt, dcjs) in enumerate(pair):
                sl = i
                if kind == "fc":
                    src_ap = wfc_in[(jt) * 128:(jt + 1) * 128, :]
                else:
                    dcid, js = dcjs
                    src_ap = wpj_in[dcid * 128:(dcid + 1) * 128, js * 1024:(js + 1) * 1024]
                evWst = p.dma("sync", wst_sb[:, sl, :], src_ap,
                              waits=[evWstFree[sl]], pub=sWst[sl])
                ch = p.chain(sDV)
                evq = quantize_chain(ch, sl, wst_sb[:, sl, :], wq_sb[:, sl, :], 1024,
                                     [evWst, evWtE[sl]], sDV)
                evqs.append(evq)
                chains.append(ch)
                evWstFree[sl] = evq
            p.seal("vector", *chains)
            for i, (kind, jt, dcjs) in enumerate(pair):
                evDone = wprep_pe_and_out(i, kind, jt, dcjs, evqs[i])
                if kind == "fc":
                    evWfcDone = evDone
                else:
                    evWpjDone = evDone

        evXfree = [None] * XR
        ev_xs_t, ev_s3_t, evA_t, evB_t = {}, {}, {}, {}
        evXq_t, evXqE_t, evXn2E_t, evXn2h0 = {}, {}, {}, {}
        evG_t, evHq_t, evHqSp_t = {}, {}, {}
        evLgE_t, evLgSp_t, evRSp_t = {}, {}, {}
        peT_xq, peT_xs, peMa, peMf = {}, {}, {}, {}
        evDbgXn_t, evDbgH_t, evDbgHq_t = {}, {}, {}
        evXsFree = [None, None]
        evXnFree = [None, None]
        evRFree = [None, None]
        evXqFree = [None, None]

        dbg_xn = dbg_out("xn", [T, D])
        dbg_xs = dbg_out("xs", [T, D])
        dbg_lg = dbg_out("logits", [T, COLS])
        dbg_xq = dbg_out("xq", [T, D], BF16)
        dbg_h = dbg_out("h", [T, JH])
        dbg_hq = dbg_out("hq", [T, JH], BF16)

        def ln_small_chain(ch, rstd_col, extra=()):
            mu_col = rstd_col - 1
            mu = st_sb[:, mu_col:mu_col + 1]
            ch.emit(lambda e: e.tensor_scalar(mu, st_sb[:, S_B1:S_B1 + 1], 1.0 / D, None, ALU.mult),
                    waits=extra)
            ch.emit(lambda e: e.tensor_scalar(stv_sb[:, 4:5], st_sb[:, S_B2:S_B2 + 1], 1.0 / D, None, ALU.mult))
            ch.emit(lambda e: e.tensor_tensor(stv_sb[:, 3:4], mu, mu, ALU.mult))
            ch.emit(lambda e: e.tensor_tensor(stv_sb[:, 4:5], stv_sb[:, 4:5], stv_sb[:, 3:4], ALU.subtract))
            ch.emit(lambda e: e.tensor_scalar(st_sb[:, S_VE:S_VE + 1], stv_sb[:, 4:5], LN_EPS, None, ALU.add))
            v = st_sb[:, S_VE:S_VE + 1]
            r = st_sb[:, rstd_col:rstd_col + 1]
            t0 = stv_sb[:, 0:1]
            t1 = stv_sb[:, 1:2]
            nvh = stv_sb[:, 2:3]
            ch.emit(lambda e: e.tensor_scalar(t0.bitcast(I32), v.bitcast(I32), -0.5, RSQC, ALU.mult, ALU.add))
            ch.emit(lambda e: e.tensor_scalar(nvh, v, -0.5, None, ALU.mult))
            ch.emit(lambda e: e.tensor_copy(r, t0))
            for _ in range(4):
                ch.emit(lambda e: e.tensor_tensor(t1, r, r, ALU.mult))
                ch.emit(lambda e: e.scalar_tensor_tensor(t1, t1, nvh, c_15[:, :], ALU.mult, ALU.add))
                ch.emit(lambda e: e.tensor_tensor(r, r, t1, ALU.mult))
            ch.emit(lambda e: e.tensor_tensor(stv_sb[:, 5:6], mu, r, ALU.mult))
            ev_end = ch.emit(lambda e: e.tensor_scalar(stv_sb[:, 5:6], stv_sb[:, 5:6], -1.0, None, ALU.mult),
                             pub=sDV)
            if rstd_col == S_RSTD1:
                ev_end = ch.emit(lambda e: e.tensor_scalar(
                    st_sb[:, S_R1P:S_R1P + 1], r, 1.0, None, ALU.add), pub=sDV)
            return ev_end

        NT = n_tiles
        for blk in range(NT + 1):
            c = blk if blk < NT else None
            b = blk - 1 if blk >= 1 else None

            if c is not None:
                xsl = c % XR
                csl = c % 2
                evX = p.dma("sync", x_sb[:, xsl, :], x_in[c * TT:(c + 1) * TT, :],
                            waits=[evXfree[xsl]], pub=sX[c % 2])

            if c is not None:
                ev_s1 = p.emit("scalar", lambda e, xsl=xsl: e.activation(
                    sq_sb[:, :], x_sb[:, xsl, :], AF.Identity, accum_out=st_sb[:, S_B1:S_B1 + 1]),
                    waits=[evX, evConst, evB_t.get(c - 1), evSmall], pub=sSC)
                ev_s2 = p.emit("scalar", lambda e, xsl=xsl: e.activation(
                    sq_sb[:, :], x_sb[:, xsl, :], AF.Square, accum_out=st_sb[:, S_B2:S_B2 + 1]),
                    waits=[ev_s1], pub=sSC)

            chC = chA = None
            if b is not None:
                bsl = b % 2
                chC = p.chain(sDV)
                evXq_t[b] = quantize_chain(chC, 0, xn_sb[:, bsl, :], xq_sb[:, bsl, :], D,
                                           [ev_xs_t[b], evXqFree[bsl], evDbgXn_t.get(b)], sDV)
                evXnFree[bsl] = evXq_t[b]
            if c is not None:
                chA = p.chain(sDV)
                evA_t[c] = ln_small_chain(chA, S_RSTD1, extra=[ev_s2])
            p.seal("vector", chC, chA)

            if b is not None:
                evT = None
                for k in range(8):
                    ww = [evXq_t[b], bank["xqt"], evSmall] if k == 0 else ()
                    evT = p.emit("tensor", lambda e, k=k, bsl=bsl: e.transpose(
                        ps_xqt[:, k * 128:(k + 1) * 128], xq_sb[:, bsl, k * 128:(k + 1) * 128],
                        idb_sb[:, :]), waits=ww, pub=(sPE if k == 7 else None))
                peT_xq[b] = evT
                evXqE_t[b] = p.emit("scalar", lambda e, bsl=bsl: e.activation(
                    xqf_sb[:, bsl, :, :].rearrange("p c k -> p (c k)"), ps_xqt[:, :], AF.Copy),
                    waits=[peT_xq[b], peMf.get(b - 2, [None, None])[1]], pub=sSC)
                bank["xqt"] = evXqE_t[b]
                evXqFree[bsl] = evXqE_t[b]
                peT_xs[b] = [None, None]
                for half in range(2):
                    for k in range(4):
                        kk = half * 4 + k
                        ww = ()
                        if k == 0:
                            ww = [ev_xs_t[b], bank["xst"], evSmall] if half == 0 else [evXn2h0[b]]
                        evTs = p.emit("tensor", lambda e, kk=kk, k=k, bsl=bsl: e.transpose(
                            ps_xst[:, k * 128:(k + 1) * 128], xs_sb[:, bsl, kk * 128:(kk + 1) * 128],
                            idf_sb[:, :]), waits=ww, pub=(sPE if k == 3 else None))
                    peT_xs[b][half] = evTs
                    evXn2l = None
                    for k in range(4):
                        kk = half * 4 + k
                        ww = [peT_xs[b][half]] if k == 0 else []
                        if k == 0 and half == 0:
                            ww = [peT_xs[b][half], peMa.get(b - 2)]
                        evXn2l = p.emit("scalar", lambda e, kk=kk, k=k, bsl=bsl: e.activation(
                            xn2_sb[:, bsl, kk, :], ps_xst[:, k * 128:(k + 1) * 128], AF.Identity,
                            bias=lnb_sb[:, kk:kk + 1], scale=lnw_sb[:, kk:kk + 1]),
                            waits=ww, pub=(sSC if k == 3 else None))
                    if half == 0:
                        evXn2h0[b] = evXn2l
                bank["xst"] = evXn2l
                evXn2E_t[b] = evXn2l
                evXsFree[bsl] = evXn2l
                for dc in range(DC):
                    ww = [evXn2E_t[b], bank["attn"], evSmall] if dc == 0 else ()
                    p.emit("tensor", lambda e, dc=dc, bsl=bsl: e.matmul(
                        ps_attn[:, :], xn2_sb[:, bsl, dc, :], wattn_sb[:, dc, :],
                        start=(dc == 0), stop=False), waits=ww)
                peMa[b] = p.emit("tensor", lambda e: e.matmul(
                    ps_attn[:, :], ones_sb[:, :], battn_sb[:, :], start=False, stop=True), pub=sPE)
                evLgE_t[b] = p.emit("scalar", lambda e, bsl=bsl: e.activation(
                    lgs_sb[:, bsl, :], ps_attn[:, :], AF.Copy),
                    waits=[peMa[b], evLgSp_t.get(b - 2)], pub=sSC)
                bank["attn"] = evLgE_t[b]
                peMf[b] = [None, None]
                for half in range(2):
                    for dc in range(DC):
                        for q in range(4):
                            jt = half * 4 + q
                            ww = ()
                            if dc == 0 and q == 0:
                                if half == 0:
                                    ww = [evXqE_t[b], evWfcDone, evG_t.get((b - 1, 1)), evSmall]
                                else:
                                    ww = [evG_t[(b, 0)]]
                            p.emit("tensor", lambda e, dc=dc, q=q, jt=jt, bsl=bsl: e.matmul(
                                ps_fc[q][:, :], xqf_sb[:, bsl, dc, :],
                                wfcqT[:, dc, jt * 512:(jt + 1) * 512],
                                start=(dc == 0), stop=False), waits=ww)
                    evMfl = None
                    for q in range(4):
                        jt = half * 4 + q
                        evMfl = p.emit("tensor", lambda e, q=q, jt=jt: e.matmul(
                            ps_fc[q][:, :], ones_sb[:, :], bfc_sb[:, jt * 512:(jt + 1) * 512],
                            start=False, stop=True), pub=(sPE if q == 3 else None))
                    peMf[b][half] = evMfl
                    evG = None
                    for q in range(4):
                        w0 = []
                        if q == 0:
                            w0 = [peMf[b][half], evHq_t.get((b - 1, half)), evHqSp_t.get((b - 1, half)),
                                  evDbgH_t.get((b - 1, half))]
                        evG = p.emit("scalar", lambda e, q=q, half=half: e.activation(
                            h_sb[:, half, q * 512:(q + 1) * 512], ps_fc[q][:, :], AF.Gelu),
                            waits=w0, pub=(sSC if q == 3 else None))
                    evG_t[(b, half)] = evG
                    if dbg_h is not None:
                        evDbgH_t[(b, half)] = dbg_dma(
                            dbg_h[b * TT:(b + 1) * TT, half * 2048:(half + 1) * 2048],
                            h_sb[:, half, :], [evG])

            if c is not None:
                ev_xn = p.emit("scalar", lambda e, xsl=xsl, csl=csl: e.activation(
                    xn_sb[:, csl, :], x_sb[:, xsl, :], AF.Identity,
                    bias=stv_sb[:, 5:6], scale=st_sb[:, S_RSTD1:S_RSTD1 + 1],
                    accum_out=st_sb[:, S_B1:S_B1 + 1]),
                    waits=[evA_t[c], evXnFree[csl]], pub=sSC)
                ev_r = p.emit("scalar", lambda e, xsl=xsl, csl=csl: e.activation(
                    r_sb[:, csl, :], x_sb[:, xsl, :], AF.Identity,
                    bias=stv_sb[:, 5:6], scale=st_sb[:, S_R1P:S_R1P + 1]),
                    waits=[evRFree[csl]], pub=sSC)
                evXfree[xsl] = ev_r
                ev_s3_t[c] = p.emit("scalar", lambda e, csl=csl: e.activation(
                    sq_sb[:, :], xn_sb[:, csl, :], AF.Square, accum_out=st_sb[:, S_B2:S_B2 + 1]),
                    waits=[ev_xn, ev_s2], pub=sSC)
                ev_xs = None
            chD0 = chD1 = chB = None
            if b is not None:
                chD0 = p.chain(sDV)
                chD1 = p.chain(sDV)
                evHq_t[(b, 0)] = quantize_chain(chD0, 0, h_sb[:, 0, :], hq_sb[:, 0, :], 2048,
                                                [evG_t[(b, 0)], evHqSp_t.get((b - 1, 0)),
                                                 evDbgH_t.get((b, 0)), evDbgHq_t.get((b - 1, 0))], sDV)
                evHq_t[(b, 1)] = quantize_chain(chD1, 1, h_sb[:, 1, :], hq_sb[:, 1, :], 2048,
                                                [evG_t[(b, 1)], evHqSp_t.get((b - 1, 1)),
                                                 evDbgH_t.get((b, 1)), evDbgHq_t.get((b - 1, 1))], sDV)
            if c is not None:
                chB = p.chain(sDV)
                evB_t[c] = ln_small_chain(chB, S_RSTD2, extra=[ev_s3_t[c]])
            p.seal("vector", chD0, chD1, chB)

            if c is not None:
                ev_xs = p.emit("scalar", lambda e, csl=csl: e.activation(
                    xs_sb[:, csl, :], xn_sb[:, csl, :], AF.Identity,
                    bias=stv_sb[:, 5:6], scale=st_sb[:, S_RSTD2:S_RSTD2 + 1]),
                    waits=[evB_t[c], evXsFree[csl], ev_s3_t[c]], pub=sSC)
                ev_xs_t[c] = ev_xs
                evRFree[csl] = evRSp_t[c] = p.dma("gpsimd", r_scr.ap()[c * TT:(c + 1) * TT, :],
                                                  r_sb[:, csl, :], waits=[ev_r], pub=sRs[csl])
                if dbg_xn is not None:
                    evDbgXn_t[c] = dbg_dma(dbg_xn[c * TT:(c + 1) * TT, :], xn_sb[:, csl, :], [ev_xs])
                if dbg_xs is not None:
                    dbg_dma(dbg_xs[c * TT:(c + 1) * TT, :], xs_sb[:, csl, :], [ev_xs])

            if b is not None:
                for half in range(2):
                    evSpPrev = evHqSp_t.get((b - 1, half))
                    evHqEg = None
                    for g in range(2):
                        for k in range(8):
                            kk = g * 8 + k
                            ww = ()
                            if k == 0:
                                ww = [evHq_t[(b, half)], bank["hqt"], evSmall] if g == 0 else [evHqEg]
                            evTh = p.emit("tensor", lambda e, kk=kk, k=k, half=half: e.transpose(
                                ps_hqt[:, k * 128:(k + 1) * 128], hq_sb[:, half, kk * 128:(kk + 1) * 128],
                                idb_sb[:, :]), waits=ww, pub=(sPE if k == 7 else None))
                        evHqEg = p.emit("scalar", lambda e, half=half: e.activation(
                            hqf_sb[:, half, :, :].rearrange("p c k -> p (c k)"), ps_hqt[:, :], AF.Copy),
                            waits=[evTh, evSpPrev], pub=sSC)
                        bank["hqt"] = evHqEg
                        chunk0 = half * 16 + g * 8
                        off = (b * JC + chunk0) * 128 * 128
                        dsth = bass.AP(hq_scr, off, [[128, 128], [128 * 128, 8], [1, 128]])
                        evSpPrev = p.dma("gpsimd", dsth, hqf_sb[:, half, :, :],
                                         waits=[evHqEg], pub=sHqs[half])
                    evHqSp_t[(b, half)] = evSpPrev
                dstlg = bass.AP(lg_scr, b * H * TT * 88, [[88, 128], [TT * 88, 4], [1, 88]])
                evLgSp_t[b] = p.dma("gpsimd", dstlg,
                                    lgs_sb[:, bsl, :].rearrange("p (h w) -> p h w", h=4),
                                    waits=[evLgE_t[b]], pub=sLgs[b % 2])
                if dbg_lg is not None:
                    dbg_dma(dbg_lg[b * TT:(b + 1) * TT, :], lgs_sb[:, bsl, :], [evLgE_t[b]])
                if dbg_xq is not None:
                    dbg_dma(dbg_xq[b * TT:(b + 1) * TT, :], xq_sb[:, bsl, :], [evXq_t[b]])
                if dbg_hq is not None:
                    for half in range(2):
                        evDbgHq_t[(b, half)] = dbg_dma(
                            dbg_hq[b * TT:(b + 1) * TT, half * 2048:(half + 1) * 2048],
                            hq_sb[:, half, :], [evHq_t[(b, half)]])



        TPC = TT
        TL = n_tiles
        evP1 = p.emit("gpsimd", lambda e: e.memset(jnk[:, 3:4], 0.0),
                      waits=[x for x in [peMf.get(NT - 1, [None, None])[1],
                                         evHqSp_t.get((NT - 1, 0)), evHqSp_t.get((NT - 1, 1)),
                                         evRSp_t.get(NT - 1), evRSp_t.get(NT - 2),
                                         evLgSp_t.get(NT - 1), evLgSp_t.get(NT - 2),
                                         evXq_t.get(NT - 1), evXqE_t.get(NT - 1),
                                         evXn2E_t.get(NT - 1), peMa.get(NT - 1),
                                         ev_s3_t.get(NT - 1), evWpjDone, dbg_prev[0]] if x],
                      pub=sCst)
        free_phase1()

        dbg_E = dbg_out("E", [CP, TPC * 64])
        dbg_pidx = dbg_out("pidx", [CP, TPC * 8])
        dbg_psi = dbg_out("psi", [CP, TPC * 8])
        dbg_phi = dbg_out("phi", [CP, TPC * 8])
        dbg_dtil = dbg_out("dtil", [CP, NS * TPC])
        dbg_ctil = dbg_out("ctil", [CP, NS * TPC])
        dbg_ysT = dbg_out("ysT", [NS * H, T], BF16)

        wpqT_sb = sb("wpqT", [128, JC, 1024], BF16)
        idp_sb = sb("idp", [128, 128])
        psg2 = sb("psg2", [CP, TPC, 8])
        smallE = sb("smallE", [4, TL, 8])
        smallP = sb("smallP", [4, TL + 1, 8])
        smallU = sb("smallU", [4, 8])
        smallH = sb("smallH", [4, 8])
        smallT = sb("smallT", [4, 8, 8])
        agf = sb("agf", [1, 8, 32])
        magb = sb("magb", [1, 8, 32])
        g0r = sb("g0r", [1, 32])
        g0col = sb("g0col", [32, 1])
        g0m_sb = sb("g0m_sb", [1, 8])
        trib = sb("trib", [128, 8, 8])
        hqs_sb = sb("hqs", [128, 2, JC, 128], BF16)
        rld_sb = sb("rld", [128, 2, D])
        ffe_sb = sb("ffe", [128, 2, D])
        _t2 = []
        def sbt2(name, shape, dt=F32):
            cm = nc.sbuf_tensor(name, list(shape), dt)
            h2 = cm.__enter__()
            _t2.append(cm)
            return h2
        dtil = sbt2("dtil", [CP, NS, TPC])
        ctil = sbt2("ctil", [CP, NS, TPC])
        _t1 = []
        def sbt1(name, shape, dt=F32):
            cm = nc.sbuf_tensor(name, list(shape), dt)
            h1 = cm.__enter__()
            _t1.append(cm)
            return h1
        THL = TPC // 2
        TQ = 16
        lgA = sbt1("lgA", [CP, THL, 88])
        sgS = sbt1("sgS", [CP, THL, 8])
        sgG = sbt1("sgG", [CP, THL, 8])
        bvec = sbt1("bvec", [CP, THL, 8])
        dvec = sbt1("dvec", [CP, TPC, 8])
        cvec = sbt1("cvec", [CP, TPC, 8])
        m64b = sbt1("m64b", [CP, THL])
        rsb = sbt1("rsb", [CP, THL, 8])
        rrb = sbt1("rrb", [CP, THL, 8])
        srt = sbt1("srt", [CP, TQ, 8, 8])
        gtmp = sbt1("gtmp", [CP, TQ, 8, 8])
        rmaxb = sbt1("rmaxb", [CP, TPC, 8])
        rankb = sbt1("rankb", [CP, TPC, 8])
        avn = sbt1("avn", [CP, TQ, 8])
        mk8 = sbt1("mk8", [CP, TQ, 8])
        mkb = sbt1("mkb", [CP, TQ])
        t2b = sbt1("t2b", [CP, TQ, 8])
        m2b = sbt1("m2b", [CP, TQ])
        cselb = sbt1("cselb", [CP, TQ, 8])
        pidxs = sbt1("pidxs", [CP, TPC, 8])
        pidx = sbt1("pidx", [CP, TPC, 8])
        PDf = sbt1("PDf", [CP, TPC // 2, 8, 8], BF16)
        psig = sbt1("psig", [CP, TPC, 8])
        phib = sbt1("phib", [CP, TPC, 8])
        pst = sbt1("pst", [CP, 8, 8])

        ps_pj = [psum(f"ps_pj{i}", [128, 1024]) for i in range(2)]
        ps_mx = [psum(f"ps_mx{i}", [128, 1024]) for i in range(2)]

        sB2 = p.sem("sB2")
        sCC = p.sem("sCC")
        sHqL = [p.sem("sHqL0"), p.sem("sHqL1")]
        sRL = [p.sem("sRL0"), p.sem("sRL1")]
        sAcS = [p.sem("sAcS0"), p.sem("sAcS1")]
        sAcL = [p.sem("sAcL0"), p.sem("sAcL1")]
        sOu = [p.sem("sOu0"), p.sem("sOu1")]
        sWpL = p.sem("sWpL")
        b2_prev = [None]

        def b2dma(out, in_, waits):
            b2_prev[0] = p.dma("scalar", out, in_, waits=list(waits) + [b2_prev[0], evP1], pub=sB2)
            return b2_prev[0]

        evWpL = None
        for dcid in range(DC):
            srcw = bass.AP(wpqt_scr, dcid * 4 * 8 * 128 * 128,
                           [[128, 128], [8 * 128 * 128, 4], [128 * 128, 8], [1, 128]])
            ww0 = [evP1, evWqSp[0], evWqSp[1]] if dcid == 0 else [evP1, evWpL]
            evWpL = p.dma("sync", wpqT_sb[:, :, dcid * 128:(dcid + 1) * 128]
                          .rearrange("p (a b) k -> p a b k", a=4),
                          srcw, waits=ww0, pub=sWpL)
        evIdL = p.dma("sync", idp_sb[:, :], id_f, waits=[evP1, evWpL], pub=sWpL)
        evG0L = p.dma("sync", g0m_sb[:, :], g0m_in, waits=[evP1, evIdL], pub=sWpL)
        evHqL = [None, None]
        evRL = [None, None]
        evPjE = {}
        evAccSp = {}
        proj_next = [0]

        def pump_proj(k):
            while k > 0 and proj_next[0] < NT:
                tl = proj_next[0]
                proj_next[0] += 1
                sl2 = tl % 2
                evHqL[sl2] = p.dma("sync", hqs_sb[:, sl2, :, :],
                                   bass.AP(hq_scr, tl * JC * 128 * 128,
                                           [[128, 128], [128 * 128, JC], [1, 128]]),
                                   waits=[evPjE.get(tl - 2), evP1], pub=sHqL[sl2])
                evRL[sl2] = p.dma("sync", rld_sb[:, sl2, :],
                                  r_scr.ap()[tl * TT:(tl + 1) * TT, :],
                                  waits=[evPjE.get(tl - 2), evP1], pub=sRL[sl2])
                for jc in range(JC):
                    for hf in range(2):
                        ww = ()
                        if jc == 0 and hf == 0:
                            ww = [evHqL[sl2], evWpL, evPjE.get(tl - 2), evG0L]
                        p.emit("tensor", lambda e, jc=jc, hf=hf, sl2=sl2, tl=tl: e.matmul(
                            ps_pj[tl % 2][:, hf * 512:(hf + 1) * 512], hqs_sb[:, sl2, jc, :],
                            wpqT_sb[:, jc, hf * 512:(hf + 1) * 512],
                            start=(jc == 0), stop=False), waits=ww)
                for hf in range(2):
                    p.emit("tensor", lambda e, hf=hf, sl2=sl2, tl=tl: e.matmul(
                        ps_pj[tl % 2][:, hf * 512:(hf + 1) * 512], idp_sb[:, :],
                        rld_sb[:, sl2, hf * 512:(hf + 1) * 512],
                        start=False, stop=False),
                        waits=[evRL[sl2], evIdL] if hf == 0 else ())
                evMj = None
                for hf in range(2):
                    evMj = p.emit("tensor", lambda e, hf=hf, tl=tl: e.matmul(
                        ps_pj[tl % 2][:, hf * 512:(hf + 1) * 512], ones_sb[:, :],
                        bpj_sb[:, hf * 512:(hf + 1) * 512],
                        start=False, stop=True), pub=(sPE if hf == 1 else None))
                evPjE[tl] = p.emit("scalar", lambda e, sl2=sl2, tl=tl: e.activation(
                    ffe_sb[:, sl2, :], ps_pj[tl % 2][:, :], AF.Copy),
                    waits=[evMj, evAccSp.get(tl - 2)], pub=sSC)
                evAccSp[tl] = p.dma("sync", acc_scr.ap()[tl * TT:(tl + 1) * TT, :],
                                    ffe_sb[:, sl2, :], waits=[evPjE[tl]], pub=sAcS[sl2])
                k -= 1

        def drain_acc(k):
            pump_proj(k)

        ch = p.chain(sDV)
        ch.emit(lambda e: e.tensor_tensor(
            trib[:, :, :], bcast(iota8[:, :].unsqueeze(2), (128, 8, 8)),
            bcast(iota8[:, :].unsqueeze(1), (128, 8, 8)), ALU.is_lt),
            waits=[evConst, evP1])
        p.seal("vector", ch)
        evTri = ch.last

        evPsiL = None
        evDbgE = None
        evLgL = None
        for th in range(2):
            tof = th * THL
            evLgL = p.dma("scalar", lgA[:, :, :],
                          bass.AP(lg_scr, tof * 88, [[TPC * 88, CP], [88, THL], [1, 88]]),
                          waits=[evP1, evPsiL], pub=sB2)
            evSgS = p.emit("scalar", lambda e: e.activation(sgS[:, :, :], lgA[:, :, 64:72], AF.Sigmoid),
                           waits=[evLgL], pub=sSC)
            evSgG = p.emit("scalar", lambda e: e.activation(sgG[:, :, :], lgA[:, :, 72:80], AF.Sigmoid),
                           waits=[evSgS], pub=sSC)
            evBv = p.emit("scalar", lambda e: e.activation(bvec[:, :, :], lgA[:, :, 80:88], AF.Copy),
                          waits=[evSgG], pub=sSC)
            ch = p.chain(sDV)
            ch.emit(lambda e: e.tensor_scalar(sgS[:, :, :], sgS[:, :, :], 2.0, -1.0, ALU.mult, ALU.add),
                    waits=[evSgS])
            ch.emit(lambda e, tof=tof: e.tensor_tensor(dvec[:, tof:tof + THL, :], sgG[:, :, :],
                                                       sgS[:, :, :], ALU.mult), waits=[evSgG])
            ch.emit(lambda e: e.tensor_scalar(sgG[:, :, :], sgG[:, :, :], -1.0, 1.0, ALU.mult, ALU.add))
            ch.emit(lambda e, tof=tof: e.tensor_tensor(cvec[:, tof:tof + THL, :], sgG[:, :, :],
                                                       bvec[:, :, :], ALU.mult), waits=[evBv])
            ch.emit(lambda e: e.tensor_reduce(
                m64b[:, :].rearrange("p (t o) -> p t o", o=1),
                lgA[:, :, 0:64], axis=AX.X, op=ALU.max))
            evEsub = ch.emit(lambda e: e.tensor_tensor(
                lgA[:, :, 0:64], lgA[:, :, 0:64],
                bcast(m64b[:, :].rearrange("p (t o) -> p t o", o=1), (CP, THL, 64)), ALU.subtract))
            p.seal("vector", ch)
            drain_acc(2)
            evExp = p.emit("scalar", lambda e: e.activation(
                lgA[:, :, 0:64], lgA[:, :, 0:64], AF.Exp), waits=[evEsub], pub=sSC)
            E4 = lgA[:, :, 0:64].rearrange("p t (i j) -> p t i j", j=8)
            ch = p.chain(sDV)
            prevw = evExp
            for it in range(5):
                ch.emit(lambda e, E4=E4: e.tensor_reduce(
                    rsb[:, :, :].unsqueeze(3), E4,
                    axis=AX.X, op=ALU.add), waits=[prevw])
                prevw = None
                ch.emit(lambda e: e.reciprocal(rrb[:, :, :], rsb[:, :, :]))
                ch.emit(lambda e, E4=E4: e.tensor_tensor(
                    E4, E4, bcast(rrb[:, :, :].unsqueeze(3), (CP, THL, 8, 8)),
                    ALU.mult))
                ch.emit(lambda e, E4=E4: e.tensor_reduce(
                    rsb[:, :, :].unsqueeze(3),
                    E4.rearrange("p t i j -> p t j i"), axis=AX.X, op=ALU.add))
                ch.emit(lambda e: e.reciprocal(rrb[:, :, :], rsb[:, :, :]))
                ch.emit(lambda e, E4=E4: e.tensor_tensor(
                    E4, E4, bcast(rrb[:, :, :].unsqueeze(2), (CP, THL, 8, 8)),
                    ALU.mult))
                if it == 2:
                    p.seal("vector", ch)
                    drain_acc(2)
                    nch = p.chain(sDV)
                    nch.last = ch.last
                    ch = nch
            evSk = ch.last
            p.seal("vector", ch)
            drain_acc(2)
            if dbg_E is not None:
                evDbgE = b2dma(dbg_E[:, tof * 64:(tof + THL) * 64], lgA[:, :, 0:64], [evSk])
            evGr = evSk
            for tq in range(THL // TQ):
                t0 = tof + tq * TQ
                tl0 = tq * TQ
                Eh = E4[:, tl0:tl0 + TQ, :, :]
                ch = p.chain(sDV)
                ch.emit(lambda e, Eh=Eh, t0=t0: e.tensor_reduce(
                    rmaxb[:, t0:t0 + TQ, :].unsqueeze(3), Eh,
                    axis=AX.X, op=ALU.max), waits=[evGr])
                ch.emit(lambda e, t0=t0: e.tensor_tensor(
                    gtmp[:, :, :, :],
                    bcast(rmaxb[:, t0:t0 + TQ, :].unsqueeze(3), (CP, TQ, 8, 8)),
                    bcast(rmaxb[:, t0:t0 + TQ, :].unsqueeze(2), (CP, TQ, 8, 8)),
                    ALU.is_gt))
                ch.emit(lambda e, t0=t0: e.tensor_tensor(
                    srt[:, :, :, :],
                    bcast(rmaxb[:, t0:t0 + TQ, :].unsqueeze(3), (CP, TQ, 8, 8)),
                    bcast(rmaxb[:, t0:t0 + TQ, :].unsqueeze(2), (CP, TQ, 8, 8)),
                    ALU.is_equal))
                ch.emit(lambda e: e.tensor_tensor(
                    srt[:, :, :, :], srt[:, :, :, :],
                    bcast(trib[0:CP, :, :].unsqueeze(1), (CP, TQ, 8, 8)), ALU.mult),
                    waits=[evTri])
                ch.emit(lambda e: e.tensor_tensor(gtmp[:, :, :, :], gtmp[:, :, :, :],
                                                  srt[:, :, :, :], ALU.add))
                ch.emit(lambda e, t0=t0: e.tensor_reduce(
                    rankb[:, t0:t0 + TQ, :].unsqueeze(3),
                    gtmp[:, :, :, :].rearrange("p t a b -> p t b a"), axis=AX.X, op=ALU.add))
                for i in range(8):
                    ch.emit(lambda e, i=i, t0=t0: e.tensor_tensor(
                        mk8[:, :, :],
                        bcast(rankb[:, t0:t0 + TQ, i].rearrange("p (t o) -> p t o", o=1),
                              (CP, TQ, 8)),
                        bcast(iota8[0:CP, :].unsqueeze(1), (CP, TQ, 8)), ALU.is_equal))
                    if i == 0:
                        ch.emit(lambda e, i=i, Eh=Eh: e.tensor_tensor(
                            srt[:, :, :, :],
                            bcast(mk8[:, :, :].unsqueeze(3), (CP, TQ, 8, 8)),
                            bcast(Eh[:, :, i, :].unsqueeze(2), (CP, TQ, 8, 8)),
                            ALU.mult))
                    else:
                        ch.emit(lambda e, i=i, Eh=Eh: e.tensor_tensor(
                            gtmp[:, :, :, :],
                            bcast(mk8[:, :, :].unsqueeze(3), (CP, TQ, 8, 8)),
                            bcast(Eh[:, :, i, :].unsqueeze(2), (CP, TQ, 8, 8)),
                            ALU.mult))
                        ch.emit(lambda e: e.tensor_tensor(srt[:, :, :, :], srt[:, :, :, :],
                                                          gtmp[:, :, :, :], ALU.add))
                ch.emit(lambda e: e.memset(avn[:, :, :], 0.0))
                for s in range(8):
                    ch.emit(lambda e, s=s: e.tensor_tensor(mk8[:, :, :], srt[:, :, s, :],
                                                           avn[:, :, :], ALU.add))
                    ch.emit(lambda e: e.tensor_reduce(
                        mkb[:, :].rearrange("p (t o) -> p t o", o=1), mk8[:, :, :],
                        axis=AX.X, op=ALU.max))
                    ch.emit(lambda e: e.tensor_tensor(
                        t2b[:, :, :], mk8[:, :, :],
                        bcast(mkb[:, :].rearrange("p (t o) -> p t o", o=1), (CP, TQ, 8)),
                        ALU.is_equal))
                    ch.emit(lambda e: e.tensor_tensor(
                        t2b[:, :, :], t2b[:, :, :],
                        bcast(iorev8[0:CP, :].unsqueeze(1), (CP, TQ, 8)), ALU.mult))
                    ch.emit(lambda e: e.tensor_reduce(
                        m2b[:, :].rearrange("p (t o) -> p t o", o=1), t2b[:, :, :],
                        axis=AX.X, op=ALU.max))
                    ch.emit(lambda e: e.tensor_tensor(
                        cselb[:, :, :], t2b[:, :, :],
                        bcast(m2b[:, :].rearrange("p (t o) -> p t o", o=1), (CP, TQ, 8)),
                        ALU.is_equal))
                    ch.emit(lambda e: e.scalar_tensor_tensor(
                        avn[:, :, :], cselb[:, :, :], -BIG, avn[:, :, :], ALU.mult, ALU.add))
                    ch.emit(lambda e: e.tensor_tensor(
                        t2b[:, :, :], cselb[:, :, :],
                        bcast(iota8[0:CP, :].unsqueeze(1), (CP, TQ, 8)), ALU.mult))
                    ch.emit(lambda e, s=s, t0=t0: e.tensor_reduce(
                        pidxs[:, t0:t0 + TQ, s].rearrange("p (t o) -> p t o", o=1), t2b[:, :, :],
                        axis=AX.X, op=ALU.add))
                for rk in range(8):
                    ch.emit(lambda e, rk=rk, t0=t0: e.tensor_scalar(
                        mk8[:, :, :], rankb[:, t0:t0 + TQ, :], float(rk), None, ALU.is_equal))
                    if rk == 0:
                        ch.emit(lambda e, rk=rk, t0=t0: e.tensor_tensor(
                            pidx[:, t0:t0 + TQ, :], mk8[:, :, :],
                            bcast(pidxs[:, t0:t0 + TQ, rk].rearrange("p (t o) -> p t o", o=1),
                                  (CP, TQ, 8)), ALU.mult))
                    else:
                        ch.emit(lambda e, rk=rk, t0=t0: e.tensor_tensor(
                            t2b[:, :, :], mk8[:, :, :],
                            bcast(pidxs[:, t0:t0 + TQ, rk].rearrange("p (t o) -> p t o", o=1),
                                  (CP, TQ, 8)), ALU.mult))
                        ch.emit(lambda e, t0=t0: e.tensor_tensor(
                            pidx[:, t0:t0 + TQ, :], pidx[:, t0:t0 + TQ, :], t2b[:, :, :],
                            ALU.add))
                p.seal("vector", ch)
                evGr = ch.last
                drain_acc(2)
            ch = p.chain(sDV)
            ch.emit(lambda e, tof=tof: e.tensor_tensor(
                PDf[:, :, :, :],
                bcast(pidx[:, tof:tof + THL, :].unsqueeze(3), (CP, THL, 8, 8)),
                bcast(iota8[0:CP, :].unsqueeze(1).unsqueeze(1), (CP, THL, 8, 8)), ALU.is_equal),
                waits=[evGr, evDbgE])
            for t in range(tof, tof + THL):
                if t == 0:
                    ch.emit(lambda e: e.tensor_tensor(
                        pst[:, :, :], PDf[:, 0, :, :],
                        bcast(iota8[0:CP, :].unsqueeze(1), (CP, 8, 8)), ALU.mult))
                else:
                    ch.emit(lambda e, t=t, tof=tof: e.tensor_tensor(
                        pst[:, :, :], PDf[:, t - tof, :, :],
                        bcast(psig[:, t - 1, :].unsqueeze(1), (CP, 8, 8)), ALU.mult))
                ch.emit(lambda e, t=t: e.tensor_reduce(
                    psig[:, t, :].rearrange("p (i o) -> p i o", o=1), pst[:, :, :],
                    axis=AX.X, op=ALU.add))
            p.seal("vector", ch)
            evPsiL = ch.last
            drain_acc(2)
            if dbg_pidx is not None:
                b2dma(dbg_pidx[:, tof * 8:(tof + THL) * 8],
                      pidx[:, tof:tof + THL, :].rearrange("p t i -> p (t i)"), [evPsiL])

        evB1 = b2dma(bass.AP(pre1_scr, 0, [[8, CP], [1, 8]]), psig[:, TPC - 1, :], [evPsiL])
        evB2 = b2dma(smallE[:, :, :], bass.AP(pre1_scr, 0, [[8, 4], [32, TL], [1, 8]]), [evB1])
        ch = p.chain(sDV)
        ch.emit(lambda e: e.tensor_copy(smallP[:, 0, :], iota8[0:4, :]), waits=[evB2])
        for cc in range(1, TL + 1):
            for j in range(8):
                ch.emit(lambda e, cc=cc, j=j: e.tensor_scalar(
                    smallT[:, j, :], smallE[:, cc - 1, :], float(j),
                    smallP[:, cc - 1, j:j + 1], ALU.is_equal, ALU.mult),
                    independent=(j != 0))
            ch.emit(lambda e, cc=cc: e.tensor_reduce(
                smallP[:, cc, :].rearrange("p (i o) -> p i o", o=1),
                smallT[:, :, :].rearrange("p j i -> p i j"), axis=AX.X, op=ALU.add))
        p.seal("vector", ch)
        evPre = ch.last
        evB3 = b2dma(bass.AP(pre2_scr, 0, [[8, 4], [32, TL], [1, 8]]), smallP[:, 0:TL, :], [evPre])
        evB4 = b2dma(rrb[:, 0, :], bass.AP(pre2_scr, 0, [[8, CP], [1, 8]]), [evB3])
        ch = p.chain(sDV)
        for j in range(8):
            if j == 0:
                ch.emit(lambda e, j=j: e.tensor_scalar(
                    psg2[:, :, :], psig[:, :, :], float(j),
                    rrb[:, 0, j:j + 1], ALU.is_equal, ALU.mult), waits=[evB4, evPsiL])
            else:
                ch.emit(lambda e, j=j: e.tensor_scalar(
                    rmaxb[:, :, :], psig[:, :, :], float(j),
                    rrb[:, 0, j:j + 1], ALU.is_equal, ALU.mult))
                ch.emit(lambda e: e.tensor_tensor(psg2[:, :, :], psg2[:, :, :],
                                                  rmaxb[:, :, :], ALU.add))
        ch.emit(lambda e: e.memset(phib[:, :, :], 0.0))
        for i in range(1, 8):
            ch.emit(lambda e, i=i: e.tensor_tensor(
                rmaxb[:, :, :],
                bcast(psg2[:, :, i].rearrange("p (t o) -> p t o", o=1), (CP, TPC, 8)),
                bcast(iota8[0:CP, :].unsqueeze(1), (CP, TPC, 8)), ALU.is_equal))
            ch.emit(lambda e, i=i: e.scalar_tensor_tensor(
                phib[:, :, :], rmaxb[:, :, :], float(i), phib[:, :, :], ALU.mult, ALU.add))
        for (dst, vsrc) in ((dtil, dvec), (ctil, cvec)):
            dview = dst[:, :, :].rearrange("p i t -> p t i")
            for j in range(8):
                if j == 0:
                    ch.emit(lambda e, j=j: e.tensor_scalar(
                        rmaxb[:, :, :], phib[:, :, :], float(j), None, ALU.is_equal))
                    ch.emit(lambda e, j=j, dview=dview, vsrc=vsrc: e.tensor_tensor(
                        dview, rmaxb[:, :, :],
                        bcast(vsrc[:, :, j].rearrange("p (t o) -> p t o", o=1), (CP, TPC, 8)),
                        ALU.mult))
                else:
                    ch.emit(lambda e, j=j: e.tensor_scalar(
                        rmaxb[:, :, :], phib[:, :, :], float(j), None, ALU.is_equal))
                    ch.emit(lambda e, j=j, vsrc=vsrc: e.tensor_tensor(
                        rmaxb[:, :, :], rmaxb[:, :, :],
                        bcast(vsrc[:, :, j].rearrange("p (t o) -> p t o", o=1), (CP, TPC, 8)),
                        ALU.mult))
                    ch.emit(lambda e, dview=dview: e.tensor_tensor(
                        dview, dview, rmaxb[:, :, :], ALU.add))
        p.seal("vector", ch)
        evGath = ch.last
        drain_acc(4)
        if dbg_phi is not None:
            b2dma(dbg_phi[:, :], phib[:, :, :].rearrange("p t i -> p (t i)"), [evGath])
        if dbg_psi is not None:
            b2dma(dbg_psi[:, :], psg2[:, :, :].rearrange("p t i -> p (t i)"), [evGath])
        if dbg_dtil is not None:
            b2dma(dbg_dtil[:, :], dtil[:, :, :].rearrange("p i t -> p (i t)"), [evGath])
        if dbg_ctil is not None:
            b2dma(dbg_ctil[:, :], ctil[:, :, :].rearrange("p i t -> p (i t)"), [evGath])

        evB5 = b2dma(bass.AP(dc_scr, 0, [[NS * TPC, CP], [TPC, NS], [1, TPC]]),
                     dtil[:, :, :], [evGath])
        evB6 = b2dma(bass.AP(dc_scr, CP * NS * TPC, [[NS * TPC, CP], [TPC, NS], [1, TPC]]),
                     ctil[:, :, :], [evB5])
        for cm2 in reversed(_t1):
            cm2.__exit__(None, None, None)
        for cm2 in reversed(_t2):
            cm2.__exit__(None, None, None)
        packD = sb("packD", [32, T])
        packC = sb("packC", [32, T])
        packU = sb("packU", [32, T])
        packZ = sb("packZ", [32, T])
        ysT_bf = sb("ysTbf", [NS * H, T], BF16)
        ysb = sb("ysb", [CP, NS, TPC])
        ysb_bf = sb("ysbbf", [CP, NS, TPC], BF16)
        gch = sb("gch", [CP, NS, TPC])
        scrT = sb("scrT", [CP, TPC, 8])
        outwL = sb("outwL", [NS * H, D], BF16)
        outwF = sb("outwF", [NS * H, D])
        outbL = sb("outbL", [1, D])
        evB7 = b2dma(packD[:, :].rearrange("p (c t) -> p c t", t=TPC),
                     bass.AP(dc_scr, 0, [[TPC, 32], [NS * TPC * 4, TL], [1, TPC]]), [evB6])
        evB8 = b2dma(packC[:, :].rearrange("p (c t) -> p c t", t=TPC),
                     bass.AP(dc_scr, CP * NS * TPC, [[TPC, 32], [NS * TPC * 4, TL], [1, TPC]]),
                     [evB7])
        evZer = p.emit("gpsimd", lambda e: e.memset(packZ[:, :], 0.0),
                       waits=[evB6], pub=sCst)
        ch = p.chain(sDV)
        ch.emit(lambda e: e.tensor_tensor_scan(
            packU[:, :], packD[:, :], packC[:, :], 0.0, ALU.mult, ALU.add),
            waits=[evB8, evZer])
        ch.emit(lambda e: e.tensor_tensor_scan(
            packD[:, :], packD[:, :], packZ[:, :], 1.0, ALU.mult, ALU.add))
        p.seal("vector", ch)
        evScan = ch.last
        evB9 = b2dma(bass.AP(ue_scr, 0, [[1, 32], [1, 1]]), packU[:, T - 1:T], [evScan])
        evB10 = b2dma(smallU[:, :], bass.AP(ue_scr, 0, [[8, 4], [1, 8]]), [evB9])
        ch = p.chain(sDV)
        for j in range(8):
            ch.emit(lambda e, j=j: e.tensor_scalar(
                smallT[:, j, :], smallP[:, TL, :], float(j),
                smallU[:, j:j + 1], ALU.is_equal, ALU.mult),
                waits=[evB10] if j == 0 else (), independent=(j != 0))
        ch.emit(lambda e: e.tensor_reduce(
            smallH[:, :].rearrange("p (i o) -> p i o", o=1),
            smallT[:, :, :].rearrange("p j i -> p i j"), axis=AX.X, op=ALU.add))
        p.seal("vector", ch)
        evHmd = ch.last
        evB11 = b2dma(bass.AP(hm_scr, 0, [[8, 4], [1, 8]]), smallH[:, :], [evHmd])
        evCC = p.emit("gpsimd", lambda e: e.collective_compute(
            "AllGather", ALU.bypass, replica_groups=[list(range(N_CORES))],
            ins=[hm_scr.ap()], outs=[ag_scr.ap()]), waits=[evB11], pub=sCC)
        evB12 = b2dma(agf[:, :, :], bass.AP(ag_scr, 0, [[256, 1], [32, 8], [1, 32]]), [evCC])
        ch = p.chain(sDV)
        ch.emit(lambda e: e.tensor_tensor(
            magb[:, :, :], agf[:, :, :],
            bcast(g0m_sb[:, :].unsqueeze(2), (1, 8, 32)), ALU.mult),
            waits=[evB12, evG0L])
        ch.emit(lambda e: e.tensor_reduce(
            g0r[:, :].rearrange("p (k o) -> p k o", o=1),
            magb[:, :, :].rearrange("p r k -> p k r"), axis=AX.X, op=ALU.add))
        p.seal("vector", ch)
        evG0 = ch.last
        evB13 = b2dma(bass.AP(g0_scr, 0, [[32, 1], [1, 32]]), g0r[:, :], [evG0])
        evB14 = b2dma(g0col[:, :], bass.AP(g0_scr, 0, [[1, 32], [1, 1]]), [evB13])
        ch = p.chain(sDV)
        ch.emit(lambda e: e.tensor_scalar(
            packC[:, :], packD[:, :], g0col[:, :], None, ALU.mult),
            waits=[evB14, evScan])
        ch.emit(lambda e: e.tensor_tensor(
            packC[:, :], packC[:, :], packU[:, :], ALU.add))
        p.seal("vector", ch)
        evY = ch.last
        evB15 = b2dma(bass.AP(g_scr, 0, [[TPC, 32], [NS * TPC * 4, TL], [1, TPC]]),
                      packC[:, :].rearrange("p (c t) -> p c t", t=TPC), [evY])
        evB16 = b2dma(gch[:, :, :], bass.AP(g_scr, 0, [[NS * TPC, CP], [TPC, NS], [1, TPC]]),
                      [evB15])
        ch = p.chain(sDV)
        yview = ysb[:, :, :].rearrange("p i t -> p t i")
        for j in range(8):
            ch.emit(lambda e, j=j: e.tensor_scalar(
                scrT[:, :, :], psg2[:, :, :], float(j), None, ALU.is_equal),
                waits=[evB16] if j == 0 else ())
            if j == 0:
                ch.emit(lambda e, j=j: e.tensor_tensor(
                    yview, scrT[:, :, :],
                    bcast(gch[:, j, :].rearrange("p (t o) -> p t o", o=1), (CP, TPC, 8)),
                    ALU.mult))
            else:
                ch.emit(lambda e, j=j: e.tensor_tensor(
                    scrT[:, :, :], scrT[:, :, :],
                    bcast(gch[:, j, :].rearrange("p (t o) -> p t o", o=1), (CP, TPC, 8)),
                    ALU.mult))
                ch.emit(lambda e: e.tensor_tensor(yview, yview, scrT[:, :, :], ALU.add))
        ch.emit(lambda e: e.tensor_copy(ysb_bf[:, :, :], ysb[:, :, :]))
        p.seal("vector", ch)
        evYs = ch.last
        drain_acc(32)
        evB17 = b2dma(bass.AP(ys_scr, 0, [[NS * TPC, CP], [TPC, NS], [1, TPC]]),
                      ysb_bf[:, :, :], [evYs])
        evYsT = b2dma(ysT_bf[:, :].rearrange("p (c t) -> p c t", t=TPC),
                      bass.AP(ys_scr, 0, [[TPC, 32], [NS * TPC * 4, TL], [1, TPC]]), [evB17])
        if dbg_ysT is not None:
            b2dma(dbg_ysT[:, :], ysT_bf[:, :], [evYsT])

        evOwF = p.dma("sync", outwF[:, :], outw_in,
                      waits=[evP1, evG0L, evGath, evB6], pub=sWpL)
        evObL = p.dma("sync", outbL[:, :], outb_in, waits=[evP1, evOwF], pub=sWpL)
        ch = p.chain(sDV)
        ch.emit(lambda e: e.tensor_copy(outwL[:, :], outwF[:, :]), waits=[evOwF])
        p.seal("vector", ch)
        evOwc2 = ch.last

        evMxE = {}
        evOut = {}
        evAcL = [None, None]
        for tl in range(NT):
            sl3 = tl % 2
            rfree = evPjE[NT - 2 + sl3] if tl < 2 else evMxE[tl - 2]
            evAcL[sl3] = p.dma("sync", rld_sb[:, sl3, :], acc_scr.ap()[tl * TT:(tl + 1) * TT, :],
                               waits=[evAccSp[tl], rfree], pub=sAcL[sl3])
            for hf in range(2):
                ww = [evYsT, evMxE.get(tl - 2), evOwc2, evObL] if hf == 0 else ()
                p.emit("tensor", lambda e, hf=hf, tl=tl: e.matmul(
                    ps_mx[tl % 2][:, hf * 512:(hf + 1) * 512],
                    ysT_bf[:, tl * TT:(tl + 1) * TT], outwL[:, hf * 512:(hf + 1) * 512],
                    start=True, stop=False), waits=ww)
            for hf in range(2):
                p.emit("tensor", lambda e, hf=hf, sl3=sl3, tl=tl: e.matmul(
                    ps_mx[tl % 2][:, hf * 512:(hf + 1) * 512], idp_sb[:, :],
                    rld_sb[:, sl3, hf * 512:(hf + 1) * 512],
                    start=False, stop=False),
                    waits=[evAcL[sl3]] if hf == 0 else ())
            evMm = None
            for hf in range(2):
                evMm = p.emit("tensor", lambda e, hf=hf, tl=tl: e.matmul(
                    ps_mx[tl % 2][:, hf * 512:(hf + 1) * 512], ones_sb[:, :],
                    outbL[:, hf * 512:(hf + 1) * 512],
                    start=False, stop=True), pub=(sPE if hf == 1 else None))
            ffree = evAccSp[NT - 2 + sl3] if tl < 2 else evOut[tl - 2]
            evMxE[tl] = p.emit("scalar", lambda e, sl3=sl3, tl=tl: e.activation(
                ffe_sb[:, sl3, :], ps_mx[tl % 2][:, :], AF.Copy),
                waits=[evMm, ffree], pub=sSC)
            evOut[tl] = p.dma("sync", out_ap[tl * TT:(tl + 1) * TT, :], ffe_sb[:, sl3, :],
                              waits=[evMxE[tl]], pub=sOu[sl3])

        fin = []
        for s in (sWL, sSC, sDV, sPE, sCst, sDbg, sB2, sCC, sWpL,
                  sX[0], sX[1], sWst[0], sWst[1], sWqS[0], sWqS[1],
                  sRs[0], sRs[1], sLgs[0], sLgs[1], sHqs[0], sHqs[1],
                  sHqL[0], sHqL[1], sRL[0], sRL[1], sAcS[0], sAcS[1],
                  sAcL[0], sAcL[1], sOu[0], sOu[1]):
            e2 = Ev(s)
            e2.val = s.total
            fin.append(e2)
        p.emit("sync", lambda e: e.nop(), waits=fin)
        p.finalize(block)
    es.close()
    return nc, dbg


def shard_inputs(inputs, core_id, n_tiles=32):
    T = n_tiles * TT
    b, half = core_id // 2, core_id % 2
    x = np.ascontiguousarray(inputs["L0_attn_x"][b, half * 4096: half * 4096 + T]).astype(np.float32, copy=False)
    pw, sw = inputs["L0_attn_perm_W"], inputs["L0_attn_sign_W"]
    gw, iw = inputs["L0_attn_gate_W"], inputs["L0_attn_in_W"]
    pb, sbv = inputs["L0_attn_perm_b"], inputs["L0_attn_sign_b"]
    gb, ib = inputs["L0_attn_gate_b"], inputs["L0_attn_in_b"]
    w_attn = np.concatenate(
        [np.concatenate([pw[:, h * 64:(h + 1) * 64], sw[:, h * 8:(h + 1) * 8],
                         gw[:, h * 8:(h + 1) * 8], iw[:, h * 8:(h + 1) * 8]], axis=1)
         for h in range(H)], axis=1)
    b_attn = np.concatenate(
        [np.concatenate([pb[h * 64:(h + 1) * 64], sbv[h * 8:(h + 1) * 8],
                         gb[h * 8:(h + 1) * 8], ib[h * 8:(h + 1) * 8]])
         for h in range(H)])[None, :]
    g0mask = np.zeros((1, 8), np.float32)
    if core_id % 2 == 1:
        g0mask[0, core_id - 1] = 1.0
    im = {
        "x_sh": x,
        "w_attn": np.ascontiguousarray(w_attn, dtype=np.float32),
        "b_attn": np.ascontiguousarray(b_attn, dtype=np.float32),
        "lnw_rs": np.ascontiguousarray(np.asarray(inputs["L0_attn_ln_w"]).reshape(DC, 128).T, dtype=np.float32),
        "lnb_rs": np.ascontiguousarray(np.asarray(inputs["L0_attn_ln_b"]).reshape(DC, 128).T, dtype=np.float32),
        "w_fc": np.ascontiguousarray(inputs["L0_ffn_w_fc"], dtype=np.float32),
        "b_fc": np.ascontiguousarray(np.asarray(inputs["L0_ffn_b_fc"])[None, :], dtype=np.float32),
        "w_proj": np.ascontiguousarray(inputs["L0_ffn_w_proj"], dtype=np.float32),
        "b_proj": np.ascontiguousarray(np.asarray(inputs["L0_ffn_b_proj"])[None, :], dtype=np.float32),
        "out_w": np.ascontiguousarray(inputs["L0_attn_out_W"], dtype=np.float32),
        "out_b": np.ascontiguousarray(np.asarray(inputs["L0_attn_out_b"])[None, :], dtype=np.float32),
        "id_f": np.eye(128, dtype=np.float32),
        "id_b": np.eye(128).astype(ml_dtypes.bfloat16),
        "ones_r": np.ones((1, 128), np.float32),
        "g0mask": g0mask,
    }
    return im


def kernel(**inputs):
    n_tiles = 32
    nc, _ = build(n_tiles=n_tiles)
    in_maps = [shard_inputs(inputs, c, n_tiles) for c in range(N_CORES)]
    res = run_bass_kernel_spmd(nc, in_maps, core_ids=list(range(N_CORES)))
    out = np.zeros((4, 8192, 1024), dtype=np.float32)
    for c in range(N_CORES):
        b, half = c // 2, c % 2
        out[b, half * 4096:(half + 1) * 4096] = res.results[c]["out_sh"]
    return out
